# version 01
# baseline (speedup 1.0000x reference)
import contextlib
import numpy as np

import concourse.bass as bass
import concourse.mybir as mybir
from concourse.bass_utils import run_bass_kernel_spmd

F32 = mybir.dt.float32
F16 = mybir.dt.float16
I16 = mybir.dt.int16
Alu = mybir.AluOpType
Act = mybir.ActivationFunctionType
AX = mybir.AxisListType.X

NT = 2048
D = 1024
DC = 8
F = 2048
FC = 16
E = 8
NH = 16
HD = 64
NPAIR = 8
TT = 256
NTT = NT // TT
LNT = 512
FREQS = [31.0, 37.0, 41.0, 43.0, 47.0]
NFREQ = 5
TWO_PI_SAFE = 6.283185
INV_2PI = 1.0 / (2.0 * np.pi)
IT = 1024
NIT = NT // IT
NJT = NT // 128

DO_FFN1 = True
DO_ATTN = True
DO_FFN2 = True
DEBUG_TAPS = False
DEBUG_SYNC = False


class _DrainAfter:
    def __init__(self, eng):
        self._eng = eng

    def __getattr__(self, name):
        attr = getattr(self._eng, name)
        if name in ("wait_ge", "drain", "dma_start", "sem_inc", "nop"):
            return attr
        if not callable(attr):
            return attr

        def wrapped(*a, **kw):
            r = attr(*a, **kw)
            try:
                self._eng.drain()
            except Exception:
                pass
            return r
        return wrapped


class Ctx:
    def __init__(self, nc, es):
        self.nc = nc
        self.es = es
        self.sy, self.ve, self.sc, self.te, self.gp = (
            nc.sync, nc.vector, nc.scalar, nc.tensor, nc.gpsimd)
        if DEBUG_SYNC:
            self.ve = _DrainAfter(nc.vector)
            self.sc = _DrainAfter(nc.scalar)
        self.engs = [self.sy, self.ve, self.sc, self.te, self.gp]
        self.sems = {}
        self.cnt = {}
        self.lastwait = {}
        for name in ("L", "G", "P", "A", "V", "Q"):
            self.sems[name] = es.enter_context(nc.semaphore("sem" + name))
            self.cnt[name] = 0

    def dsem(self, name):
        if name not in self.sems:
            self.sems[name] = self.es.enter_context(
                self.nc.semaphore("sem" + name))
            self.cnt[name] = 0
        return self.sems[name]

    def inc(self, instr, name, amt=1):
        if name not in self.sems:
            self.dsem(name)
        instr.then_inc(self.sems[name], amt)
        self.cnt[name] += amt
        return self.cnt[name]

    def wait(self, eng, name, val=None):
        if val is None:
            val = self.cnt[name]
        if val <= 0:
            return
        key = (id(eng), name)
        if self.lastwait.get(key, -1) >= val:
            return
        eng.wait_ge(self.sems[name], val)
        self.lastwait[key] = val

    def barrier(self):
        for eng in self.engs:
            for name in list(self.sems):
                self.wait(eng, name)

    def sb(self, name, shape, dtype):
        return self.es.enter_context(self.nc.sbuf_tensor(name, shape, dtype))



def emit_setup(k, dr, g):
    nc = k.nc
    g["r"] = k.sb("r", [128, DC * NT], F32)
    g["b1r"] = k.sb("s_b1r", [128, 128], F32)
    g["b2_16"] = k.sb("b2_16", [8, D], F16)
    for nm in ("bqr", "bkr", "bor", "gr", "br"):
        g[nm] = k.sb("s_" + nm, [128, 8], F32)
    g["bv16"] = k.sb("bv16", [1, D], F16)
    g["rw16"] = k.sb("rw16", [128, 64], F16)
    g["ident16"] = k.sb("ident16", [128, 128], F16)
    g["ones_c32"] = k.sb("ones_c32", [128, 1], F32)
    g["ones_c16"] = k.sb("ones_c16", [128, 1], F16)
    g["ones_r32"] = k.sb("ones_r32", [1, 128], F32)
    g["ones_r16"] = k.sb("ones_r16", [1, 128], F16)
    g["kvec"] = k.sb("kvec", [128, 176], F32)
    g["cst"] = k.sb("cst", [128, 8], F32)

    sy, ve, sc, te, gp = k.sy, k.ve, k.sc, k.te, k.gp

    with contextlib.ExitStack() as es:
        def sb(name, shape, dtype):
            return es.enter_context(nc.sbuf_tensor(name, shape, dtype))
        rows = sb("rows", [1, 96], F32)
        alphast = sb("alphast", [1, 80], F32)
        phist = sb("phist", [1, 80], F32)
        t11 = sb("t11", [1, 1], F32)
        iop = sb("iop", [128, 1], F32)
        iof = sb("iof", [128, 128], F32)
        b2f = sb("b2f_stage", [8, D], F32)
        bvf = sb("bv_stage", [1, D], F32)
        rwf = sb("rw_stage", [128, 64], F32)

        for c in range(DC):
            k.inc(sy.dma_start(out=g["r"][:, c * NT:(c + 1) * NT],
                               in_=dr["xT"][c * 128:(c + 1) * 128, :]), "L", 16)
        k.inc(sy.dma_start(out=g["b1r"][:], in_=dr["b1r"]), "L", 16)
        k.inc(sy.dma_start(out=b2f[:], in_=dr["b2"]), "L", 16)
        for nm in ("bqr", "bkr", "bor", "gr", "br"):
            k.inc(sy.dma_start(out=g[nm][:], in_=dr[nm]), "L", 16)
        k.inc(sy.dma_start(out=bvf[:], in_=dr["bv"]), "L", 16)
        k.inc(sy.dma_start(out=rwf[:].rearrange("p (c e) -> p c e", e=8),
                           in_=dr["rwT"].rearrange("(c p) e -> p c e", p=128)),
              "L", 16)
        k.inc(sy.dma_start(out=phist[:], in_=dr["phif"]), "L", 16)
        k.inc(sy.dma_start(out=alphast[:], in_=dr["alphaf"]), "L", 16)
        k.inc(sy.dma_start(out=t11[:], in_=dr["temp"]), "L", 16)
        ldone = k.cnt["L"]

        k.inc(gp.iota(iop[:], pattern=[[0, 1]], channel_multiplier=1,
                      allow_small_or_imprecise_dtypes=True), "Q", 1)
        k.inc(gp.iota(iof[:], pattern=[[1, 128]], channel_multiplier=0,
                      allow_small_or_imprecise_dtypes=True), "Q", 1)

        k.wait(ve, "L", ldone)
        k.wait(ve, "Q")
        ve.tensor_scalar(g["ident16"][:], iof[:], iop[:], None, Alu.is_equal)
        for i, v in enumerate([1e-5, 1024.0 + 1e-10, -0.5, 0.5, TWO_PI_SAFE, -1.0]):
            ve.memset(g["cst"][:, i:i + 1], v)
        ve.memset(g["ones_c32"][:], 1.0)
        ve.memset(g["ones_c16"][:], 1.0)
        ve.memset(g["ones_r32"][:], 1.0)
        ve.memset(g["ones_r16"][:], 1.0)
        ve.tensor_copy(g["b2_16"][:], b2f[:])
        ve.tensor_copy(g["bv16"][:], bvf[:])
        ve.tensor_copy(g["rw16"][:], rwf[:])
        k.wait(sc, "L", ldone)
        k.inc(sc.activation(rows[0:1, 7:8], t11[:], Act.Abs, bias=0.0, scale=1.0),
              "A", 1)
        k.wait(ve, "A")
        ve.tensor_scalar(rows[0:1, 6:7], rows[0:1, 7:8], 8.0, 0.8, Alu.mult, Alu.add)
        ve.drain()
        ve.reciprocal(rows[0:1, 5:6], rows[0:1, 6:7])
        for i, f in enumerate(FREQS):
            ve.memset(rows[0:1, i:i + 1], 1.0 / f)
        ve.drain()
        ve.tensor_scalar(rows[0:1, 8:13], rows[0:1, 0:5], rows[0:1, 5:6],
                         None, Alu.mult)
        k.inc(ve.tensor_scalar(rows[0:1, 16:96], phist[:], INV_2PI, 2.0,
                               Alu.mult, Alu.add), "V", 1)
        with nc.psum_tensor("ps_setup", [128, 512], F32) as psb:
            k.wait(te, "V")
            te.matmul(psb[:, 0:5], g["ones_r32"][:], rows[0:1, 8:13],
                      start=True, stop=True)
            te.matmul(psb[:, 16:96], g["ones_r32"][:], rows[0:1, 16:96],
                      start=True, stop=True)
            k.inc(te.matmul(psb[:, 96:176], g["ones_r32"][:], alphast[:],
                            start=True, stop=True), "P", 1)
            k.wait(ve, "P")
            ve.tensor_copy(g["kvec"][:, 0:5], psb[:, 0:5])
            ve.tensor_copy(g["kvec"][:, 16:96], psb[:, 16:96])
            k.inc(ve.tensor_copy(g["kvec"][:, 96:176], psb[:, 96:176]), "V", 1)
        k.barrier()


def cst(g, i):
    return g["cst"][:, i:i + 1]


def k1_ap(g, fi):
    return g["kvec"][:, fi:fi + 1]


def k2_ap(g, h, fi):
    return g["kvec"][:, 16 + h * 5 + fi:16 + h * 5 + fi + 1]


def al_ap(g, h, fi):
    return g["kvec"][:, 96 + h * 5 + fi:96 + h * 5 + fi + 1]



def emit_ln_tile(k, g, src_fn, psSUM, psSQ, psBC1, psBC2, sqp, rowA, rowB,
                 rowC, ab16):
    te, sc, ve = k.te, k.sc, k.ve
    k.wait(te, "V")
    sq_mm_P = [0, 0]
    for c in range(DC):
        par = c % 2
        k.wait(sc, "P", sq_mm_P[par])
        av = k.inc(sc.activation(sqp[par][:], src_fn(c), Act.Square,
                                 bias=0.0, scale=1.0), "A", 1)
        te.matmul(psSUM[0:1, 0:LNT], g["ones_c32"][:], src_fn(c),
                  start=(c == 0), stop=(c == DC - 1))
        k.wait(te, "A", av)
        mm = te.matmul(psSQ[0:1, 0:LNT], g["ones_c32"][:], sqp[par][:],
                       start=(c == 0), stop=(c == DC - 1))
        sq_mm_P[par] = k.inc(mm, "P", 1)
    pv = k.cnt["P"]
    k.wait(ve, "P", pv)
    ve.tensor_scalar(rowA[:], psSUM[0:1, 0:LNT], 1.0 / D, None, Alu.mult)
    ve.tensor_scalar(psSUM[0:1, 0:LNT], psSQ[0:1, 0:LNT], 1.0 / D, None, Alu.mult)
    ve.tensor_tensor(rowB[:], rowA[:], rowA[:], Alu.mult)
    k.inc(ve.tensor_tensor(psSQ[0:1, 0:LNT], psSUM[0:1, 0:LNT], rowB[:],
                           Alu.subtract), "V", 1)
    k.wait(sc, "V")
    sc.activation(psSUM[0:1, 0:LNT], psSQ[0:1, 0:LNT], Act.Ln,
                  bias=cst(g, 0)[0:1, :], scale=1.0)
    k.inc(sc.activation(rowC[:], psSUM[0:1, 0:LNT], Act.Exp, bias=0.0,
                        scale=cst(g, 2)[0:1, :]), "A", 1)
    k.wait(ve, "A")
    ve.tensor_tensor(rowB[:], rowA[:], rowC[:], Alu.mult)
    k.inc(ve.tensor_scalar(rowA[:], rowB[:], -1.0, None, Alu.mult), "V", 1)
    k.wait(te, "V")
    te.matmul(psBC1[:, 0:LNT], g["ones_r16"][:], rowC[:], start=True, stop=True)
    pv = k.inc(te.matmul(psBC2[:, 0:LNT], g["ones_r16"][:], rowA[:],
                         start=True, stop=True), "P", 1)
    k.wait(ve, "P", pv)
    ve.tensor_copy(ab16[:, 0:LNT], psBC1[:, 0:LNT])
    return k.inc(ve.tensor_copy(ab16[:, LNT:2 * LNT], psBC2[:, 0:LNT]), "V", 1)



def emit_ffn(k, g, dr, tag):
    nc = k.nc
    te, sc, ve, sy, gp = k.te, k.sc, k.ve, k.sy, k.gp
    k.barrier()
    with contextlib.ExitStack() as es:
        def sb(name, shape, dtype):
            return es.enter_context(nc.sbuf_tensor(f"{name}_{tag}", shape, dtype))

        def psf(name, shape, dtype=F32):
            return es.enter_context(nc.psum_tensor(f"{name}_{tag}", shape, dtype))

        k.dsem("Lst0"), k.dsem("Lst1"), k.dsem("Gc0"), k.dsem("Gc1")
        u16 = sb("u16", [128, DC * NT], F16)
        w1s = sb("w1s", [128, DC * F], F16)
        w2s = sb("w2s", [128, FC * D], F16)
        stage = sb("wstage", [128, 2048], F32)
        g16 = [sb(f"g16_{i}", [128, FC * TT], F16) for i in range(2)]
        graw = sb("graw", [128, 512], F16)
        tmp32 = [sb(f"tmp32_{i}", [128, 512], F32) for i in range(2)]
        rowA = sb("rowA", [1, LNT], F16)
        rowB = sb("rowB", [1, LNT], F16)
        rowC = sb("rowC", [1, LNT], F16)
        ab16 = sb("ab16", [128, 2 * LNT], F16)
        t16 = graw
        cem16 = sb("cem16", [8, NT], F16)
        lgt = sb("lgt", [128, 8], F32)
        msk1 = sb("msk1", [128, 8], F32)
        msk2 = sb("msk2", [128, 8], F32)
        xmk = sb("xmk", [128, 8], F32)
        mrow = sb("mrow", [128, 8], F32)
        ctm16 = sb("ctm16", [128, 8], F16)
        cbc16 = [sb(f"cbc16_{i}", [128, TT], F16) for i in range(2)]
        cemrow = [sb(f"cemrow_{i}", [1, TT], F16) for i in range(2)]

        ps_main = psf("ps_main", [128, 2048])
        psA = psf("psA", [128, 512])
        psB = psf("psB", [128, 512])
        psC = psf("psC", [128, 512])

        for tt in range(NT // LNT):
            lo = tt * LNT
            abv = emit_ln_tile(
                k, g,
                lambda c, lo=lo: g["r"][:, c * NT + lo: c * NT + lo + LNT],
                psA, psB, psC, psA, [tmp32[0], tmp32[1]], rowA, rowB, rowC, ab16)
            for c in range(DC):
                k.wait(ve, "V", abv)
                ve.tensor_tensor(t16[:], g["r"][:, c * NT + lo: c * NT + lo + LNT],
                                 ab16[:, 0:LNT], Alu.mult)
                vv = ve.tensor_tensor(u16[:, c * NT + lo: c * NT + lo + LNT],
                                      t16[:], ab16[:, LNT:2 * LNT], Alu.add)
            k.inc(vv, "V", 1)
        u_ready_V = k.cnt["V"]

        with nc.psum_tensor(f"psT16_{tag}", [128, 128], F16) as psT16:
            for tj in range(NT // 128):
                jlo = tj * 128
                k.wait(te, "V", u_ready_V)
                for c in range(DC):
                    mm = te.matmul(psA[:, 0:8],
                                   u16[:, c * NT + jlo: c * NT + jlo + 128],
                                   g["rw16"][:, 8 * c:8 * c + 8],
                                   start=(c == 0), stop=(c == DC - 1))
                pv = k.inc(mm, "P", 1)
                k.wait(ve, "P", pv)
                k.inc(ve.tensor_copy(lgt[:], psA[:, 0:8]), "V", 1)
                ve.drain()
                ve.reduce_max(mrow[:, 0:1], lgt[:], axis=AX)
                ve.drain()
                ve.tensor_scalar(msk1[:], lgt[:], mrow[:, 0:1], None, Alu.is_equal)
                ve.drain()
                ve.scalar_tensor_tensor(xmk[:], msk1[:], -1e30, lgt[:],
                                        Alu.mult, Alu.add)
                ve.drain()
                ve.reduce_max(mrow[:, 1:2], xmk[:], axis=AX)
                ve.drain()
                ve.tensor_scalar(msk2[:], xmk[:], mrow[:, 1:2], None, Alu.is_equal)
                ve.drain()
                k.inc(ve.tensor_tensor(mrow[:, 2:3], mrow[:, 0:1], mrow[:, 1:2],
                                       Alu.subtract), "V", 1)
                k.wait(sc, "V")
                k.inc(sc.activation(mrow[:, 3:4], mrow[:, 2:3], Act.Tanh,
                                    bias=0.0, scale=cst(g, 3)), "A", 1)
                k.wait(ve, "A")
                ve.tensor_scalar(mrow[:, 4:5], mrow[:, 3:4], 0.25, 0.25,
                                 Alu.mult, Alu.add)
                ve.tensor_scalar(mrow[:, 5:6], mrow[:, 3:4], -0.25, 0.25,
                                 Alu.mult, Alu.add)
                ve.drain()
                ve.tensor_scalar(xmk[:], msk1[:], mrow[:, 4:5], None, Alu.mult)
                ve.drain()
                k.inc(ve.scalar_tensor_tensor(ctm16[:], msk2[:], mrow[:, 5:6],
                                              xmk[:], Alu.mult, Alu.add), "V", 1)
                k.wait(te, "V")
                pv = k.inc(te.transpose(psT16[0:8, 0:128], ctm16[:],
                                        g["ident16"][:]), "P", 1)
                k.wait(ve, "P", pv)
                k.inc(ve.tensor_copy(cem16[0:8, jlo:jlo + 128],
                                     psT16[0:8, 0:128]), "V", 1)
        cem_ready_V = k.cnt["V"]

        if DEBUG_TAPS:
            dbg_u = k.nc.dram_tensor(f"dbg_u16_{tag}", [128, DC * NT], F16,
                                     kind="ExternalOutput").ap()
            dbg_c = k.nc.dram_tensor(f"dbg_cem_{tag}", [8, NT], F16,
                                     kind="ExternalOutput").ap()
            k.wait(gp, "V", cem_ready_V)
            k.inc(gp.dma_start(out=dbg_u, in_=u16[:]), "G", 16)
            k.inc(gp.dma_start(out=dbg_c, in_=cem16[:]), "G", 16)

        stage_free = [0, 0]

        def stream_w1(e, gate_P):
            for hc in range(DC):
                for half in range(2):
                    par = (hc * 2 + half) % 2
                    k.wait(sy, "V", stage_free[par])
                    lv = k.inc(sy.dma_start(
                        out=stage[:, par * 1024:(par + 1) * 1024],
                        in_=dr["w1T"][e, hc * 128:(hc + 1) * 128,
                                      half * 1024:(half + 1) * 1024]),
                        f"Lst{par}", 16)
                    k.wait(ve, f"Lst{par}", lv)
                    k.wait(ve, "P", gate_P)
                    stage_free[par] = k.inc(ve.tensor_copy(
                        w1s[:, hc * F + half * 1024: hc * F + (half + 1) * 1024],
                        stage[:, par * 1024:(par + 1) * 1024]), "V", 1)
            return k.cnt["V"]

        def stream_w2(e, gate_P):
            for fc in range(FC):
                par = fc % 2
                k.wait(sy, "V", stage_free[par])
                lv = k.inc(sy.dma_start(
                    out=stage[:, par * 1024:(par + 1) * 1024],
                    in_=dr["w2T"][e, fc * 128:(fc + 1) * 128, :]),
                    f"Lst{par}", 16)
                k.wait(ve, f"Lst{par}", lv)
                k.wait(ve, "P", gate_P)
                stage_free[par] = k.inc(ve.tensor_copy(
                    w2s[:, fc * D: fc * D + 1024],
                    stage[:, par * 1024:(par + 1) * 1024]), "V", 1)
            return k.cnt["V"]

        w1_ready = stream_w1(0, 0)
        w2_ready = stream_w2(0, 0)
        aw_done_P = 0
        bw_done_P = 0
        gelu_free_A = {0: 0, 1: 0}
        graw_free_V = 0
        cbc_free_V = 0
        g16_free_P = {0: 0, 1: 0}
        evac_last_A = 0
        k.dsem("Gt0"), k.dsem("Gt1")
        r_acc_G = [[0, 0] for _ in range(NTT)]
        tmp_free_G = [0, 0]

        cemrow_G = [0, 0]
        cemrow_mm_P = [0, 0]

        def issue_cemrow(e2, tt2):
            par2 = tt2 % 2
            k.wait(gp, "V", cem_ready_V)
            k.wait(gp, "P", cemrow_mm_P[par2])
            cemrow_G[par2] = k.inc(gp.dma_start(
                out=cemrow[par2][:],
                in_=cem16[e2:e2 + 1, tt2 * TT:(tt2 + 1) * TT]), f"Gc{par2}", 16)

        issue_cemrow(0, 0)
        for e in range(E):
            for tt in range(NTT):
                gpar = tt % 2
                lo = tt * TT
                if tt + 1 < NTT:
                    issue_cemrow(e, tt + 1)
                elif e + 1 < E:
                    issue_cemrow(e + 1, 0)
                k.wait(te, "V", max(cem_ready_V, cbc_free_V, w1_ready))
                k.wait(te, f"Gc{gpar}", cemrow_G[gpar])
                pv = k.inc(te.matmul(psC[:, 0:TT], g["ones_r16"][:],
                                     cemrow[gpar][:],
                                     start=True, stop=True), "P", 1)
                cemrow_mm_P[gpar] = pv
                k.wait(ve, "P", pv)
                cbv = k.inc(ve.tensor_copy(cbc16[gpar][:], psC[:, 0:TT]), "V", 1)
                cbc_free_V = cbv
                for fcp in range(FC // 2):
                    par = fcp % 2
                    psg = psA if par == 0 else psB
                    k.wait(te, "A", gelu_free_A[par])
                    k.wait(sc, "V", graw_free_V)
                    for sub in range(2):
                        fc = fcp * 2 + sub
                        for hc in range(DC):
                            mm = te.matmul(
                                psg[:, sub * TT:(sub + 1) * TT],
                                w1s[:, hc * F + fc * 128: hc * F + (fc + 1) * 128],
                                u16[:, hc * NT + lo: hc * NT + lo + TT],
                                start=(hc == 0), stop=(hc == DC - 1))
                    pv = k.inc(mm, "P", 1)
                    k.wait(sc, "P", max(pv, g16_free_P[gpar]))
                    for sub in range(2):
                        fc = fcp * 2 + sub
                        av = k.inc(sc.activation(
                            graw[:, sub * TT:(sub + 1) * TT],
                            psg[:, sub * TT:(sub + 1) * TT], Act.Gelu,
                            bias=g["b1r"][:, e * 16 + fc:e * 16 + fc + 1],
                            scale=1.0), "A", 1)
                    gelu_free_A[par] = av
                    k.wait(ve, "A", av)
                    k.wait(ve, "V", cbv)
                    k.wait(ve, "P", g16_free_P[gpar])
                    for sub in range(2):
                        fc = fcp * 2 + sub
                        vv = ve.tensor_tensor(
                            g16[gpar][:, fc * TT:(fc + 1) * TT],
                            graw[:, sub * TT:(sub + 1) * TT],
                            cbc16[gpar][:], Alu.mult)
                    graw_free_V = k.inc(vv, "V", 1)
                g_ready_V = k.cnt["V"]
                if tt == NTT - 1:
                    aw_done_P = k.cnt["P"]
                k.wait(te, "V", max(w2_ready, g_ready_V))
                k.wait(te, "A", evac_last_A)
                for hc in range(DC):
                    first = True
                    if e == 0:
                        te.matmul(ps_main[:, hc * TT:(hc + 1) * TT],
                                  g["b2_16"][:, hc * 128:(hc + 1) * 128],
                                  cem16[:, lo:lo + TT], start=True, stop=False)
                        first = False
                    for fc in range(FC):
                        mm = te.matmul(
                            ps_main[:, hc * TT:(hc + 1) * TT],
                            w2s[:, fc * D + hc * 128: fc * D + (hc + 1) * 128],
                            g16[gpar][:, fc * TT:(fc + 1) * TT],
                            start=first, stop=(fc == FC - 1))
                        first = False
                pv = k.inc(mm, "P", 1)
                g16_free_P[gpar] = pv
                if tt == NTT - 1:
                    bw_done_P = k.cnt["P"]
                k.wait(sc, "P", pv)
                for q in range(4):
                    k.wait(sc, f"Gt{q % 2}", tmp_free_G[q % 2])
                    for hh in range(2):
                        hc = q * 2 + hh
                        av = k.inc(sc.activation(
                            tmp32[q % 2][:, hh * TT:(hh + 1) * TT],
                            ps_main[:, hc * TT:(hc + 1) * TT],
                            Act.Copy, bias=0.0, scale=1.0), "A", 1)
                    if q == 3:
                        evac_last_A = av
                    k.wait(gp, "A", av)
                    k.wait(gp, "Gt0", r_acc_G[tt][0])
                    k.wait(gp, "Gt1", r_acc_G[tt][1])
                    for hh in range(2):
                        hc = q * 2 + hh
                        gv = k.inc(gp.dma_start(
                            out=g["r"][:, hc * NT + lo: hc * NT + lo + TT],
                            in_=tmp32[q % 2][:, hh * TT:(hh + 1) * TT],
                            accum_op=Alu.add), f"Gt{q % 2}", 16)
                    tmp_free_G[q % 2] = gv
                r_acc_G[tt] = [k.cnt["Gt0"], k.cnt["Gt1"]]
            if e + 1 < E:
                w1_ready = stream_w1(e + 1, aw_done_P)
                w2_ready = stream_w2(e + 1, bw_done_P)
    k.barrier()



def emit_attention(k, g, dr):
    nc = k.nc
    te, sc, ve, sy, gp = k.te, k.sc, k.ve, k.sy, k.gp
    k.barrier()
    with contextlib.ExitStack() as es0:
        def sb0(name, shape, dtype):
            return es0.enter_context(nc.sbuf_tensor(name, shape, dtype))

        rs = sb0("rs", [16, NT], F32)
        lnt16 = sb0("lnt16", [16, NT], F16)
        nrm16 = sb0("nrm16", [16, NT], F16)

        with contextlib.ExitStack() as es:
            def sb(name, shape, dtype):
                return es.enter_context(nc.sbuf_tensor(name, shape, dtype))
            psA = es.enter_context(nc.psum_tensor("l2_psA", [128, 512], F32))
            psB = es.enter_context(nc.psum_tensor("l2_psB", [128, 512], F32))
            psC = es.enter_context(nc.psum_tensor("l2_psC", [128, 512], F32))
            sq32 = sb("l2_sq32", [128, LNT], F32)
            sq32b = sb("l2_sq32b", [128, LNT], F32)
            rowA = sb("l2_rowA", [1, LNT], F16)
            rowB = sb("l2_rowB", [1, LNT], F16)
            rowC = sb("l2_rowC", [1, LNT], F16)
            ab16 = sb("l2_ab16", [128, 2 * LNT], F16)
            f32t = sb("l2_f32t", [128, LNT], F32)
            ut16 = [sb(f"l2_ut16_{i}", [128, LNT], F16) for i in range(2)]
            ut_free_G = [0, 0]
            for tt in range(NT // LNT):
                lo = tt * LNT
                abv = emit_ln_tile(
                    k, g,
                    lambda c, lo=lo: g["r"][:, c * NT + lo: c * NT + lo + LNT],
                    psA, psB, psC, psA, [sq32, sq32b], rowA, rowB, rowC, ab16)
                for c in range(DC):
                    par = c % 2
                    k.wait(ve, "V", abv)
                    k.wait(ve, f"Gu{par}", ut_free_G[par])
                    ve.tensor_tensor(f32t[:],
                                     g["r"][:, c * NT + lo: c * NT + lo + LNT],
                                     ab16[:, 0:LNT], Alu.mult)
                    vv = k.inc(ve.tensor_tensor(ut16[par][:], f32t[:],
                                                ab16[:, LNT:2 * LNT], Alu.add),
                               "V", 1)
                    k.wait(gp, "V", vv)
                    ut_free_G[par] = k.inc(gp.dma_start(
                        out=dr["usp"][c * 128:(c + 1) * 128, lo:lo + LNT],
                        in_=ut16[par][:]), f"Gu{par}", 16)
        u_sp_G = [k.cnt["Gu0"], k.cnt["Gu1"]]
        k.barrier()

        with contextlib.ExitStack() as es:
            def sb(name, shape, dtype):
                return es.enter_context(nc.sbuf_tensor(name, shape, dtype))

            ps_main = es.enter_context(nc.psum_tensor("at_main", [128, 2048], F32))
            psA = es.enter_context(nc.psum_tensor("at_A", [128, 512], F32))
            psB = es.enter_context(nc.psum_tensor("at_B", [128, 512], F32))
            psC = es.enter_context(nc.psum_tensor("at_C", [128, 512], F32))
            psD = es.enter_context(nc.psum_tensor("at_D", [128, 512], F32))

            for _nm in ("Lw", "Lu0", "Lu1"):
                k.dsem(_nm)
            wq16 = sb("wq16", [128, D], F16)
            wk16 = sb("wk16", [128, D], F16)
            wv16 = sb("wv16", [128, D], F16)
            wstage = sb("at_wstage", [128, D], F32)
            ustr = [sb(f"ustr_{i}", [128, IT], F16) for i in range(2)]
            q16 = [sb(f"q16_{i}", [128, NT], F16) for i in range(2)]
            k16 = [sb(f"k16_{i}", [128, NT], F16) for i in range(2)]
            vt16 = [sb(f"vt16_{i}", [128, NJT * 130], F16) for i in range(2)]
            s16 = sb("s16", [128, IT], F16)
            ctmp16 = sb("ctmp16", [128, IT], F16)
            slabY = sb("slabY", [128, NFREQ * IT], F16)
            slabN = sb("slabN", [128, NFREQ * IT], I16)
            slabS = sb("slabS", [128, 2 * IT], F16)
            acc16 = sb("acc16", [128, NJT * IT], F16)
            w16r = [sb(f"w16r_{i}", [128, 2 * IT], F16) for i in range(2)]
            cv16 = sb("cv16", [64, 4], F32)
            ah16 = [sb(f"ah16_{i}", [64, NT], F16) for i in range(2)]
            rssc = sb("rssc", [128, IT], F32)

            wconv_V = 0
            qk_scores_P = {0: 0, 1: 0}
            vt_av_P = {0: 0, 1: 0}
            ustr_free_P = [0, 0]
            acc_slice_A = [0] * NJT
            w16r_free_P = [0, 0]
            ah_free_G = [0, 0]
            s16_free_V = 0
            s_evac_V = {0: 0, 1: 0}
            cv_read_V = {0: 0, 1: 0}
            rssc_free_G = 0
            k.dsem("Gr"), k.dsem("Ga0"), k.dsem("Ga1")

            for parx in range(2):
                for jt in range(NJT):
                    ve.memset(vt16[parx][:, jt * 130 + 64: jt * 130 + 65], 1.0)
                    mv = ve.memset(vt16[parx][:, jt * 130 + 129: jt * 130 + 130], 1.0)
            vt_ones_V = k.inc(mv, "V", 1)

            def stream_w(pair):
                nonlocal wconv_V
                par = pair % 2
                for (nm, dst) in (("wqT", wq16), ("wkT", wk16), ("wvT", wv16)):
                    k.wait(sy, "V", wconv_V)
                    lv = k.inc(sy.dma_start(
                        out=wstage[:].rearrange("p (c n) -> p c n", n=128),
                        in_=dr[nm][:, pair * 128:(pair + 1) * 128]
                            .rearrange("(c p) n -> p c n", p=128)), "Lw", 16)
                    k.wait(ve, "Lw", lv)
                    k.wait(ve, "P", qkv_used_P[0])
                    wconv_V = k.inc(ve.tensor_copy(dst[:], wstage[:]), "V", 1)

            qkv_used_P = [0]

            def emit_qkv(pair):
                par = pair % 2
                k.wait(te, "V")
                for (wsb, dst, brow) in ((wq16, q16[par], g["bqr"]),
                                         (wk16, k16[par], g["bkr"])):
                    k.wait(te, "P", qk_scores_P[par])
                    k.wait(te, "V", wconv_V)
                    for th in range(2):
                        tlo = th * IT
                        for c in range(DC):
                            upar = c % 2
                            k.wait(sy, "P", ustr_free_P[upar])
                            k.wait(sy, "Gu0", u_sp_G[0])
                            k.wait(sy, "Gu1", u_sp_G[1])
                            lv = k.inc(sy.dma_start(
                                out=ustr[upar][:],
                                in_=dr["usp"][c * 128:(c + 1) * 128,
                                              tlo:tlo + IT]), f"Lu{upar}", 16)
                            k.wait(te, f"Lu{upar}", lv)
                            for ih in range(2):
                                psx = psC if ih == 0 else psD
                                mm = te.matmul(
                                    psx[:, 0:512],
                                    wsb[:, c * 128:(c + 1) * 128],
                                    ustr[upar][:, ih * 512:(ih + 1) * 512],
                                    start=(c == 0), stop=(c == DC - 1))
                            ustr_free_P[upar] = k.inc(mm, "P", 1)
                        pv = k.cnt["P"]
                        k.wait(sc, "P", pv)
                        for ih in range(2):
                            psx = psC if ih == 0 else psD
                            av = k.inc(sc.activation(
                                dst[:, tlo + ih * 512: tlo + (ih + 1) * 512],
                                psx[:, 0:512], Act.Identity,
                                bias=brow[:, pair:pair + 1], scale=1.0), "A", 1)
                        k.wait(te, "A", av)
                k.wait(te, "P", vt_av_P[par])
                k.wait(te, "V", vt_ones_V)
                for jh in range(2):
                    jbase = jh * 8
                    for c in range(DC):
                        upar = c % 2
                        k.wait(sy, "P", ustr_free_P[upar])
                        lv = k.inc(sy.dma_start(
                            out=ustr[upar][:],
                            in_=dr["usp"][c * 128:(c + 1) * 128,
                                          jbase * 128: jbase * 128 + IT]),
                            f"Lu{upar}", 16)
                        k.wait(te, f"Lu{upar}", lv)
                        for jj in range(8):
                            psx = psC if jj < 4 else psD
                            mm = te.matmul(
                                psx[:, (jj % 4) * 128:(jj % 4) * 128 + 128],
                                ustr[upar][:, jj * 128:(jj + 1) * 128],
                                wv16[:, c * 128:(c + 1) * 128],
                                start=(c == 0 and jj % 4 == 0), stop=False)
                        ustr_free_P[upar] = k.inc(mm, "P", 1)
                    for jj in range(8):
                        psx = psC if jj < 4 else psD
                        mm = te.matmul(psx[:, (jj % 4) * 128:(jj % 4) * 128 + 128],
                                       g["ones_r16"][:],
                                       g["bv16"][0:1, pair * 128:(pair + 1) * 128],
                                       start=False, stop=(jj % 4 == 3))
                    pv = k.inc(mm, "P", 1)
                    k.wait(sc, "P", pv)
                    for jj in range(8):
                        jt = jbase + jj
                        psx = psC if jj < 4 else psD
                        co = (jj % 4) * 128
                        sc.activation(vt16[par][:, jt * 130: jt * 130 + 64],
                                      psx[:, co: co + 64], Act.Copy,
                                      bias=0.0, scale=1.0)
                        av = k.inc(sc.activation(
                            vt16[par][:, jt * 130 + 65: jt * 130 + 129],
                            psx[:, co + 64: co + 128], Act.Copy,
                            bias=0.0, scale=1.0), "A", 1)
                    k.wait(te, "A", av)
                vt_ready_A = k.cnt["A"]
                k.wait(te, "A", vt_ready_A)
                k.wait(te, "V", cv_read_V[par])
                for hh in range(2):
                    for jt in range(NJT):
                        mm = te.matmul(
                            psC[0:64, hh:hh + 1],
                            vt16[par][:, jt * 130 + hh * 65: jt * 130 + hh * 65 + 64],
                            g["ones_c16"][:],
                            start=(jt == 0), stop=(jt == NJT - 1))
                pv = k.inc(mm, "P", 1)
                k.wait(ve, "P", pv)
                k.inc(ve.tensor_scalar(cv16[:, 2 * par:2 * par + 2],
                                       psC[0:64, 0:2], 0.5, None, Alu.mult),
                      "V", 1)
                qkv_used_P[0] = k.cnt["P"]
                qk_ready_A[par] = k.cnt["A"]

            qk_ready_A = [0, 0]
            av_evac_V = 0
            stream_w(0)
            emit_qkv(0)
            for pair in range(NPAIR):
                par = pair % 2
                if pair + 1 < NPAIR:
                    stream_w(pair + 1)
                    emit_qkv(pair + 1)
                k.wait(te, "A", qk_ready_A[par])
                for hh in range(2):
                    h = pair * 2 + hh
                    for it in range(NIT):
                        ilo = it * IT
                        for jt in range(NJT):
                            sh = jt % 2
                            shalf = ps_main[:, sh * IT:(sh + 1) * IT]
                            k.wait(te, "V", s_evac_V[sh])
                            for ih in range(2):
                                mm = te.matmul(
                                    shalf[:, ih * 512:(ih + 1) * 512],
                                    k16[par][64 * hh:64 * hh + 64,
                                             jt * 128:(jt + 1) * 128],
                                    q16[par][64 * hh:64 * hh + 64,
                                             ilo + ih * 512: ilo + (ih + 1) * 512],
                                    start=True, stop=True)
                            pv = k.inc(mm, "P", 1)
                            k.wait(ve, "P", pv)
                            k.wait(ve, "V", s16_free_V)
                            sv = k.inc(ve.tensor_copy(s16[:], shalf[:]), "V", 1)
                            s_evac_V[sh] = sv
                            for fi in range(NFREQ):
                                vv = ve.tensor_scalar(
                                    slabY[:, fi * IT:(fi + 1) * IT],
                                    s16[:], k1_ap(g, fi), k2_ap(g, h, fi),
                                    Alu.mult, Alu.add)
                            s16_free_V = k.inc(vv, "V", 1)
                            ve.tensor_copy(slabN[:], slabY[:])
                            ve.tensor_tensor(slabY[:], slabY[:], slabN[:],
                                             Alu.subtract)
                            ve.tensor_scalar(slabN[:], slabY[:], 0.5, None,
                                             Alu.is_ge)
                            k.inc(ve.tensor_tensor(slabY[:], slabY[:], slabN[:],
                                                   Alu.subtract), "V", 1)
                            k.wait(ve, "A", acc_slice_A[jt])
                            for fp in range(3):
                                f0 = fp * 2
                                nf = 2 if fp < 2 else 1
                                k.wait(sc, "V")
                                av = k.inc(sc.activation(
                                    slabS[:, 0:nf * IT],
                                    slabY[:, f0 * IT:(f0 + nf) * IT],
                                    Act.Sin, bias=0.0, scale=cst(g, 4)), "A", 1)
                                k.wait(ve, "A", av)
                                for ff in range(nf):
                                    fi = f0 + ff
                                    if fi == 0:
                                        vv = ve.tensor_scalar(
                                            acc16[:, jt * IT:(jt + 1) * IT],
                                            slabS[:, 0:IT], al_ap(g, h, 0),
                                            None, Alu.mult)
                                    else:
                                        ve.tensor_scalar(
                                            ctmp16[:],
                                            slabS[:, ff * IT:(ff + 1) * IT],
                                            al_ap(g, h, fi), None, Alu.mult)
                                        vv = ve.tensor_tensor(
                                            acc16[:, jt * IT:(jt + 1) * IT],
                                            acc16[:, jt * IT:(jt + 1) * IT],
                                            ctmp16[:], Alu.add)
                                k.inc(vv, "V", 1)
                            if jt % 2 == 1:
                                grp = jt // 2
                                rslot = grp % 2
                                k.wait(sc, "V")
                                k.wait(sc, "P", w16r_free_P[rslot])
                                av = k.inc(sc.activation(
                                    w16r[rslot][:],
                                    acc16[:, (jt - 1) * IT:(jt + 1) * IT],
                                    Act.Tanh, bias=0.0, scale=cst(g, 3)), "A", 1)
                                acc_slice_A[jt - 1] = av
                                acc_slice_A[jt] = av
                                k.wait(te, "A", av)
                                if grp == 0:
                                    k.wait(te, "V", av_evac_V)
                                for jj in range(2):
                                    jc = grp * 2 + jj
                                    for ih in range(2):
                                        psx = psA if ih == 0 else psB
                                        mm = te.matmul(
                                            psx[0:65, 0:512],
                                            vt16[par][:, jc * 130 + hh * 65:
                                                      jc * 130 + hh * 65 + 65],
                                            w16r[rslot][:, jj * IT + ih * 512:
                                                        jj * IT + (ih + 1) * 512],
                                            start=(jc == 0), stop=(jc == NJT - 1))
                                w16r_free_P[rslot] = k.inc(mm, "P", 1)
                        av_done_P = k.cnt["P"]
                        apar = h % 2
                        k.wait(ve, "P", av_done_P)
                        if it == 0:
                            k.wait(ve, f"Ga{apar}", ah_free_G[apar])
                        k.wait(ve, "Gr", rssc_free_G)
                        for ih in range(2):
                            psx = psA if ih == 0 else psB
                            ve.tensor_scalar(
                                ah16[apar][:, ilo + ih * 512: ilo + (ih + 1) * 512],
                                psx[0:64, 0:512], 0.5,
                                cv16[:, 2 * par + hh:2 * par + hh + 1],
                                Alu.mult, Alu.add)
                            vv = ve.tensor_copy(
                                rssc[64:65, ih * 512:(ih + 1) * 512],
                                psx[64:65, 0:512])
                        avv = k.inc(vv, "V", 1)
                        av_evac_V = avv
                        cv_read_V[par] = avv
                        k.wait(gp, "V", avv)
                        rssc_free_G = k.inc(gp.dma_start(
                            out=rs[h:h + 1, ilo:ilo + IT],
                            in_=rssc[64:65, 0:IT]), "Gr", 16)
                        if it == NIT - 1:
                            ah_free_G[apar] = k.inc(gp.dma_start(
                                out=dr["asp"][h, :, :], in_=ah16[apar][:]),
                                f"Ga{apar}", 16)
                vt_av_P[par] = k.cnt["P"]
                qk_scores_P[par] = k.cnt["P"]
            k.wait(sc, "V")
            k.wait(sc, "Gr")
            sc.activation(lnt16[:], rs[:], Act.Ln, bias=cst(g, 1)[0:16, :],
                          scale=cst(g, 3)[0:16, :])
            k.inc(sc.activation(nrm16[:], lnt16[:], Act.Exp, bias=0.0,
                                scale=cst(g, 5)[0:16, :]), "A", 1)
        nrm_A = k.cnt["A"]
        a_spill_G = [k.cnt.get("Ga0", 0), k.cnt.get("Ga1", 0)]
        k.barrier()

        with contextlib.ExitStack() as es:
            def sb(name, shape, dtype):
                return es.enter_context(nc.sbuf_tensor(name, shape, dtype))
            k.dsem("Lwo0"), k.dsem("Lwo1")
            psO = es.enter_context(nc.psum_tensor("wo_O", [128, 512], F32))
            psN = es.enter_context(nc.psum_tensor("wo_N", [128, 512], F32))
            psS1 = es.enter_context(nc.psum_tensor("wo_S1", [128, 512], F32))
            psS2 = es.enter_context(nc.psum_tensor("wo_S2", [128, 512], F32))
            wo16 = sb("wo16", [128, DC * D], F16)
            wstg = [sb(f"wo_stg_{i}", [128, D], F32) for i in range(2)]
            ahl = [sb(f"wo_ahl_{i}", [64, 512], F16) for i in range(2)]
            nbc16 = [sb(f"wo_nbc_{i}", [64, 512], F16) for i in range(2)]
            nrow16 = [sb(f"wo_nrow_{i}", [1, 512], F16) for i in range(2)]
            anrm = [sb(f"wo_anrm_{i}", [64, 512], F16) for i in range(2)]
            worhs = sb("wo_rhs", [128, DC * 512], F16)
            o_sb = sb("wo_osb", [128, DC * 512], F32)
            sq32 = sb("wo_sq32", [128, LNT], F32)
            sq32b = sb("wo_sq32b", [128, LNT], F32)
            rowA = sb("wo_rowA", [1, LNT], F16)
            rowB = sb("wo_rowB", [1, LNT], F16)
            rowC = sb("wo_rowC", [1, LNT], F16)
            ab16 = sb("wo_ab16", [128, 2 * LNT], F16)
            t32 = sb("wo_t32", [128, 512], F32)
            aout32 = [sb(f"wo_aout_{i}", [128, 512], F32) for i in range(2)]

            wo_conv_V = 0
            for c in range(DC):
                par = c % 2
                k.wait(sy, "V", wo_conv_V if c >= 2 else 0)
                lv = k.inc(sy.dma_start(out=wstg[par][:],
                                        in_=dr["woT"][c * 128:(c + 1) * 128, :]),
                           f"Lwo{par}", 16)
                k.wait(ve, f"Lwo{par}", lv)
                wo_conv_V = k.inc(ve.tensor_copy(wo16[:, c * D:(c + 1) * D],
                                                 wstg[par][:]), "V", 1)

            rel_free_V = [0, 0]
            nbc_free_V = [0, 0]
            nrow_mm_P = [0, 0]
            for _nm in ("Gl0", "Gl1", "Gn0", "Gn1", "Gm0", "Gm1", "Go0", "Go1"):
                k.dsem(_nm)
            anrm_free_G = [0, 0]
            worhs_ready_G = 0
            aout_free_G = [0, 0]
            wo_mm_P = 0
            osb_read_V = 0
            for it4 in range(4):
                tlo = it4 * 512
                k.wait(gp, "P", wo_mm_P)
                for c in range(DC):
                    for hh in range(2):
                        hx = 2 * c + hh
                        par = hx % 2
                        k.wait(gp, "Ga0", a_spill_G[0])
                        k.wait(gp, "Ga1", a_spill_G[1])
                        k.wait(gp, "V", rel_free_V[par])
                        lv = k.inc(gp.dma_start(
                            out=ahl[par][:],
                            in_=dr["asp"][hx, :, tlo:tlo + 512]), f"Gl{par}", 16)
                        k.wait(gp, "A", nrm_A)
                        k.wait(gp, "P", nrow_mm_P[par])
                        nlv = k.inc(gp.dma_start(
                            out=nrow16[par][:],
                            in_=nrm16[hx:hx + 1, tlo:tlo + 512]), f"Gn{par}", 16)
                        k.wait(te, f"Gn{par}", nlv)
                        k.wait(te, "V", nbc_free_V[par])
                        pv = k.inc(te.matmul(
                            psN[0:64, 0:512],
                            g["ones_r16"][0:1, 0:64],
                            nrow16[par][:],
                            start=True, stop=True), "P", 1)
                        nrow_mm_P[par] = pv
                        k.wait(ve, "P", pv)
                        nbv = k.inc(ve.tensor_copy(
                            nbc16[par][:], psN[0:64, 0:512]), "V", 1)
                        nbc_free_V[par] = nbv
                        k.wait(ve, f"Gl{par}", lv)
                        k.wait(ve, f"Gm{par}", anrm_free_G[par])
                        rel_free_V[par] = k.inc(ve.tensor_tensor(
                            anrm[par][:], ahl[par][:], nbc16[par][:], Alu.mult),
                            "V", 1)
                        k.wait(gp, "V", rel_free_V[par])
                        anrm_free_G[par] = k.inc(gp.dma_start(
                            out=worhs[hh * 64:(hh + 1) * 64,
                                      c * 512:(c + 1) * 512],
                            in_=anrm[par][:]), f"Gm{par}", 16)
                k.wait(te, "Gm0", k.cnt["Gm0"])
                k.wait(te, "Gm1", k.cnt["Gm1"])
                k.wait(te, "V", wo_conv_V)
                k.wait(sc, "V", osb_read_V)
                for hc in range(DC):
                    for c in range(DC):
                        mm = te.matmul(psO[:, 0:512],
                                       wo16[:, c * D + hc * 128: c * D + (hc + 1) * 128],
                                       worhs[:, c * 512:(c + 1) * 512],
                                       start=(c == 0), stop=(c == DC - 1))
                    pv = k.inc(mm, "P", 1)
                    k.wait(sc, "P", pv)
                    av = k.inc(sc.activation(o_sb[:, hc * 512:(hc + 1) * 512],
                                             psO[:, 0:512], Act.Identity,
                                             bias=g["bor"][:, hc:hc + 1],
                                             scale=1.0), "A", 1)
                    k.wait(te, "A", av)
                wo_mm_P = k.cnt["P"]
                abv = emit_ln_tile(
                    k, g, lambda c: o_sb[:, c * 512:(c + 1) * 512],
                    psS1, psS2, psS1, psS2, [sq32, sq32b], rowA, rowB, rowC, ab16)
                for c in range(DC):
                    par = c % 2
                    k.wait(ve, "V", abv)
                    k.wait(ve, f"Go{par}", aout_free_G[par])
                    ve.tensor_tensor(t32[:], o_sb[:, c * 512:(c + 1) * 512],
                                     ab16[:, 0:LNT], Alu.mult)
                    ve.tensor_tensor(t32[:], t32[:], ab16[:, LNT:2 * LNT], Alu.add)
                    vv = k.inc(ve.tensor_scalar(aout32[par][:], t32[:],
                                                g["gr"][:, c:c + 1],
                                                g["br"][:, c:c + 1],
                                                Alu.mult, Alu.add), "V", 1)
                    k.wait(gp, "V", vv)
                    aout_free_G[par] = k.inc(gp.dma_start(
                        out=g["r"][:, c * NT + tlo: c * NT + tlo + 512],
                        in_=aout32[par][:], accum_op=Alu.add), f"Go{par}", 16)
                osb_read_V = k.cnt["V"]
    k.barrier()



def emit_output(k, g, dr):
    k.barrier()
    gp = k.gp
    for c in range(DC):
        k.inc(gp.dma_start(out=dr["outT"][c * 128:(c + 1) * 128, :],
                           in_=g["r"][:, c * NT:(c + 1) * NT]), "G", 16)
    k.wait(gp, "G")


def build_program():
    nc = bass.Bass("TRN2", target_bir_lowering=False,
                   detect_race_conditions=DEBUG_SYNC)
    dr = {}
    dr["xT"] = nc.dram_tensor("xT", [D, NT], F32, kind="ExternalInput").ap()
    dr["wqT"] = nc.dram_tensor("wqT", [D, D], F32, kind="ExternalInput").ap()
    dr["wkT"] = nc.dram_tensor("wkT", [D, D], F32, kind="ExternalInput").ap()
    dr["wvT"] = nc.dram_tensor("wvT", [D, D], F32, kind="ExternalInput").ap()
    dr["woT"] = nc.dram_tensor("woT", [D, D], F32, kind="ExternalInput").ap()
    dr["w1T"] = nc.dram_tensor("w1T", [E, D, F], F32, kind="ExternalInput").ap()
    dr["w2T"] = nc.dram_tensor("w2T", [E, F, D], F32, kind="ExternalInput").ap()
    dr["rwT"] = nc.dram_tensor("rwT", [D, 8], F32, kind="ExternalInput").ap()
    dr["b1r"] = nc.dram_tensor("b1r", [128, 128], F32, kind="ExternalInput").ap()
    dr["b2"] = nc.dram_tensor("b2", [8, D], F32, kind="ExternalInput").ap()
    for nm in ("bqr", "bkr", "bor", "gr", "br"):
        dr[nm] = nc.dram_tensor(nm, [128, 8], F32, kind="ExternalInput").ap()
    dr["bv"] = nc.dram_tensor("bv", [1, D], F32, kind="ExternalInput").ap()
    dr["alphaf"] = nc.dram_tensor("alphaf", [1, 80], F32, kind="ExternalInput").ap()
    dr["phif"] = nc.dram_tensor("phif", [1, 80], F32, kind="ExternalInput").ap()
    dr["temp"] = nc.dram_tensor("temp", [1, 1], F32, kind="ExternalInput").ap()
    dr["outT"] = nc.dram_tensor("outT", [D, NT], F32, kind="ExternalOutput").ap()
    dr["usp"] = nc.dram_tensor("usp", [D, NT], F16, kind="Internal").ap()
    dr["asp"] = nc.dram_tensor("asp", [NH, 64, NT], F16, kind="Internal").ap()

    with contextlib.ExitStack() as es:
        k = Ctx(nc, es)
        g = {}
        emit_setup(k, dr, g)
        if DO_FFN1:
            emit_ffn(k, g, dr, "f1")
        if DO_ATTN:
            emit_attention(k, g, dr)
        if DO_FFN2:
            emit_ffn(k, g, dr, "f2")
        emit_output(k, g, dr)
    return nc


def host_prep(inputs):
    asc = np.ascontiguousarray
    w = {}
    w["wqT"] = asc(inputs["L0_attn_Wq"].T)
    w["wkT"] = asc(inputs["L0_attn_Wk"].T)
    w["wvT"] = asc(inputs["L0_attn_Wv"].T)
    w["woT"] = asc(inputs["L0_attn_Wo"].T)
    w["w1T"] = asc(np.transpose(inputs["L0_ffn_w1"], (0, 2, 1)))
    w["w2T"] = asc(np.transpose(inputs["L0_ffn_w2"], (0, 2, 1)))
    w["rwT"] = asc(inputs["L0_ffn_router_w"].T)
    w["b1r"] = asc(np.transpose(inputs["L0_ffn_b1"].reshape(8, 16, 128),
                                (2, 0, 1)).reshape(128, 128))
    w["b2"] = asc(inputs["L0_ffn_b2"])
    w["bqr"] = asc(inputs["L0_attn_bq"].reshape(8, 128).T)
    w["bkr"] = asc(inputs["L0_attn_bk"].reshape(8, 128).T)
    w["bor"] = asc(inputs["L0_attn_bo"].reshape(8, 128).T)
    w["gr"] = asc(inputs["L0_attn_gamma"].reshape(8, 128).T)
    w["br"] = asc(inputs["L0_attn_beta"].reshape(8, 128).T)
    w["bv"] = asc(inputs["L0_attn_bv"].reshape(1, D))
    w["alphaf"] = asc(inputs["L0_attn_alpha"].reshape(1, 80))
    w["phif"] = asc(inputs["L0_attn_phi"].reshape(1, 80))
    w["temp"] = asc(inputs["L0_attn_temperature"].reshape(1, 1))
    return w


def kernel(**inputs):
    x = np.asarray(inputs["L0_attn_x"], dtype=np.float32)
    inputs = {kk: np.asarray(v, dtype=np.float32) for kk, v in inputs.items()}
    w = host_prep(inputs)
    nc = build_program()
    in_maps = []
    for c in range(8):
        m = dict(w)
        m["xT"] = np.ascontiguousarray(x[c].T)
        in_maps.append(m)
    res = run_bass_kernel_spmd(nc, in_maps, core_ids=list(range(8)))
    out = np.stack([np.asarray(res.results[c]["outT"]).T for c in range(8)])
    return out.astype(np.float32)


# version 02
# speedup vs baseline: 1.4180x; 1.4180x over previous
import contextlib
import numpy as np

import concourse.bass as bass
import concourse.mybir as mybir
from concourse.bass_utils import run_bass_kernel_spmd

F32 = mybir.dt.float32
F16 = mybir.dt.float16
I16 = mybir.dt.int16
Alu = mybir.AluOpType
Act = mybir.ActivationFunctionType
AX = mybir.AxisListType.X

NT = 2048
D = 1024
DC = 8
F = 2048
FC = 16
E = 8
NH = 16
HD = 64
NPAIR = 8
TT = 256
NTT = NT // TT
LNT = 512
FREQS = [31.0, 37.0, 41.0, 43.0, 47.0]
NFREQ = 5
TWO_PI_SAFE = 6.283185
INV_2PI = 1.0 / (2.0 * np.pi)
IT = 1024
NIT = NT // IT
NJT = NT // 128

DO_FFN1 = True
DO_ATTN = True
DO_FFN2 = True
DEBUG_TAPS = False
DEBUG_SYNC = False


class _DrainAfter:
    def __init__(self, eng):
        self._eng = eng

    def __getattr__(self, name):
        attr = getattr(self._eng, name)
        if name in ("wait_ge", "drain", "dma_start", "sem_inc", "nop"):
            return attr
        if not callable(attr):
            return attr

        def wrapped(*a, **kw):
            r = attr(*a, **kw)
            try:
                self._eng.drain()
            except Exception:
                pass
            return r
        return wrapped


class Ctx:
    def __init__(self, nc, es):
        self.nc = nc
        self.es = es
        self.sy, self.ve, self.sc, self.te, self.gp = (
            nc.sync, nc.vector, nc.scalar, nc.tensor, nc.gpsimd)
        if DEBUG_SYNC:
            self.ve = _DrainAfter(nc.vector)
            self.sc = _DrainAfter(nc.scalar)
        self.engs = [self.sy, self.ve, self.sc, self.te, self.gp]
        self.sems = {}
        self.cnt = {}
        self.lastwait = {}
        for name in ("L", "G", "P", "A", "V", "Q"):
            self.sems[name] = es.enter_context(nc.semaphore("sem" + name))
            self.cnt[name] = 0

    def dsem(self, name):
        if name not in self.sems:
            self.sems[name] = self.es.enter_context(
                self.nc.semaphore("sem" + name))
            self.cnt[name] = 0
        return self.sems[name]

    def inc(self, instr, name, amt=1):
        if name not in self.sems:
            self.dsem(name)
        instr.then_inc(self.sems[name], amt)
        self.cnt[name] += amt
        return self.cnt[name]

    def wait(self, eng, name, val=None):
        if val is None:
            val = self.cnt[name]
        if val <= 0:
            return
        key = (id(eng), name)
        if self.lastwait.get(key, -1) >= val:
            return
        eng.wait_ge(self.sems[name], val)
        self.lastwait[key] = val

    def barrier(self):
        for eng in self.engs:
            for name in list(self.sems):
                self.wait(eng, name)

    def sb(self, name, shape, dtype):
        return self.es.enter_context(self.nc.sbuf_tensor(name, shape, dtype))



def emit_setup(k, dr, g):
    nc = k.nc
    g["r"] = k.sb("r", [128, DC * NT], F32)
    g["b1r"] = k.sb("s_b1r", [128, 128], F32)
    g["b2_16"] = k.sb("b2_16", [8, D], F16)
    for nm in ("bqr", "bkr", "bor", "gr", "br"):
        g[nm] = k.sb("s_" + nm, [128, 8], F32)
    g["bv16"] = k.sb("bv16", [1, D], F16)
    g["rw16"] = k.sb("rw16", [128, 64], F16)
    g["ident16"] = k.sb("ident16", [128, 128], F16)
    g["ones_c32"] = k.sb("ones_c32", [128, 1], F32)
    g["ones_c16"] = k.sb("ones_c16", [128, 1], F16)
    g["ones_r32"] = k.sb("ones_r32", [1, 128], F32)
    g["ones_r16"] = k.sb("ones_r16", [1, 128], F16)
    g["kvec"] = k.sb("kvec", [128, 176], F32)
    g["cst"] = k.sb("cst", [128, 8], F32)

    sy, ve, sc, te, gp = k.sy, k.ve, k.sc, k.te, k.gp

    with contextlib.ExitStack() as es:
        def sb(name, shape, dtype):
            return es.enter_context(nc.sbuf_tensor(name, shape, dtype))
        rows = sb("rows", [1, 96], F32)
        alphast = sb("alphast", [1, 80], F32)
        phist = sb("phist", [1, 80], F32)
        t11 = sb("t11", [1, 1], F32)
        iop = sb("iop", [128, 1], F32)
        iof = sb("iof", [128, 128], F32)
        b2f = sb("b2f_stage", [8, D], F32)
        bvf = sb("bv_stage", [1, D], F32)
        rwf = sb("rw_stage", [128, 64], F32)

        for c in range(DC):
            k.inc(sy.dma_start(out=g["r"][:, c * NT:(c + 1) * NT],
                               in_=dr["xT"][c * 128:(c + 1) * 128, :]), "L", 16)
        k.inc(sy.dma_start(out=g["b1r"][:], in_=dr["b1r"]), "L", 16)
        k.inc(sy.dma_start(out=b2f[:], in_=dr["b2"]), "L", 16)
        for nm in ("bqr", "bkr", "bor", "gr", "br"):
            k.inc(sy.dma_start(out=g[nm][:], in_=dr[nm]), "L", 16)
        k.inc(sy.dma_start(out=bvf[:], in_=dr["bv"]), "L", 16)
        k.inc(sy.dma_start(out=rwf[:].rearrange("p (c e) -> p c e", e=8),
                           in_=dr["rwT"].rearrange("(c p) e -> p c e", p=128)),
              "L", 16)
        k.inc(sy.dma_start(out=phist[:], in_=dr["phif"]), "L", 16)
        k.inc(sy.dma_start(out=alphast[:], in_=dr["alphaf"]), "L", 16)
        k.inc(sy.dma_start(out=t11[:], in_=dr["temp"]), "L", 16)
        ldone = k.cnt["L"]

        k.inc(gp.iota(iop[:], pattern=[[0, 1]], channel_multiplier=1,
                      allow_small_or_imprecise_dtypes=True), "Q", 1)
        k.inc(gp.iota(iof[:], pattern=[[1, 128]], channel_multiplier=0,
                      allow_small_or_imprecise_dtypes=True), "Q", 1)

        k.wait(ve, "L", ldone)
        k.wait(ve, "Q")
        ve.tensor_scalar(g["ident16"][:], iof[:], iop[:], None, Alu.is_equal)
        for i, v in enumerate([1e-5, 1024.0 + 1e-10, -0.5, 0.5, TWO_PI_SAFE, -1.0]):
            ve.memset(g["cst"][:, i:i + 1], v)
        ve.memset(g["ones_c32"][:], 1.0)
        ve.memset(g["ones_c16"][:], 1.0)
        ve.memset(g["ones_r32"][:], 1.0)
        ve.memset(g["ones_r16"][:], 1.0)
        ve.tensor_copy(g["b2_16"][:], b2f[:])
        ve.tensor_copy(g["bv16"][:], bvf[:])
        ve.tensor_copy(g["rw16"][:], rwf[:])
        k.wait(sc, "L", ldone)
        k.inc(sc.activation(rows[0:1, 7:8], t11[:], Act.Abs, bias=0.0, scale=1.0),
              "A", 1)
        k.wait(ve, "A")
        ve.tensor_scalar(rows[0:1, 6:7], rows[0:1, 7:8], 8.0, 0.8, Alu.mult, Alu.add)
        ve.drain()
        ve.reciprocal(rows[0:1, 5:6], rows[0:1, 6:7])
        for i, f in enumerate(FREQS):
            ve.memset(rows[0:1, i:i + 1], 1.0 / f)
        ve.drain()
        ve.tensor_scalar(rows[0:1, 8:13], rows[0:1, 0:5], rows[0:1, 5:6],
                         None, Alu.mult)
        k.inc(ve.tensor_scalar(rows[0:1, 16:96], phist[:], INV_2PI, 2.0,
                               Alu.mult, Alu.add), "V", 1)
        with nc.psum_tensor("ps_setup", [128, 512], F32) as psb:
            k.wait(te, "V")
            te.matmul(psb[:, 0:5], g["ones_r32"][:], rows[0:1, 8:13],
                      start=True, stop=True)
            te.matmul(psb[:, 16:96], g["ones_r32"][:], rows[0:1, 16:96],
                      start=True, stop=True)
            k.inc(te.matmul(psb[:, 96:176], g["ones_r32"][:], alphast[:],
                            start=True, stop=True), "P", 1)
            k.wait(ve, "P")
            ve.tensor_copy(g["kvec"][:, 0:5], psb[:, 0:5])
            ve.tensor_copy(g["kvec"][:, 16:96], psb[:, 16:96])
            k.inc(ve.tensor_copy(g["kvec"][:, 96:176], psb[:, 96:176]), "V", 1)
        k.barrier()


def cst(g, i):
    return g["cst"][:, i:i + 1]


def k1_ap(g, fi):
    return g["kvec"][:, fi:fi + 1]


def k2_ap(g, h, fi):
    return g["kvec"][:, 16 + h * 5 + fi:16 + h * 5 + fi + 1]


def al_ap(g, h, fi):
    return g["kvec"][:, 96 + h * 5 + fi:96 + h * 5 + fi + 1]



def emit_ln_tile(k, g, src_fn, psSUM, psSQ, psBC1, psBC2, sqp, rowA, rowB,
                 rowC, ab16):
    te, sc, ve = k.te, k.sc, k.ve
    k.wait(te, "V")
    sq_mm_P = [0, 0]
    for c in range(DC):
        par = c % 2
        k.wait(sc, "P", sq_mm_P[par])
        av = k.inc(sc.activation(sqp[par][:], src_fn(c), Act.Square,
                                 bias=0.0, scale=1.0), "A", 1)
        te.matmul(psSUM[0:1, 0:LNT], g["ones_c32"][:], src_fn(c),
                  start=(c == 0), stop=(c == DC - 1))
        k.wait(te, "A", av)
        mm = te.matmul(psSQ[0:1, 0:LNT], g["ones_c32"][:], sqp[par][:],
                       start=(c == 0), stop=(c == DC - 1))
        sq_mm_P[par] = k.inc(mm, "P", 1)
    pv = k.cnt["P"]
    k.wait(ve, "P", pv)
    ve.tensor_scalar(rowA[:], psSUM[0:1, 0:LNT], 1.0 / D, None, Alu.mult)
    ve.tensor_scalar(psSUM[0:1, 0:LNT], psSQ[0:1, 0:LNT], 1.0 / D, None, Alu.mult)
    ve.tensor_tensor(rowB[:], rowA[:], rowA[:], Alu.mult)
    k.inc(ve.tensor_tensor(psSQ[0:1, 0:LNT], psSUM[0:1, 0:LNT], rowB[:],
                           Alu.subtract), "V", 1)
    k.wait(sc, "V")
    sc.activation(psSUM[0:1, 0:LNT], psSQ[0:1, 0:LNT], Act.Ln,
                  bias=cst(g, 0)[0:1, :], scale=1.0)
    k.inc(sc.activation(rowC[:], psSUM[0:1, 0:LNT], Act.Exp, bias=0.0,
                        scale=cst(g, 2)[0:1, :]), "A", 1)
    k.wait(ve, "A")
    ve.tensor_tensor(rowB[:], rowA[:], rowC[:], Alu.mult)
    k.inc(ve.tensor_scalar(rowA[:], rowB[:], -1.0, None, Alu.mult), "V", 1)
    k.wait(te, "V")
    te.matmul(psBC1[:, 0:LNT], g["ones_r16"][:], rowC[:], start=True, stop=True)
    pv = k.inc(te.matmul(psBC2[:, 0:LNT], g["ones_r16"][:], rowA[:],
                         start=True, stop=True), "P", 1)
    k.wait(ve, "P", pv)
    ve.tensor_copy(ab16[:, 0:LNT], psBC1[:, 0:LNT])
    return k.inc(ve.tensor_copy(ab16[:, LNT:2 * LNT], psBC2[:, 0:LNT]), "V", 1)



def emit_ffn(k, g, dr, tag):
    nc = k.nc
    te, sc, ve, sy, gp = k.te, k.sc, k.ve, k.sy, k.gp
    k.barrier()
    with contextlib.ExitStack() as es:
        def sb(name, shape, dtype):
            return es.enter_context(nc.sbuf_tensor(f"{name}_{tag}", shape, dtype))

        def psf(name, shape, dtype=F32):
            return es.enter_context(nc.psum_tensor(f"{name}_{tag}", shape, dtype))

        k.dsem("Lst0"), k.dsem("Lst1"), k.dsem("Gc0"), k.dsem("Gc1")
        u16 = sb("u16", [128, DC * NT], F16)
        w1s = sb("w1s", [128, DC * F], F16)
        w2s = sb("w2s", [128, FC * D], F16)
        stage = sb("wstage", [128, 2048], F32)
        g16 = [sb(f"g16_{i}", [128, FC * TT], F16) for i in range(2)]
        graw = sb("graw", [128, 512], F16)
        tmp32 = [sb(f"tmp32_{i}", [128, 512], F32) for i in range(2)]
        rowA = sb("rowA", [1, LNT], F16)
        rowB = sb("rowB", [1, LNT], F16)
        rowC = sb("rowC", [1, LNT], F16)
        ab16 = sb("ab16", [128, 2 * LNT], F16)
        t16 = graw
        cem16 = sb("cem16", [8, NT], F16)
        lgt = sb("lgt", [128, 8], F32)
        msk1 = sb("msk1", [128, 8], F32)
        msk2 = sb("msk2", [128, 8], F32)
        xmk = sb("xmk", [128, 8], F32)
        mrow = sb("mrow", [128, 8], F32)
        ctm16 = sb("ctm16", [128, 8], F16)
        cbc16 = [sb(f"cbc16_{i}", [128, TT], F16) for i in range(2)]
        cemrow = [sb(f"cemrow_{i}", [1, TT], F16) for i in range(2)]

        ps_main = psf("ps_main", [128, 2048])
        psA = psf("psA", [128, 512])
        psB = psf("psB", [128, 512])
        psC = psf("psC", [128, 512])

        for tt in range(NT // LNT):
            lo = tt * LNT
            abv = emit_ln_tile(
                k, g,
                lambda c, lo=lo: g["r"][:, c * NT + lo: c * NT + lo + LNT],
                psA, psB, psC, psA, [tmp32[0], tmp32[1]], rowA, rowB, rowC, ab16)
            for c in range(DC):
                k.wait(ve, "V", abv)
                ve.tensor_tensor(t16[:], g["r"][:, c * NT + lo: c * NT + lo + LNT],
                                 ab16[:, 0:LNT], Alu.mult)
                vv = ve.tensor_tensor(u16[:, c * NT + lo: c * NT + lo + LNT],
                                      t16[:], ab16[:, LNT:2 * LNT], Alu.add)
            k.inc(vv, "V", 1)
        u_ready_V = k.cnt["V"]

        with nc.psum_tensor(f"psT16_{tag}", [128, 128], F16) as psT16:
            for tj in range(NT // 128):
                jlo = tj * 128
                k.wait(te, "V", u_ready_V)
                for c in range(DC):
                    mm = te.matmul(psA[:, 0:8],
                                   u16[:, c * NT + jlo: c * NT + jlo + 128],
                                   g["rw16"][:, 8 * c:8 * c + 8],
                                   start=(c == 0), stop=(c == DC - 1))
                pv = k.inc(mm, "P", 1)
                k.wait(ve, "P", pv)
                k.inc(ve.tensor_copy(lgt[:], psA[:, 0:8]), "V", 1)
                ve.drain()
                ve.reduce_max(mrow[:, 0:1], lgt[:], axis=AX)
                ve.drain()
                ve.tensor_scalar(msk1[:], lgt[:], mrow[:, 0:1], None, Alu.is_equal)
                ve.drain()
                ve.scalar_tensor_tensor(xmk[:], msk1[:], -1e30, lgt[:],
                                        Alu.mult, Alu.add)
                ve.drain()
                ve.reduce_max(mrow[:, 1:2], xmk[:], axis=AX)
                ve.drain()
                ve.tensor_scalar(msk2[:], xmk[:], mrow[:, 1:2], None, Alu.is_equal)
                ve.drain()
                k.inc(ve.tensor_tensor(mrow[:, 2:3], mrow[:, 0:1], mrow[:, 1:2],
                                       Alu.subtract), "V", 1)
                k.wait(sc, "V")
                k.inc(sc.activation(mrow[:, 3:4], mrow[:, 2:3], Act.Tanh,
                                    bias=0.0, scale=cst(g, 3)), "A", 1)
                k.wait(ve, "A")
                ve.tensor_scalar(mrow[:, 4:5], mrow[:, 3:4], 0.25, 0.25,
                                 Alu.mult, Alu.add)
                ve.tensor_scalar(mrow[:, 5:6], mrow[:, 3:4], -0.25, 0.25,
                                 Alu.mult, Alu.add)
                ve.drain()
                ve.tensor_scalar(xmk[:], msk1[:], mrow[:, 4:5], None, Alu.mult)
                ve.drain()
                k.inc(ve.scalar_tensor_tensor(ctm16[:], msk2[:], mrow[:, 5:6],
                                              xmk[:], Alu.mult, Alu.add), "V", 1)
                k.wait(te, "V")
                pv = k.inc(te.transpose(psT16[0:8, 0:128], ctm16[:],
                                        g["ident16"][:]), "P", 1)
                k.wait(ve, "P", pv)
                k.inc(ve.tensor_copy(cem16[0:8, jlo:jlo + 128],
                                     psT16[0:8, 0:128]), "V", 1)
        cem_ready_V = k.cnt["V"]

        if DEBUG_TAPS:
            dbg_u = k.nc.dram_tensor(f"dbg_u16_{tag}", [128, DC * NT], F16,
                                     kind="ExternalOutput").ap()
            dbg_c = k.nc.dram_tensor(f"dbg_cem_{tag}", [8, NT], F16,
                                     kind="ExternalOutput").ap()
            k.wait(gp, "V", cem_ready_V)
            k.inc(gp.dma_start(out=dbg_u, in_=u16[:]), "G", 16)
            k.inc(gp.dma_start(out=dbg_c, in_=cem16[:]), "G", 16)

        stage_free = [0, 0]

        def stream_w1(e, gate_P):
            for hc in range(DC):
                for half in range(2):
                    par = (hc * 2 + half) % 2
                    k.wait(sy, "V", stage_free[par])
                    lv = k.inc(sy.dma_start(
                        out=stage[:, par * 1024:(par + 1) * 1024],
                        in_=dr["w1T"][e, hc * 128:(hc + 1) * 128,
                                      half * 1024:(half + 1) * 1024]),
                        f"Lst{par}", 16)
                    k.wait(ve, f"Lst{par}", lv)
                    k.wait(ve, "P", gate_P)
                    stage_free[par] = k.inc(ve.tensor_copy(
                        w1s[:, hc * F + half * 1024: hc * F + (half + 1) * 1024],
                        stage[:, par * 1024:(par + 1) * 1024]), "V", 1)
            return k.cnt["V"]

        def stream_w2(e, gate_P):
            for fc in range(FC):
                par = fc % 2
                k.wait(sy, "V", stage_free[par])
                lv = k.inc(sy.dma_start(
                    out=stage[:, par * 1024:(par + 1) * 1024],
                    in_=dr["w2T"][e, fc * 128:(fc + 1) * 128, :]),
                    f"Lst{par}", 16)
                k.wait(ve, f"Lst{par}", lv)
                k.wait(ve, "P", gate_P)
                stage_free[par] = k.inc(ve.tensor_copy(
                    w2s[:, fc * D: fc * D + 1024],
                    stage[:, par * 1024:(par + 1) * 1024]), "V", 1)
            return k.cnt["V"]

        w1_ready = stream_w1(0, 0)
        w2_ready = stream_w2(0, 0)
        aw_done_P = 0
        bw_done_P = 0
        gelu_free_A = {0: 0, 1: 0}
        graw_free_V = 0
        cbc_free_V = 0
        g16_free_P = {0: 0, 1: 0}
        evac_last_A = 0
        k.dsem("Gt0"), k.dsem("Gt1")
        r_acc_G = [[0, 0] for _ in range(NTT)]
        tmp_free_G = [0, 0]

        cemrow_G = [0, 0]
        cemrow_mm_P = [0, 0]

        def issue_cemrow(e2, tt2):
            par2 = tt2 % 2
            k.wait(gp, "V", cem_ready_V)
            k.wait(gp, "P", cemrow_mm_P[par2])
            cemrow_G[par2] = k.inc(gp.dma_start(
                out=cemrow[par2][:],
                in_=cem16[e2:e2 + 1, tt2 * TT:(tt2 + 1) * TT]), f"Gc{par2}", 16)

        issue_cemrow(0, 0)
        for e in range(E):
            for tt in range(NTT):
                gpar = tt % 2
                lo = tt * TT
                if tt + 1 < NTT:
                    issue_cemrow(e, tt + 1)
                elif e + 1 < E:
                    issue_cemrow(e + 1, 0)
                k.wait(te, "V", max(cem_ready_V, cbc_free_V, w1_ready))
                k.wait(te, f"Gc{gpar}", cemrow_G[gpar])
                pv = k.inc(te.matmul(psC[:, 0:TT], g["ones_r16"][:],
                                     cemrow[gpar][:],
                                     start=True, stop=True), "P", 1)
                cemrow_mm_P[gpar] = pv
                k.wait(ve, "P", pv)
                cbv = k.inc(ve.tensor_copy(cbc16[gpar][:], psC[:, 0:TT]), "V", 1)
                cbc_free_V = cbv
                for fcp in range(FC // 2):
                    par = fcp % 2
                    psg = psA if par == 0 else psB
                    k.wait(te, "A", gelu_free_A[par])
                    k.wait(sc, "V", graw_free_V)
                    for sub in range(2):
                        fc = fcp * 2 + sub
                        for hc in range(DC):
                            mm = te.matmul(
                                psg[:, sub * TT:(sub + 1) * TT],
                                w1s[:, hc * F + fc * 128: hc * F + (fc + 1) * 128],
                                u16[:, hc * NT + lo: hc * NT + lo + TT],
                                start=(hc == 0), stop=(hc == DC - 1))
                    pv = k.inc(mm, "P", 1)
                    k.wait(sc, "P", max(pv, g16_free_P[gpar]))
                    for sub in range(2):
                        fc = fcp * 2 + sub
                        av = k.inc(sc.activation(
                            graw[:, sub * TT:(sub + 1) * TT],
                            psg[:, sub * TT:(sub + 1) * TT], Act.Gelu,
                            bias=g["b1r"][:, e * 16 + fc:e * 16 + fc + 1],
                            scale=1.0), "A", 1)
                    gelu_free_A[par] = av
                    k.wait(ve, "A", av)
                    k.wait(ve, "V", cbv)
                    k.wait(ve, "P", g16_free_P[gpar])
                    for sub in range(2):
                        fc = fcp * 2 + sub
                        vv = ve.tensor_tensor(
                            g16[gpar][:, fc * TT:(fc + 1) * TT],
                            graw[:, sub * TT:(sub + 1) * TT],
                            cbc16[gpar][:], Alu.mult)
                    graw_free_V = k.inc(vv, "V", 1)
                g_ready_V = k.cnt["V"]
                if tt == NTT - 1:
                    aw_done_P = k.cnt["P"]
                k.wait(te, "V", max(w2_ready, g_ready_V))
                k.wait(te, "A", evac_last_A)
                for hc in range(DC):
                    first = True
                    if e == 0:
                        te.matmul(ps_main[:, hc * TT:(hc + 1) * TT],
                                  g["b2_16"][:, hc * 128:(hc + 1) * 128],
                                  cem16[:, lo:lo + TT], start=True, stop=False)
                        first = False
                    for fc in range(FC):
                        mm = te.matmul(
                            ps_main[:, hc * TT:(hc + 1) * TT],
                            w2s[:, fc * D + hc * 128: fc * D + (hc + 1) * 128],
                            g16[gpar][:, fc * TT:(fc + 1) * TT],
                            start=first, stop=(fc == FC - 1))
                        first = False
                pv = k.inc(mm, "P", 1)
                g16_free_P[gpar] = pv
                if tt == NTT - 1:
                    bw_done_P = k.cnt["P"]
                k.wait(sc, "P", pv)
                for q in range(4):
                    k.wait(sc, f"Gt{q % 2}", tmp_free_G[q % 2])
                    for hh in range(2):
                        hc = q * 2 + hh
                        av = k.inc(sc.activation(
                            tmp32[q % 2][:, hh * TT:(hh + 1) * TT],
                            ps_main[:, hc * TT:(hc + 1) * TT],
                            Act.Copy, bias=0.0, scale=1.0), "A", 1)
                    if q == 3:
                        evac_last_A = av
                    k.wait(gp, "A", av)
                    k.wait(gp, "Gt0", r_acc_G[tt][0])
                    k.wait(gp, "Gt1", r_acc_G[tt][1])
                    for hh in range(2):
                        hc = q * 2 + hh
                        gv = k.inc(gp.dma_start(
                            out=g["r"][:, hc * NT + lo: hc * NT + lo + TT],
                            in_=tmp32[q % 2][:, hh * TT:(hh + 1) * TT],
                            accum_op=Alu.add), f"Gt{q % 2}", 16)
                    tmp_free_G[q % 2] = gv
                r_acc_G[tt] = [k.cnt["Gt0"], k.cnt["Gt1"]]
            if e + 1 < E:
                w1_ready = stream_w1(e + 1, aw_done_P)
                w2_ready = stream_w2(e + 1, bw_done_P)
    k.barrier()



def emit_attention(k, g, dr):
    nc = k.nc
    te, sc, ve, sy, gp = k.te, k.sc, k.ve, k.sy, k.gp
    k.barrier()
    with contextlib.ExitStack() as es0:
        def sb0(name, shape, dtype):
            return es0.enter_context(nc.sbuf_tensor(name, shape, dtype))

        rs = sb0("rs", [16, NT], F32)
        lnt16 = sb0("lnt16", [16, NT], F16)
        nrm16 = sb0("nrm16", [16, NT], F16)

        with contextlib.ExitStack() as es:
            def sb(name, shape, dtype):
                return es.enter_context(nc.sbuf_tensor(name, shape, dtype))
            psA = es.enter_context(nc.psum_tensor("l2_psA", [128, 512], F32))
            psB = es.enter_context(nc.psum_tensor("l2_psB", [128, 512], F32))
            psC = es.enter_context(nc.psum_tensor("l2_psC", [128, 512], F32))
            sq32 = sb("l2_sq32", [128, LNT], F32)
            sq32b = sb("l2_sq32b", [128, LNT], F32)
            rowA = sb("l2_rowA", [1, LNT], F16)
            rowB = sb("l2_rowB", [1, LNT], F16)
            rowC = sb("l2_rowC", [1, LNT], F16)
            ab16 = sb("l2_ab16", [128, 2 * LNT], F16)
            f32t = sb("l2_f32t", [128, LNT], F32)
            ut16 = [sb(f"l2_ut16_{i}", [128, LNT], F16) for i in range(2)]
            ut_free_G = [0, 0]
            for tt in range(NT // LNT):
                lo = tt * LNT
                abv = emit_ln_tile(
                    k, g,
                    lambda c, lo=lo: g["r"][:, c * NT + lo: c * NT + lo + LNT],
                    psA, psB, psC, psA, [sq32, sq32b], rowA, rowB, rowC, ab16)
                for c in range(DC):
                    par = c % 2
                    k.wait(ve, "V", abv)
                    k.wait(ve, f"Gu{par}", ut_free_G[par])
                    ve.tensor_tensor(f32t[:],
                                     g["r"][:, c * NT + lo: c * NT + lo + LNT],
                                     ab16[:, 0:LNT], Alu.mult)
                    vv = k.inc(ve.tensor_tensor(ut16[par][:], f32t[:],
                                                ab16[:, LNT:2 * LNT], Alu.add),
                               "V", 1)
                    k.wait(gp, "V", vv)
                    ut_free_G[par] = k.inc(gp.dma_start(
                        out=dr["usp"][c * 128:(c + 1) * 128, lo:lo + LNT],
                        in_=ut16[par][:]), f"Gu{par}", 16)
        u_sp_G = [k.cnt["Gu0"], k.cnt["Gu1"]]
        k.barrier()

        with contextlib.ExitStack() as es:
            def sb(name, shape, dtype):
                return es.enter_context(nc.sbuf_tensor(name, shape, dtype))

            ps_main = es.enter_context(nc.psum_tensor("at_main", [128, 2048], F32))
            psA = es.enter_context(nc.psum_tensor("at_A", [128, 512], F32))
            psB = es.enter_context(nc.psum_tensor("at_B", [128, 512], F32))
            psC = es.enter_context(nc.psum_tensor("at_C", [128, 512], F32))
            psD = es.enter_context(nc.psum_tensor("at_D", [128, 512], F32))

            for _nm in ("Lw", "Lu0", "Lu1"):
                k.dsem(_nm)
            wq16 = sb("wq16", [128, D], F16)
            wk16 = sb("wk16", [128, D], F16)
            wv16 = sb("wv16", [128, D], F16)
            wstage = sb("at_wstage", [128, D], F32)
            ustr = [sb(f"ustr_{i}", [128, IT], F16) for i in range(2)]
            q16 = [sb(f"q16_{i}", [128, NT], F16) for i in range(2)]
            k16 = [sb(f"k16_{i}", [128, NT], F16) for i in range(2)]
            vt16 = [sb(f"vt16_{i}", [128, NJT * 130], F16) for i in range(2)]
            s16 = [sb(f"s16_{i}", [128, IT], F16) for i in range(2)]
            slabY = sb("slabY", [128, NFREQ * IT], F16)
            slabN = sb("slabN", [128, NFREQ * IT], I16)
            slabS = sb("slabS", [128, NFREQ * IT], F16)
            diag16 = sb("diag16", [128, NFREQ * 128], F16)
            w16r = [sb(f"w16r_{i}", [128, IT], F16) for i in range(2)]
            cv16 = sb("cv16", [64, 4], F32)
            ah16 = [sb(f"ah16_{i}", [64, NT], F16) for i in range(2)]
            rssc = sb("rssc", [128, IT], F32)

            wconv_V = 0
            qk_scores_P = {0: 0, 1: 0}
            vt_av_P = {0: 0, 1: 0}
            ustr_free_P = [0, 0]
            w16r_free_P = [0, 0]
            ah_free_G = [0, 0]

            cv_read_V = {0: 0, 1: 0}
            rssc_free_G = 0
            k.dsem("Gr"), k.dsem("Ga0"), k.dsem("Ga1")

            for parx in range(2):
                for jt in range(NJT):
                    ve.memset(vt16[parx][:, jt * 130 + 64: jt * 130 + 65], 1.0)
                    mv = ve.memset(vt16[parx][:, jt * 130 + 129: jt * 130 + 130], 1.0)
            vt_ones_V = k.inc(mv, "V", 1)

            def stream_w(pair):
                nonlocal wconv_V
                par = pair % 2
                for (nm, dst) in (("wqT", wq16), ("wkT", wk16), ("wvT", wv16)):
                    k.wait(sy, "V", wconv_V)
                    lv = k.inc(sy.dma_start(
                        out=wstage[:].rearrange("p (c n) -> p c n", n=128),
                        in_=dr[nm][:, pair * 128:(pair + 1) * 128]
                            .rearrange("(c p) n -> p c n", p=128)), "Lw", 16)
                    k.wait(ve, "Lw", lv)
                    k.wait(ve, "P", qkv_used_P[0])
                    wconv_V = k.inc(ve.tensor_copy(dst[:], wstage[:]), "V", 1)

            qkv_used_P = [0]

            def emit_qkv(pair):
                par = pair % 2
                k.wait(te, "V")
                for (wsb, dst, brow) in ((wq16, q16[par], g["bqr"]),
                                         (wk16, k16[par], g["bkr"])):
                    k.wait(te, "P", qk_scores_P[par])
                    k.wait(te, "V", wconv_V)
                    for th in range(2):
                        tlo = th * IT
                        for c in range(DC):
                            upar = c % 2
                            k.wait(sy, "P", ustr_free_P[upar])
                            k.wait(sy, "Gu0", u_sp_G[0])
                            k.wait(sy, "Gu1", u_sp_G[1])
                            lv = k.inc(sy.dma_start(
                                out=ustr[upar][:],
                                in_=dr["usp"][c * 128:(c + 1) * 128,
                                              tlo:tlo + IT]), f"Lu{upar}", 16)
                            k.wait(te, f"Lu{upar}", lv)
                            for ih in range(2):
                                psx = psC if ih == 0 else psD
                                mm = te.matmul(
                                    psx[:, 0:512],
                                    wsb[:, c * 128:(c + 1) * 128],
                                    ustr[upar][:, ih * 512:(ih + 1) * 512],
                                    start=(c == 0), stop=(c == DC - 1))
                            ustr_free_P[upar] = k.inc(mm, "P", 1)
                        pv = k.cnt["P"]
                        k.wait(sc, "P", pv)
                        for ih in range(2):
                            psx = psC if ih == 0 else psD
                            av = k.inc(sc.activation(
                                dst[:, tlo + ih * 512: tlo + (ih + 1) * 512],
                                psx[:, 0:512], Act.Identity,
                                bias=brow[:, pair:pair + 1], scale=1.0), "A", 1)
                        k.wait(te, "A", av)
                k.wait(te, "P", vt_av_P[par])
                k.wait(te, "V", vt_ones_V)
                for jh in range(2):
                    jbase = jh * 8
                    for c in range(DC):
                        upar = c % 2
                        k.wait(sy, "P", ustr_free_P[upar])
                        lv = k.inc(sy.dma_start(
                            out=ustr[upar][:],
                            in_=dr["usp"][c * 128:(c + 1) * 128,
                                          jbase * 128: jbase * 128 + IT]),
                            f"Lu{upar}", 16)
                        k.wait(te, f"Lu{upar}", lv)
                        for jj in range(8):
                            psx = psC if jj < 4 else psD
                            mm = te.matmul(
                                psx[:, (jj % 4) * 128:(jj % 4) * 128 + 128],
                                ustr[upar][:, jj * 128:(jj + 1) * 128],
                                wv16[:, c * 128:(c + 1) * 128],
                                start=(c == 0 and jj % 4 == 0), stop=False)
                        ustr_free_P[upar] = k.inc(mm, "P", 1)
                    for jj in range(8):
                        psx = psC if jj < 4 else psD
                        mm = te.matmul(psx[:, (jj % 4) * 128:(jj % 4) * 128 + 128],
                                       g["ones_r16"][:],
                                       g["bv16"][0:1, pair * 128:(pair + 1) * 128],
                                       start=False, stop=(jj % 4 == 3))
                    pv = k.inc(mm, "P", 1)
                    k.wait(sc, "P", pv)
                    for jj in range(8):
                        jt = jbase + jj
                        psx = psC if jj < 4 else psD
                        co = (jj % 4) * 128
                        sc.activation(vt16[par][:, jt * 130: jt * 130 + 64],
                                      psx[:, co: co + 64], Act.Copy,
                                      bias=0.0, scale=1.0)
                        av = k.inc(sc.activation(
                            vt16[par][:, jt * 130 + 65: jt * 130 + 129],
                            psx[:, co + 64: co + 128], Act.Copy,
                            bias=0.0, scale=1.0), "A", 1)
                    k.wait(te, "A", av)
                vt_ready_A = k.cnt["A"]
                k.wait(te, "A", vt_ready_A)
                k.wait(te, "V", cv_read_V[par])
                for hh in range(2):
                    for jt in range(NJT):
                        mm = te.matmul(
                            psC[0:64, hh:hh + 1],
                            vt16[par][:, jt * 130 + hh * 65: jt * 130 + hh * 65 + 64],
                            g["ones_c16"][:],
                            start=(jt == 0), stop=(jt == NJT - 1))
                pv = k.inc(mm, "P", 1)
                k.wait(ve, "P", pv)
                k.inc(ve.tensor_scalar(cv16[:, 2 * par:2 * par + 2],
                                       psC[0:64, 0:2], 0.5, None, Alu.mult),
                      "V", 1)
                qkv_used_P[0] = k.cnt["P"]
                qk_ready_A[par] = k.cnt["A"]

            qk_ready_A = [0, 0]
            av_evac_V = 0
            s_evac_A = 0
            s_evac_of = [0, 0]
            s16_free_V = [0, 0]
            tanh_A = 0
            diag_used_P = 0
            stream_w(0)
            emit_qkv(0)
            for pair in range(NPAIR):
                par = pair % 2
                if pair + 1 < NPAIR:
                    stream_w(pair + 1)
                    emit_qkv(pair + 1)
                k.wait(te, "A", qk_ready_A[par])
                for hh in range(2):
                    h = pair * 2 + hh
                    k.wait(ve, "P", diag_used_P)
                    for fi in range(NFREQ):
                        vv = ve.tensor_scalar(
                            diag16[:, fi * 128:(fi + 1) * 128],
                            g["ident16"][:], al_ap(g, h, fi), None, Alu.mult)
                    diag_V = k.inc(vv, "V", 1)
                    for it in range(NIT):
                        ilo = it * IT
                        sin_A = [0] * NFREQ
                        diag_P = [0] * NFREQ
                        rr_V = [0] * NFREQ
                        for jt in range(NJT + 1):
                            if jt < NJT:
                                spar = jt % 2
                                k.wait(te, "A", s_evac_A)
                                for ih in range(2):
                                    mm = te.matmul(
                                        ps_main[:, ih * 512:(ih + 1) * 512],
                                        k16[par][64 * hh:64 * hh + 64,
                                                 jt * 128:(jt + 1) * 128],
                                        q16[par][64 * hh:64 * hh + 64,
                                                 ilo + ih * 512: ilo + (ih + 1) * 512],
                                        start=True, stop=True)
                                pv = k.inc(mm, "P", 1)
                                k.wait(sc, "P", pv)
                                k.wait(sc, "V", s16_free_V[spar])
                                s_evac_A = k.inc(sc.activation(
                                    s16[spar][:], ps_main[:, 0:IT], Act.Identity,
                                    bias=0.0, scale=1.0), "A", 1)
                                s_evac_of[spar] = s_evac_A
                            if jt >= 1:
                                j = jt - 1
                                spar = j % 2
                                k.wait(ve, "A", s_evac_of[spar])
                                for fp in range(3):
                                    nf = 2 if fp < 2 else 1
                                    for ff in range(nf):
                                        fi = fp * 2 + ff
                                        ys = slabY[:, fi * IT:(fi + 1) * IT]
                                        ns = slabN[:, fi * IT:(fi + 1) * IT]
                                        k.wait(ve, "A", sin_A[fi])
                                        ve.tensor_scalar(ns, s16[spar][:],
                                                         k1_ap(g, fi),
                                                         k2_ap(g, h, fi),
                                                         Alu.mult, Alu.add)
                                        ve.tensor_scalar(ys, s16[spar][:],
                                                         k1_ap(g, fi),
                                                         k2_ap(g, h, fi),
                                                         Alu.mult, Alu.add)
                                        ve.tensor_tensor(ys, ys, ns, Alu.subtract)
                                        ve.tensor_scalar(ns.bitcast(F16) if False
                                                         else ns, ys, 0.5, None,
                                                         Alu.is_ge)
                                        vv = ve.tensor_tensor(ys, ys, ns,
                                                              Alu.subtract)
                                        if fi == NFREQ - 1:
                                            s16_free_V[spar] = k.cnt["V"] + 1
                                        rr_V[fi] = k.inc(vv, "V", 1)
                                    f0 = fp * 2
                                    k.wait(sc, "V", rr_V[f0 + nf - 1])
                                    k.wait(sc, "P", diag_P[f0 + nf - 1])
                                    av = k.inc(sc.activation(
                                        slabS[:, f0 * IT:(f0 + nf) * IT],
                                        slabY[:, f0 * IT:(f0 + nf) * IT],
                                        Act.Sin, bias=0.0, scale=cst(g, 4)),
                                        "A", 1)
                                    for ff in range(nf):
                                        sin_A[f0 + ff] = av
                                    k.wait(te, "A", av)
                                    k.wait(te, "V", diag_V)
                                    if fp == 0:
                                        k.wait(te, "A", tanh_A)
                                    for ff in range(nf):
                                        fi = f0 + ff
                                        for ih in range(2):
                                            mm = te.matmul(
                                                ps_main[:, IT + ih * 512:
                                                        IT + (ih + 1) * 512],
                                                diag16[:, fi * 128:(fi + 1) * 128],
                                                slabS[:, fi * IT + ih * 512:
                                                      fi * IT + (ih + 1) * 512],
                                                start=(fi == 0), stop=(fi == NFREQ - 1))
                                        diag_P[fi] = k.inc(mm, "P", 1)
                                rslot = j % 2
                                k.wait(sc, "P", max(diag_P[NFREQ - 1],
                                                    w16r_free_P[rslot]))
                                tanh_A = k.inc(sc.activation(
                                    w16r[rslot][:], ps_main[:, IT:2 * IT],
                                    Act.Tanh, bias=0.0, scale=cst(g, 3)), "A", 1)
                                k.wait(te, "A", tanh_A)
                                if j == 0:
                                    k.wait(te, "V", av_evac_V)
                                for ih in range(2):
                                    psx = psA if ih == 0 else psB
                                    mm = te.matmul(
                                        psx[0:65, 0:512],
                                        vt16[par][:, j * 130 + hh * 65:
                                                  j * 130 + hh * 65 + 65],
                                        w16r[rslot][:, ih * 512:(ih + 1) * 512],
                                        start=(j == 0), stop=(j == NJT - 1))
                                w16r_free_P[rslot] = k.inc(mm, "P", 1)
                        av_done_P = k.cnt["P"]
                        apar = h % 2
                        k.wait(ve, "P", av_done_P)
                        if it == 0:
                            k.wait(ve, f"Ga{apar}", ah_free_G[apar])
                        k.wait(ve, "Gr", rssc_free_G)
                        for ih in range(2):
                            psx = psA if ih == 0 else psB
                            ve.tensor_scalar(
                                ah16[apar][:, ilo + ih * 512: ilo + (ih + 1) * 512],
                                psx[0:64, 0:512], 0.5,
                                cv16[:, 2 * par + hh:2 * par + hh + 1],
                                Alu.mult, Alu.add)
                            vv = ve.tensor_copy(
                                rssc[64:65, ih * 512:(ih + 1) * 512],
                                psx[64:65, 0:512])
                        avv = k.inc(vv, "V", 1)
                        av_evac_V = avv
                        cv_read_V[par] = avv
                        k.wait(gp, "V", avv)
                        rssc_free_G = k.inc(gp.dma_start(
                            out=rs[h:h + 1, ilo:ilo + IT],
                            in_=rssc[64:65, 0:IT]), "Gr", 16)
                        if it == NIT - 1:
                            ah_free_G[apar] = k.inc(gp.dma_start(
                                out=dr["asp"][h, :, :], in_=ah16[apar][:]),
                                f"Ga{apar}", 16)
                    diag_used_P = k.cnt["P"]
                vt_av_P[par] = k.cnt["P"]
                qk_scores_P[par] = k.cnt["P"]
            k.wait(sc, "V")
            k.wait(sc, "Gr")
            sc.activation(lnt16[:], rs[:], Act.Ln, bias=cst(g, 1)[0:16, :],
                          scale=cst(g, 3)[0:16, :])
            k.inc(sc.activation(nrm16[:], lnt16[:], Act.Exp, bias=0.0,
                                scale=cst(g, 5)[0:16, :]), "A", 1)
        nrm_A = k.cnt["A"]
        a_spill_G = [k.cnt.get("Ga0", 0), k.cnt.get("Ga1", 0)]
        k.barrier()

        with contextlib.ExitStack() as es:
            def sb(name, shape, dtype):
                return es.enter_context(nc.sbuf_tensor(name, shape, dtype))
            k.dsem("Lwo0"), k.dsem("Lwo1")
            psO = es.enter_context(nc.psum_tensor("wo_O", [128, 512], F32))
            psN = es.enter_context(nc.psum_tensor("wo_N", [128, 512], F32))
            psS1 = es.enter_context(nc.psum_tensor("wo_S1", [128, 512], F32))
            psS2 = es.enter_context(nc.psum_tensor("wo_S2", [128, 512], F32))
            wo16 = sb("wo16", [128, DC * D], F16)
            wstg = [sb(f"wo_stg_{i}", [128, D], F32) for i in range(2)]
            ahl = [sb(f"wo_ahl_{i}", [64, 512], F16) for i in range(2)]
            nbc16 = [sb(f"wo_nbc_{i}", [64, 512], F16) for i in range(2)]
            nrow16 = [sb(f"wo_nrow_{i}", [1, 512], F16) for i in range(2)]
            anrm = [sb(f"wo_anrm_{i}", [64, 512], F16) for i in range(2)]
            worhs = sb("wo_rhs", [128, DC * 512], F16)
            o_sb = sb("wo_osb", [128, DC * 512], F32)
            sq32 = sb("wo_sq32", [128, LNT], F32)
            sq32b = sb("wo_sq32b", [128, LNT], F32)
            rowA = sb("wo_rowA", [1, LNT], F16)
            rowB = sb("wo_rowB", [1, LNT], F16)
            rowC = sb("wo_rowC", [1, LNT], F16)
            ab16 = sb("wo_ab16", [128, 2 * LNT], F16)
            t32 = sb("wo_t32", [128, 512], F32)
            aout32 = [sb(f"wo_aout_{i}", [128, 512], F32) for i in range(2)]

            wo_conv_V = 0
            for c in range(DC):
                par = c % 2
                k.wait(sy, "V", wo_conv_V if c >= 2 else 0)
                lv = k.inc(sy.dma_start(out=wstg[par][:],
                                        in_=dr["woT"][c * 128:(c + 1) * 128, :]),
                           f"Lwo{par}", 16)
                k.wait(ve, f"Lwo{par}", lv)
                wo_conv_V = k.inc(ve.tensor_copy(wo16[:, c * D:(c + 1) * D],
                                                 wstg[par][:]), "V", 1)

            rel_free_V = [0, 0]
            nbc_free_V = [0, 0]
            nrow_mm_P = [0, 0]
            for _nm in ("Gl0", "Gl1", "Gn0", "Gn1", "Gm0", "Gm1", "Go0", "Go1"):
                k.dsem(_nm)
            anrm_free_G = [0, 0]
            worhs_ready_G = 0
            aout_free_G = [0, 0]
            wo_mm_P = 0
            osb_read_V = 0
            for it4 in range(4):
                tlo = it4 * 512
                k.wait(gp, "P", wo_mm_P)
                for c in range(DC):
                    for hh in range(2):
                        hx = 2 * c + hh
                        par = hx % 2
                        k.wait(gp, "Ga0", a_spill_G[0])
                        k.wait(gp, "Ga1", a_spill_G[1])
                        k.wait(gp, "V", rel_free_V[par])
                        lv = k.inc(gp.dma_start(
                            out=ahl[par][:],
                            in_=dr["asp"][hx, :, tlo:tlo + 512]), f"Gl{par}", 16)
                        k.wait(gp, "A", nrm_A)
                        k.wait(gp, "P", nrow_mm_P[par])
                        nlv = k.inc(gp.dma_start(
                            out=nrow16[par][:],
                            in_=nrm16[hx:hx + 1, tlo:tlo + 512]), f"Gn{par}", 16)
                        k.wait(te, f"Gn{par}", nlv)
                        k.wait(te, "V", nbc_free_V[par])
                        pv = k.inc(te.matmul(
                            psN[0:64, 0:512],
                            g["ones_r16"][0:1, 0:64],
                            nrow16[par][:],
                            start=True, stop=True), "P", 1)
                        nrow_mm_P[par] = pv
                        k.wait(ve, "P", pv)
                        nbv = k.inc(ve.tensor_copy(
                            nbc16[par][:], psN[0:64, 0:512]), "V", 1)
                        nbc_free_V[par] = nbv
                        k.wait(ve, f"Gl{par}", lv)
                        k.wait(ve, f"Gm{par}", anrm_free_G[par])
                        rel_free_V[par] = k.inc(ve.tensor_tensor(
                            anrm[par][:], ahl[par][:], nbc16[par][:], Alu.mult),
                            "V", 1)
                        k.wait(gp, "V", rel_free_V[par])
                        anrm_free_G[par] = k.inc(gp.dma_start(
                            out=worhs[hh * 64:(hh + 1) * 64,
                                      c * 512:(c + 1) * 512],
                            in_=anrm[par][:]), f"Gm{par}", 16)
                k.wait(te, "Gm0", k.cnt["Gm0"])
                k.wait(te, "Gm1", k.cnt["Gm1"])
                k.wait(te, "V", wo_conv_V)
                k.wait(sc, "V", osb_read_V)
                for hc in range(DC):
                    for c in range(DC):
                        mm = te.matmul(psO[:, 0:512],
                                       wo16[:, c * D + hc * 128: c * D + (hc + 1) * 128],
                                       worhs[:, c * 512:(c + 1) * 512],
                                       start=(c == 0), stop=(c == DC - 1))
                    pv = k.inc(mm, "P", 1)
                    k.wait(sc, "P", pv)
                    av = k.inc(sc.activation(o_sb[:, hc * 512:(hc + 1) * 512],
                                             psO[:, 0:512], Act.Identity,
                                             bias=g["bor"][:, hc:hc + 1],
                                             scale=1.0), "A", 1)
                    k.wait(te, "A", av)
                wo_mm_P = k.cnt["P"]
                abv = emit_ln_tile(
                    k, g, lambda c: o_sb[:, c * 512:(c + 1) * 512],
                    psS1, psS2, psS1, psS2, [sq32, sq32b], rowA, rowB, rowC, ab16)
                for c in range(DC):
                    par = c % 2
                    k.wait(ve, "V", abv)
                    k.wait(ve, f"Go{par}", aout_free_G[par])
                    ve.tensor_tensor(t32[:], o_sb[:, c * 512:(c + 1) * 512],
                                     ab16[:, 0:LNT], Alu.mult)
                    ve.tensor_tensor(t32[:], t32[:], ab16[:, LNT:2 * LNT], Alu.add)
                    vv = k.inc(ve.tensor_scalar(aout32[par][:], t32[:],
                                                g["gr"][:, c:c + 1],
                                                g["br"][:, c:c + 1],
                                                Alu.mult, Alu.add), "V", 1)
                    k.wait(gp, "V", vv)
                    aout_free_G[par] = k.inc(gp.dma_start(
                        out=g["r"][:, c * NT + tlo: c * NT + tlo + 512],
                        in_=aout32[par][:], accum_op=Alu.add), f"Go{par}", 16)
                osb_read_V = k.cnt["V"]
    k.barrier()



def emit_output(k, g, dr):
    k.barrier()
    gp = k.gp
    for c in range(DC):
        k.inc(gp.dma_start(out=dr["outT"][c * 128:(c + 1) * 128, :],
                           in_=g["r"][:, c * NT:(c + 1) * NT]), "G", 16)
    k.wait(gp, "G")


def build_program():
    nc = bass.Bass("TRN2", target_bir_lowering=False,
                   detect_race_conditions=DEBUG_SYNC)
    dr = {}
    dr["xT"] = nc.dram_tensor("xT", [D, NT], F32, kind="ExternalInput").ap()
    dr["wqT"] = nc.dram_tensor("wqT", [D, D], F32, kind="ExternalInput").ap()
    dr["wkT"] = nc.dram_tensor("wkT", [D, D], F32, kind="ExternalInput").ap()
    dr["wvT"] = nc.dram_tensor("wvT", [D, D], F32, kind="ExternalInput").ap()
    dr["woT"] = nc.dram_tensor("woT", [D, D], F32, kind="ExternalInput").ap()
    dr["w1T"] = nc.dram_tensor("w1T", [E, D, F], F32, kind="ExternalInput").ap()
    dr["w2T"] = nc.dram_tensor("w2T", [E, F, D], F32, kind="ExternalInput").ap()
    dr["rwT"] = nc.dram_tensor("rwT", [D, 8], F32, kind="ExternalInput").ap()
    dr["b1r"] = nc.dram_tensor("b1r", [128, 128], F32, kind="ExternalInput").ap()
    dr["b2"] = nc.dram_tensor("b2", [8, D], F32, kind="ExternalInput").ap()
    for nm in ("bqr", "bkr", "bor", "gr", "br"):
        dr[nm] = nc.dram_tensor(nm, [128, 8], F32, kind="ExternalInput").ap()
    dr["bv"] = nc.dram_tensor("bv", [1, D], F32, kind="ExternalInput").ap()
    dr["alphaf"] = nc.dram_tensor("alphaf", [1, 80], F32, kind="ExternalInput").ap()
    dr["phif"] = nc.dram_tensor("phif", [1, 80], F32, kind="ExternalInput").ap()
    dr["temp"] = nc.dram_tensor("temp", [1, 1], F32, kind="ExternalInput").ap()
    dr["outT"] = nc.dram_tensor("outT", [D, NT], F32, kind="ExternalOutput").ap()
    dr["usp"] = nc.dram_tensor("usp", [D, NT], F16, kind="Internal").ap()
    dr["asp"] = nc.dram_tensor("asp", [NH, 64, NT], F16, kind="Internal").ap()

    with contextlib.ExitStack() as es:
        k = Ctx(nc, es)
        g = {}
        emit_setup(k, dr, g)
        if DO_FFN1:
            emit_ffn(k, g, dr, "f1")
        if DO_ATTN:
            emit_attention(k, g, dr)
        if DO_FFN2:
            emit_ffn(k, g, dr, "f2")
        emit_output(k, g, dr)
    return nc


def host_prep(inputs):
    asc = np.ascontiguousarray
    w = {}
    w["wqT"] = asc(inputs["L0_attn_Wq"].T)
    w["wkT"] = asc(inputs["L0_attn_Wk"].T)
    w["wvT"] = asc(inputs["L0_attn_Wv"].T)
    w["woT"] = asc(inputs["L0_attn_Wo"].T)
    w["w1T"] = asc(np.transpose(inputs["L0_ffn_w1"], (0, 2, 1)))
    w["w2T"] = asc(np.transpose(inputs["L0_ffn_w2"], (0, 2, 1)))
    w["rwT"] = asc(inputs["L0_ffn_router_w"].T)
    w["b1r"] = asc(np.transpose(inputs["L0_ffn_b1"].reshape(8, 16, 128),
                                (2, 0, 1)).reshape(128, 128))
    w["b2"] = asc(inputs["L0_ffn_b2"])
    w["bqr"] = asc(inputs["L0_attn_bq"].reshape(8, 128).T)
    w["bkr"] = asc(inputs["L0_attn_bk"].reshape(8, 128).T)
    w["bor"] = asc(inputs["L0_attn_bo"].reshape(8, 128).T)
    w["gr"] = asc(inputs["L0_attn_gamma"].reshape(8, 128).T)
    w["br"] = asc(inputs["L0_attn_beta"].reshape(8, 128).T)
    w["bv"] = asc(inputs["L0_attn_bv"].reshape(1, D))
    w["alphaf"] = asc(inputs["L0_attn_alpha"].reshape(1, 80))
    w["phif"] = asc(inputs["L0_attn_phi"].reshape(1, 80))
    w["temp"] = asc(inputs["L0_attn_temperature"].reshape(1, 1))
    return w


def kernel(**inputs):
    x = np.asarray(inputs["L0_attn_x"], dtype=np.float32)
    inputs = {kk: np.asarray(v, dtype=np.float32) for kk, v in inputs.items()}
    w = host_prep(inputs)
    nc = build_program()
    in_maps = []
    for c in range(8):
        m = dict(w)
        m["xT"] = np.ascontiguousarray(x[c].T)
        in_maps.append(m)
    res = run_bass_kernel_spmd(nc, in_maps, core_ids=list(range(8)))
    out = np.stack([np.asarray(res.results[c]["outT"]).T for c in range(8)])
    return out.astype(np.float32)


# version 03
# speedup vs baseline: 1.4311x; 1.0093x over previous
import contextlib
import numpy as np

import concourse.bass as bass
import concourse.mybir as mybir
from concourse.bass_utils import run_bass_kernel_spmd

F32 = mybir.dt.float32
F16 = mybir.dt.float16
I16 = mybir.dt.int16
Alu = mybir.AluOpType
Act = mybir.ActivationFunctionType
AX = mybir.AxisListType.X

NT = 2048
D = 1024
DC = 8
F = 2048
FC = 16
E = 8
NH = 16
HD = 64
NPAIR = 8
TT = 256
NTT = NT // TT
LNT = 512
FREQS = [31.0, 37.0, 41.0, 43.0, 47.0]
NFREQ = 5
TWO_PI_SAFE = 6.283185
INV_2PI = 1.0 / (2.0 * np.pi)
IT = 1024
NIT = NT // IT
NJT = NT // 128

DO_FFN1 = True
DO_ATTN = True
DO_FFN2 = True
DEBUG_TAPS = False
DEBUG_SYNC = False


class _DrainAfter:
    def __init__(self, eng):
        self._eng = eng

    def __getattr__(self, name):
        attr = getattr(self._eng, name)
        if name in ("wait_ge", "drain", "dma_start", "sem_inc", "nop"):
            return attr
        if not callable(attr):
            return attr

        def wrapped(*a, **kw):
            r = attr(*a, **kw)
            try:
                self._eng.drain()
            except Exception:
                pass
            return r
        return wrapped


class Ctx:
    def __init__(self, nc, es):
        self.nc = nc
        self.es = es
        self.sy, self.ve, self.sc, self.te, self.gp = (
            nc.sync, nc.vector, nc.scalar, nc.tensor, nc.gpsimd)
        if DEBUG_SYNC:
            self.ve = _DrainAfter(nc.vector)
            self.sc = _DrainAfter(nc.scalar)
        self.engs = [self.sy, self.ve, self.sc, self.te, self.gp]
        self.sems = {}
        self.cnt = {}
        self.lastwait = {}
        for name in ("L", "G", "P", "A", "V", "Q"):
            self.sems[name] = es.enter_context(nc.semaphore("sem" + name))
            self.cnt[name] = 0

    def dsem(self, name):
        if name not in self.sems:
            self.sems[name] = self.es.enter_context(
                self.nc.semaphore("sem" + name))
            self.cnt[name] = 0
        return self.sems[name]

    def inc(self, instr, name, amt=1):
        if name not in self.sems:
            self.dsem(name)
        instr.then_inc(self.sems[name], amt)
        self.cnt[name] += amt
        return self.cnt[name]

    def wait(self, eng, name, val=None):
        if val is None:
            val = self.cnt[name]
        if val <= 0:
            return
        key = (id(eng), name)
        if self.lastwait.get(key, -1) >= val:
            return
        eng.wait_ge(self.sems[name], val)
        self.lastwait[key] = val

    def barrier(self):
        for eng in self.engs:
            for name in list(self.sems):
                self.wait(eng, name)

    def sb(self, name, shape, dtype):
        return self.es.enter_context(self.nc.sbuf_tensor(name, shape, dtype))



def emit_setup(k, dr, g):
    nc = k.nc
    g["r"] = k.sb("r", [128, DC * NT], F32)
    g["b1r"] = k.sb("s_b1r", [128, 128], F32)
    g["b2_16"] = k.sb("b2_16", [8, D], F16)
    for nm in ("bqr", "bkr", "bor", "gr", "br"):
        g[nm] = k.sb("s_" + nm, [128, 8], F32)
    g["bv16"] = k.sb("bv16", [1, D], F16)
    g["rw16"] = k.sb("rw16", [128, 64], F16)
    g["ident16"] = k.sb("ident16", [128, 128], F16)
    g["ones_c32"] = k.sb("ones_c32", [128, 1], F32)
    g["ones_c16"] = k.sb("ones_c16", [128, 1], F16)
    g["ones_r32"] = k.sb("ones_r32", [1, 128], F32)
    g["ones_r16"] = k.sb("ones_r16", [1, 128], F16)
    g["kvec"] = k.sb("kvec", [128, 176], F32)
    g["cst"] = k.sb("cst", [128, 8], F32)

    sy, ve, sc, te, gp = k.sy, k.ve, k.sc, k.te, k.gp

    with contextlib.ExitStack() as es:
        def sb(name, shape, dtype):
            return es.enter_context(nc.sbuf_tensor(name, shape, dtype))
        rows = sb("rows", [1, 96], F32)
        alphast = sb("alphast", [1, 80], F32)
        phist = sb("phist", [1, 80], F32)
        t11 = sb("t11", [1, 1], F32)
        iop = sb("iop", [128, 1], F32)
        iof = sb("iof", [128, 128], F32)
        b2f = sb("b2f_stage", [8, D], F32)
        bvf = sb("bv_stage", [1, D], F32)
        rwf = sb("rw_stage", [128, 64], F32)

        for c in range(DC):
            k.inc(sy.dma_start(out=g["r"][:, c * NT:(c + 1) * NT],
                               in_=dr["xT"][c * 128:(c + 1) * 128, :]), "L", 16)
        k.inc(sy.dma_start(out=g["b1r"][:], in_=dr["b1r"]), "L", 16)
        k.inc(sy.dma_start(out=b2f[:], in_=dr["b2"]), "L", 16)
        for nm in ("bqr", "bkr", "bor", "gr", "br"):
            k.inc(sy.dma_start(out=g[nm][:], in_=dr[nm]), "L", 16)
        k.inc(sy.dma_start(out=bvf[:], in_=dr["bv"]), "L", 16)
        k.inc(sy.dma_start(out=rwf[:].rearrange("p (c e) -> p c e", e=8),
                           in_=dr["rwT"].rearrange("(c p) e -> p c e", p=128)),
              "L", 16)
        k.inc(sy.dma_start(out=phist[:], in_=dr["phif"]), "L", 16)
        k.inc(sy.dma_start(out=alphast[:], in_=dr["alphaf"]), "L", 16)
        k.inc(sy.dma_start(out=t11[:], in_=dr["temp"]), "L", 16)
        ldone = k.cnt["L"]

        k.inc(gp.iota(iop[:], pattern=[[0, 1]], channel_multiplier=1,
                      allow_small_or_imprecise_dtypes=True), "Q", 1)
        k.inc(gp.iota(iof[:], pattern=[[1, 128]], channel_multiplier=0,
                      allow_small_or_imprecise_dtypes=True), "Q", 1)

        k.wait(ve, "L", ldone)
        k.wait(ve, "Q")
        ve.tensor_scalar(g["ident16"][:], iof[:], iop[:], None, Alu.is_equal)
        for i, v in enumerate([1e-5, 1024.0 + 1e-10, -0.5, 0.5, TWO_PI_SAFE, -1.0]):
            ve.memset(g["cst"][:, i:i + 1], v)
        ve.memset(g["ones_c32"][:], 1.0)
        ve.memset(g["ones_c16"][:], 1.0)
        ve.memset(g["ones_r32"][:], 1.0)
        ve.memset(g["ones_r16"][:], 1.0)
        ve.tensor_copy(g["b2_16"][:], b2f[:])
        ve.tensor_copy(g["bv16"][:], bvf[:])
        ve.tensor_copy(g["rw16"][:], rwf[:])
        k.wait(sc, "L", ldone)
        k.inc(sc.activation(rows[0:1, 7:8], t11[:], Act.Abs, bias=0.0, scale=1.0),
              "A", 1)
        k.wait(ve, "A")
        ve.tensor_scalar(rows[0:1, 6:7], rows[0:1, 7:8], 8.0, 0.8, Alu.mult, Alu.add)
        ve.drain()
        ve.reciprocal(rows[0:1, 5:6], rows[0:1, 6:7])
        for i, f in enumerate(FREQS):
            ve.memset(rows[0:1, i:i + 1], 1.0 / f)
        ve.drain()
        ve.tensor_scalar(rows[0:1, 8:13], rows[0:1, 0:5], rows[0:1, 5:6],
                         None, Alu.mult)
        k.inc(ve.tensor_scalar(rows[0:1, 16:96], phist[:], INV_2PI, 2.0,
                               Alu.mult, Alu.add), "V", 1)
        with nc.psum_tensor("ps_setup", [128, 512], F32) as psb:
            k.wait(te, "V")
            te.matmul(psb[:, 0:5], g["ones_r32"][:], rows[0:1, 8:13],
                      start=True, stop=True)
            te.matmul(psb[:, 16:96], g["ones_r32"][:], rows[0:1, 16:96],
                      start=True, stop=True)
            k.inc(te.matmul(psb[:, 96:176], g["ones_r32"][:], alphast[:],
                            start=True, stop=True), "P", 1)
            k.wait(ve, "P")
            ve.tensor_copy(g["kvec"][:, 0:5], psb[:, 0:5])
            ve.tensor_copy(g["kvec"][:, 16:96], psb[:, 16:96])
            k.inc(ve.tensor_copy(g["kvec"][:, 96:176], psb[:, 96:176]), "V", 1)
        k.barrier()


def cst(g, i):
    return g["cst"][:, i:i + 1]


def k1_ap(g, fi):
    return g["kvec"][:, fi:fi + 1]


def k2_ap(g, h, fi):
    return g["kvec"][:, 16 + h * 5 + fi:16 + h * 5 + fi + 1]


def al_ap(g, h, fi):
    return g["kvec"][:, 96 + h * 5 + fi:96 + h * 5 + fi + 1]



def emit_ln_tile(k, g, src_fn, psSUM, psSQ, psBC1, psBC2, sqp, rowA, rowB,
                 rowC, ab16):
    te, sc, ve = k.te, k.sc, k.ve
    k.wait(te, "V")
    sq_mm_P = [0, 0]
    for c in range(DC):
        par = c % 2
        k.wait(sc, "P", sq_mm_P[par])
        av = k.inc(sc.activation(sqp[par][:], src_fn(c), Act.Square,
                                 bias=0.0, scale=1.0), "A", 1)
        te.matmul(psSUM[0:1, 0:LNT], g["ones_c32"][:], src_fn(c),
                  start=(c == 0), stop=(c == DC - 1))
        k.wait(te, "A", av)
        mm = te.matmul(psSQ[0:1, 0:LNT], g["ones_c32"][:], sqp[par][:],
                       start=(c == 0), stop=(c == DC - 1))
        sq_mm_P[par] = k.inc(mm, "P", 1)
    pv = k.cnt["P"]
    k.wait(ve, "P", pv)
    ve.tensor_scalar(rowA[:], psSUM[0:1, 0:LNT], 1.0 / D, None, Alu.mult)
    ve.tensor_scalar(psSUM[0:1, 0:LNT], psSQ[0:1, 0:LNT], 1.0 / D, None, Alu.mult)
    ve.tensor_tensor(rowB[:], rowA[:], rowA[:], Alu.mult)
    k.inc(ve.tensor_tensor(psSQ[0:1, 0:LNT], psSUM[0:1, 0:LNT], rowB[:],
                           Alu.subtract), "V", 1)
    k.wait(sc, "V")
    sc.activation(psSUM[0:1, 0:LNT], psSQ[0:1, 0:LNT], Act.Ln,
                  bias=cst(g, 0)[0:1, :], scale=1.0)
    k.inc(sc.activation(rowC[:], psSUM[0:1, 0:LNT], Act.Exp, bias=0.0,
                        scale=cst(g, 2)[0:1, :]), "A", 1)
    k.wait(ve, "A")
    ve.tensor_tensor(rowB[:], rowA[:], rowC[:], Alu.mult)
    k.inc(ve.tensor_scalar(rowA[:], rowB[:], -1.0, None, Alu.mult), "V", 1)
    k.wait(te, "V")
    te.matmul(psBC1[:, 0:LNT], g["ones_r16"][:], rowC[:], start=True, stop=True)
    pv = k.inc(te.matmul(psBC2[:, 0:LNT], g["ones_r16"][:], rowA[:],
                         start=True, stop=True), "P", 1)
    k.wait(ve, "P", pv)
    ve.tensor_copy(ab16[:, 0:LNT], psBC1[:, 0:LNT])
    return k.inc(ve.tensor_copy(ab16[:, LNT:2 * LNT], psBC2[:, 0:LNT]), "V", 1)



def emit_ffn(k, g, dr, tag):
    nc = k.nc
    te, sc, ve, sy, gp = k.te, k.sc, k.ve, k.sy, k.gp
    k.barrier()
    with contextlib.ExitStack() as es:
        def sb(name, shape, dtype):
            return es.enter_context(nc.sbuf_tensor(f"{name}_{tag}", shape, dtype))

        def psf(name, shape, dtype=F32):
            return es.enter_context(nc.psum_tensor(f"{name}_{tag}", shape, dtype))

        k.dsem("Lst0"), k.dsem("Lst1"), k.dsem("Gc0"), k.dsem("Gc1")
        u16 = sb("u16", [128, DC * NT], F16)
        w1s = sb("w1s", [128, DC * F], F16)
        w2s = sb("w2s", [128, FC * D], F16)
        stage = sb("wstage", [128, 2048], F32)
        g16 = [sb(f"g16_{i}", [128, FC * TT], F16) for i in range(2)]
        graw = sb("graw", [128, 512], F16)
        tmp32 = [sb(f"tmp32_{i}", [128, 512], F32) for i in range(2)]
        rowA = sb("rowA", [1, LNT], F16)
        rowB = sb("rowB", [1, LNT], F16)
        rowC = sb("rowC", [1, LNT], F16)
        ab16 = sb("ab16", [128, 2 * LNT], F16)
        t16 = graw
        cem16 = sb("cem16", [8, NT], F16)
        lgt = sb("lgt", [128, 8], F32)
        msk1 = sb("msk1", [128, 8], F32)
        msk2 = sb("msk2", [128, 8], F32)
        xmk = sb("xmk", [128, 8], F32)
        mrow = sb("mrow", [128, 8], F32)
        ctm16 = sb("ctm16", [128, 8], F16)
        cbc16 = [sb(f"cbc16_{i}", [128, TT], F16) for i in range(2)]
        cemrow = [sb(f"cemrow_{i}", [1, TT], F16) for i in range(2)]

        ps_main = psf("ps_main", [128, 2048])
        psA = psf("psA", [128, 512])
        psB = psf("psB", [128, 512])
        psC = psf("psC", [128, 512])

        for tt in range(NT // LNT):
            lo = tt * LNT
            abv = emit_ln_tile(
                k, g,
                lambda c, lo=lo: g["r"][:, c * NT + lo: c * NT + lo + LNT],
                psA, psB, psC, psA, [tmp32[0], tmp32[1]], rowA, rowB, rowC, ab16)
            for c in range(DC):
                k.wait(ve, "V", abv)
                ve.tensor_tensor(t16[:], g["r"][:, c * NT + lo: c * NT + lo + LNT],
                                 ab16[:, 0:LNT], Alu.mult)
                vv = ve.tensor_tensor(u16[:, c * NT + lo: c * NT + lo + LNT],
                                      t16[:], ab16[:, LNT:2 * LNT], Alu.add)
            k.inc(vv, "V", 1)
        u_ready_V = k.cnt["V"]

        with nc.psum_tensor(f"psT16_{tag}", [128, 128], F16) as psT16:
            for tj in range(NT // 128):
                jlo = tj * 128
                k.wait(te, "V", u_ready_V)
                for c in range(DC):
                    mm = te.matmul(psA[:, 0:8],
                                   u16[:, c * NT + jlo: c * NT + jlo + 128],
                                   g["rw16"][:, 8 * c:8 * c + 8],
                                   start=(c == 0), stop=(c == DC - 1))
                pv = k.inc(mm, "P", 1)
                k.wait(ve, "P", pv)
                k.inc(ve.tensor_copy(lgt[:], psA[:, 0:8]), "V", 1)
                ve.drain()
                ve.reduce_max(mrow[:, 0:1], lgt[:], axis=AX)
                ve.drain()
                ve.tensor_scalar(msk1[:], lgt[:], mrow[:, 0:1], None, Alu.is_equal)
                ve.drain()
                ve.scalar_tensor_tensor(xmk[:], msk1[:], -1e30, lgt[:],
                                        Alu.mult, Alu.add)
                ve.drain()
                ve.reduce_max(mrow[:, 1:2], xmk[:], axis=AX)
                ve.drain()
                ve.tensor_scalar(msk2[:], xmk[:], mrow[:, 1:2], None, Alu.is_equal)
                ve.drain()
                k.inc(ve.tensor_tensor(mrow[:, 2:3], mrow[:, 0:1], mrow[:, 1:2],
                                       Alu.subtract), "V", 1)
                k.wait(sc, "V")
                k.inc(sc.activation(mrow[:, 3:4], mrow[:, 2:3], Act.Tanh,
                                    bias=0.0, scale=cst(g, 3)), "A", 1)
                k.wait(ve, "A")
                ve.tensor_scalar(mrow[:, 4:5], mrow[:, 3:4], 0.25, 0.25,
                                 Alu.mult, Alu.add)
                ve.tensor_scalar(mrow[:, 5:6], mrow[:, 3:4], -0.25, 0.25,
                                 Alu.mult, Alu.add)
                ve.drain()
                ve.tensor_scalar(xmk[:], msk1[:], mrow[:, 4:5], None, Alu.mult)
                ve.drain()
                k.inc(ve.scalar_tensor_tensor(ctm16[:], msk2[:], mrow[:, 5:6],
                                              xmk[:], Alu.mult, Alu.add), "V", 1)
                k.wait(te, "V")
                pv = k.inc(te.transpose(psT16[0:8, 0:128], ctm16[:],
                                        g["ident16"][:]), "P", 1)
                k.wait(ve, "P", pv)
                k.inc(ve.tensor_copy(cem16[0:8, jlo:jlo + 128],
                                     psT16[0:8, 0:128]), "V", 1)
        cem_ready_V = k.cnt["V"]

        if DEBUG_TAPS:
            dbg_u = k.nc.dram_tensor(f"dbg_u16_{tag}", [128, DC * NT], F16,
                                     kind="ExternalOutput").ap()
            dbg_c = k.nc.dram_tensor(f"dbg_cem_{tag}", [8, NT], F16,
                                     kind="ExternalOutput").ap()
            k.wait(gp, "V", cem_ready_V)
            k.inc(gp.dma_start(out=dbg_u, in_=u16[:]), "G", 16)
            k.inc(gp.dma_start(out=dbg_c, in_=cem16[:]), "G", 16)

        stage_free = [0, 0]

        def stream_w1(e, gate_P):
            for hc in range(DC):
                for half in range(2):
                    par = (hc * 2 + half) % 2
                    k.wait(sy, "V", stage_free[par])
                    lv = k.inc(sy.dma_start(
                        out=stage[:, par * 1024:(par + 1) * 1024],
                        in_=dr["w1T"][e, hc * 128:(hc + 1) * 128,
                                      half * 1024:(half + 1) * 1024]),
                        f"Lst{par}", 16)
                    k.wait(ve, f"Lst{par}", lv)
                    k.wait(ve, "P", gate_P)
                    stage_free[par] = k.inc(ve.tensor_copy(
                        w1s[:, hc * F + half * 1024: hc * F + (half + 1) * 1024],
                        stage[:, par * 1024:(par + 1) * 1024]), "V", 1)
            return k.cnt["V"]

        def stream_w2(e, gate_P):
            for fc in range(FC):
                par = fc % 2
                k.wait(sy, "V", stage_free[par])
                lv = k.inc(sy.dma_start(
                    out=stage[:, par * 1024:(par + 1) * 1024],
                    in_=dr["w2T"][e, fc * 128:(fc + 1) * 128, :]),
                    f"Lst{par}", 16)
                k.wait(ve, f"Lst{par}", lv)
                k.wait(ve, "P", gate_P)
                stage_free[par] = k.inc(ve.tensor_copy(
                    w2s[:, fc * D: fc * D + 1024],
                    stage[:, par * 1024:(par + 1) * 1024]), "V", 1)
            return k.cnt["V"]

        w1_ready = stream_w1(0, 0)
        w2_ready = stream_w2(0, 0)
        aw_done_P = 0
        bw_done_P = 0
        gelu_free_A = {0: 0, 1: 0}
        graw_free_V = 0
        cbc_used_V = [0, 0]
        g16_free_P = {0: 0, 1: 0}
        evac_last_A = 0
        k.dsem("Gt0"), k.dsem("Gt1")
        r_acc_G = [[0, 0] for _ in range(NTT)]
        tmp_free_G = [0, 0]

        cemrow_G = [0, 0]
        cemrow_mm_P = [0, 0]

        def issue_cemrow(e2, tt2):
            par2 = tt2 % 2
            k.wait(gp, "V", cem_ready_V)
            k.wait(gp, "P", cemrow_mm_P[par2])
            cemrow_G[par2] = k.inc(gp.dma_start(
                out=cemrow[par2][:],
                in_=cem16[e2:e2 + 1, tt2 * TT:(tt2 + 1) * TT]), f"Gc{par2}", 16)

        issue_cemrow(0, 0)
        cbc_V = [0, 0]

        def build_cbc(e2, tt2):
            par2 = tt2 % 2
            k.wait(te, "V", max(cem_ready_V, cbc_used_V[par2]))
            k.wait(te, f"Gc{par2}", cemrow_G[par2])
            pv = k.inc(te.matmul(psC[:, 0:TT], g["ones_r16"][:],
                                 cemrow[par2][:],
                                 start=True, stop=True), "P", 1)
            cemrow_mm_P[par2] = pv
            k.wait(ve, "P", pv)
            cbc_V[par2] = k.inc(ve.tensor_copy(cbc16[par2][:], psC[:, 0:TT]),
                                "V", 1)
            if tt2 + 1 < NTT:
                issue_cemrow(e2, tt2 + 1)
            elif e2 + 1 < E:
                issue_cemrow(e2 + 1, 0)

        build_cbc(0, 0)
        for e in range(E):
            for tt in range(NTT):
                gpar = tt % 2
                lo = tt * TT
                cbv = cbc_V[gpar]
                k.wait(te, "V", w1_ready)
                for fcp in range(FC // 2):
                    par = fcp % 2
                    psg = psA if par == 0 else psB
                    k.wait(te, "A", gelu_free_A[par])
                    k.wait(sc, "V", graw_free_V)
                    for sub in range(2):
                        fc = fcp * 2 + sub
                        for hc in range(DC):
                            mm = te.matmul(
                                psg[:, sub * TT:(sub + 1) * TT],
                                w1s[:, hc * F + fc * 128: hc * F + (fc + 1) * 128],
                                u16[:, hc * NT + lo: hc * NT + lo + TT],
                                start=(hc == 0), stop=(hc == DC - 1))
                    pv = k.inc(mm, "P", 1)
                    k.wait(sc, "P", max(pv, g16_free_P[gpar]))
                    for sub in range(2):
                        fc = fcp * 2 + sub
                        av = k.inc(sc.activation(
                            graw[:, sub * TT:(sub + 1) * TT],
                            psg[:, sub * TT:(sub + 1) * TT], Act.Gelu,
                            bias=g["b1r"][:, e * 16 + fc:e * 16 + fc + 1],
                            scale=1.0), "A", 1)
                    gelu_free_A[par] = av
                    k.wait(ve, "A", av)
                    k.wait(ve, "V", cbv)
                    k.wait(ve, "P", g16_free_P[gpar])
                    for sub in range(2):
                        fc = fcp * 2 + sub
                        vv = ve.tensor_tensor(
                            g16[gpar][:, fc * TT:(fc + 1) * TT],
                            graw[:, sub * TT:(sub + 1) * TT],
                            cbc16[gpar][:], Alu.mult)
                    graw_free_V = k.inc(vv, "V", 1)
                g_ready_V = k.cnt["V"]
                cbc_used_V[gpar] = g_ready_V
                if tt + 1 < NTT:
                    build_cbc(e, tt + 1)
                elif e + 1 < E:
                    build_cbc(e + 1, 0)
                if tt == NTT - 1:
                    aw_done_P = k.cnt["P"]
                k.wait(te, "V", max(w2_ready, g_ready_V))
                k.wait(te, "A", evac_last_A)
                for hc in range(DC):
                    first = True
                    if e == 0:
                        te.matmul(ps_main[:, hc * TT:(hc + 1) * TT],
                                  g["b2_16"][:, hc * 128:(hc + 1) * 128],
                                  cem16[:, lo:lo + TT], start=True, stop=False)
                        first = False
                    for fc in range(FC):
                        mm = te.matmul(
                            ps_main[:, hc * TT:(hc + 1) * TT],
                            w2s[:, fc * D + hc * 128: fc * D + (hc + 1) * 128],
                            g16[gpar][:, fc * TT:(fc + 1) * TT],
                            start=first, stop=(fc == FC - 1))
                        first = False
                pv = k.inc(mm, "P", 1)
                g16_free_P[gpar] = pv
                if tt == NTT - 1:
                    bw_done_P = k.cnt["P"]
                k.wait(sc, "P", pv)
                for q in range(4):
                    k.wait(sc, f"Gt{q % 2}", tmp_free_G[q % 2])
                    for hh in range(2):
                        hc = q * 2 + hh
                        av = k.inc(sc.activation(
                            tmp32[q % 2][:, hh * TT:(hh + 1) * TT],
                            ps_main[:, hc * TT:(hc + 1) * TT],
                            Act.Copy, bias=0.0, scale=1.0), "A", 1)
                    if q == 3:
                        evac_last_A = av
                    k.wait(gp, "A", av)
                    k.wait(gp, "Gt0", r_acc_G[tt][0])
                    k.wait(gp, "Gt1", r_acc_G[tt][1])
                    for hh in range(2):
                        hc = q * 2 + hh
                        gv = k.inc(gp.dma_start(
                            out=g["r"][:, hc * NT + lo: hc * NT + lo + TT],
                            in_=tmp32[q % 2][:, hh * TT:(hh + 1) * TT],
                            accum_op=Alu.add), f"Gt{q % 2}", 16)
                    tmp_free_G[q % 2] = gv
                r_acc_G[tt] = [k.cnt["Gt0"], k.cnt["Gt1"]]
            if e + 1 < E:
                w1_ready = stream_w1(e + 1, aw_done_P)
                w2_ready = stream_w2(e + 1, bw_done_P)
    k.barrier()



def emit_attention(k, g, dr):
    nc = k.nc
    te, sc, ve, sy, gp = k.te, k.sc, k.ve, k.sy, k.gp
    k.barrier()
    with contextlib.ExitStack() as es0:
        def sb0(name, shape, dtype):
            return es0.enter_context(nc.sbuf_tensor(name, shape, dtype))

        rs = sb0("rs", [16, NT], F32)
        lnt16 = sb0("lnt16", [16, NT], F16)
        nrm16 = sb0("nrm16", [16, NT], F16)

        with contextlib.ExitStack() as es:
            def sb(name, shape, dtype):
                return es.enter_context(nc.sbuf_tensor(name, shape, dtype))
            psA = es.enter_context(nc.psum_tensor("l2_psA", [128, 512], F32))
            psB = es.enter_context(nc.psum_tensor("l2_psB", [128, 512], F32))
            psC = es.enter_context(nc.psum_tensor("l2_psC", [128, 512], F32))
            sq32 = sb("l2_sq32", [128, LNT], F32)
            sq32b = sb("l2_sq32b", [128, LNT], F32)
            rowA = sb("l2_rowA", [1, LNT], F16)
            rowB = sb("l2_rowB", [1, LNT], F16)
            rowC = sb("l2_rowC", [1, LNT], F16)
            ab16 = sb("l2_ab16", [128, 2 * LNT], F16)
            f32t = sb("l2_f32t", [128, LNT], F32)
            ut16 = [sb(f"l2_ut16_{i}", [128, LNT], F16) for i in range(2)]
            ut_free_G = [0, 0]
            for tt in range(NT // LNT):
                lo = tt * LNT
                abv = emit_ln_tile(
                    k, g,
                    lambda c, lo=lo: g["r"][:, c * NT + lo: c * NT + lo + LNT],
                    psA, psB, psC, psA, [sq32, sq32b], rowA, rowB, rowC, ab16)
                for c in range(DC):
                    par = c % 2
                    k.wait(ve, "V", abv)
                    k.wait(ve, f"Gu{par}", ut_free_G[par])
                    ve.tensor_tensor(f32t[:],
                                     g["r"][:, c * NT + lo: c * NT + lo + LNT],
                                     ab16[:, 0:LNT], Alu.mult)
                    vv = k.inc(ve.tensor_tensor(ut16[par][:], f32t[:],
                                                ab16[:, LNT:2 * LNT], Alu.add),
                               "V", 1)
                    k.wait(gp, "V", vv)
                    ut_free_G[par] = k.inc(gp.dma_start(
                        out=dr["usp"][c * 128:(c + 1) * 128, lo:lo + LNT],
                        in_=ut16[par][:]), f"Gu{par}", 16)
        u_sp_G = [k.cnt["Gu0"], k.cnt["Gu1"]]
        k.barrier()

        with contextlib.ExitStack() as es:
            def sb(name, shape, dtype):
                return es.enter_context(nc.sbuf_tensor(name, shape, dtype))

            ps_main = es.enter_context(nc.psum_tensor("at_main", [128, 2048], F32))
            psA = es.enter_context(nc.psum_tensor("at_A", [128, 512], F32))
            psB = es.enter_context(nc.psum_tensor("at_B", [128, 512], F32))
            psC = es.enter_context(nc.psum_tensor("at_C", [128, 512], F32))
            psD = es.enter_context(nc.psum_tensor("at_D", [128, 512], F32))

            for _nm in ("Lw", "Lu0", "Lu1"):
                k.dsem(_nm)
            wq16 = sb("wq16", [128, D], F16)
            wk16 = sb("wk16", [128, D], F16)
            wv16 = sb("wv16", [128, D], F16)
            wstage = sb("at_wstage", [128, D], F32)
            ustr = [sb(f"ustr_{i}", [128, IT], F16) for i in range(2)]
            q16 = [sb(f"q16_{i}", [128, NT], F16) for i in range(2)]
            k16 = [sb(f"k16_{i}", [128, NT], F16) for i in range(2)]
            vt16 = [sb(f"vt16_{i}", [128, NJT * 130], F16) for i in range(2)]
            s16 = [sb(f"s16_{i}", [128, IT], F16) for i in range(2)]
            slabY = sb("slabY", [128, NFREQ * IT], F16)
            slabN = sb("slabN", [128, NFREQ * IT], I16)
            slabS = sb("slabS", [128, NFREQ * IT], F16)
            diag16 = sb("diag16", [128, NFREQ * 128], F16)
            w16r = [sb(f"w16r_{i}", [128, IT], F16) for i in range(2)]
            cv16 = sb("cv16", [64, 4], F32)
            ah16 = [sb(f"ah16_{i}", [64, NT], F16) for i in range(2)]
            rssc = sb("rssc", [128, IT], F32)

            wconv_V = 0
            qk_scores_P = {0: 0, 1: 0}
            vt_av_P = {0: 0, 1: 0}
            ustr_free_P = [0, 0]
            w16r_free_P = [0, 0]
            ah_free_G = [0, 0]

            cv_read_V = {0: 0, 1: 0}
            rssc_free_G = 0
            k.dsem("Gr"), k.dsem("Ga0"), k.dsem("Ga1")

            for parx in range(2):
                for jt in range(NJT):
                    ve.memset(vt16[parx][:, jt * 130 + 64: jt * 130 + 65], 1.0)
                    mv = ve.memset(vt16[parx][:, jt * 130 + 129: jt * 130 + 130], 1.0)
            vt_ones_V = k.inc(mv, "V", 1)

            def stream_w(pair):
                nonlocal wconv_V
                par = pair % 2
                for (nm, dst) in (("wqT", wq16), ("wkT", wk16), ("wvT", wv16)):
                    k.wait(sy, "V", wconv_V)
                    lv = k.inc(sy.dma_start(
                        out=wstage[:].rearrange("p (c n) -> p c n", n=128),
                        in_=dr[nm][:, pair * 128:(pair + 1) * 128]
                            .rearrange("(c p) n -> p c n", p=128)), "Lw", 16)
                    k.wait(ve, "Lw", lv)
                    k.wait(ve, "P", qkv_used_P[0])
                    wconv_V = k.inc(ve.tensor_copy(dst[:], wstage[:]), "V", 1)

            qkv_used_P = [0]

            def emit_qkv(pair):
                par = pair % 2
                k.wait(te, "V")
                for (wsb, dst, brow) in ((wq16, q16[par], g["bqr"]),
                                         (wk16, k16[par], g["bkr"])):
                    k.wait(te, "P", qk_scores_P[par])
                    k.wait(te, "V", wconv_V)
                    for th in range(2):
                        tlo = th * IT
                        for c in range(DC):
                            upar = c % 2
                            k.wait(sy, "P", ustr_free_P[upar])
                            k.wait(sy, "Gu0", u_sp_G[0])
                            k.wait(sy, "Gu1", u_sp_G[1])
                            lv = k.inc(sy.dma_start(
                                out=ustr[upar][:],
                                in_=dr["usp"][c * 128:(c + 1) * 128,
                                              tlo:tlo + IT]), f"Lu{upar}", 16)
                            k.wait(te, f"Lu{upar}", lv)
                            for ih in range(2):
                                psx = psC if ih == 0 else psD
                                mm = te.matmul(
                                    psx[:, 0:512],
                                    wsb[:, c * 128:(c + 1) * 128],
                                    ustr[upar][:, ih * 512:(ih + 1) * 512],
                                    start=(c == 0), stop=(c == DC - 1))
                            ustr_free_P[upar] = k.inc(mm, "P", 1)
                        pv = k.cnt["P"]
                        k.wait(sc, "P", pv)
                        for ih in range(2):
                            psx = psC if ih == 0 else psD
                            av = k.inc(sc.activation(
                                dst[:, tlo + ih * 512: tlo + (ih + 1) * 512],
                                psx[:, 0:512], Act.Identity,
                                bias=brow[:, pair:pair + 1], scale=1.0), "A", 1)
                        k.wait(te, "A", av)
                k.wait(te, "P", vt_av_P[par])
                k.wait(te, "V", vt_ones_V)
                for jh in range(2):
                    jbase = jh * 8
                    for c in range(DC):
                        upar = c % 2
                        k.wait(sy, "P", ustr_free_P[upar])
                        lv = k.inc(sy.dma_start(
                            out=ustr[upar][:],
                            in_=dr["usp"][c * 128:(c + 1) * 128,
                                          jbase * 128: jbase * 128 + IT]),
                            f"Lu{upar}", 16)
                        k.wait(te, f"Lu{upar}", lv)
                        for jj in range(8):
                            psx = psC if jj < 4 else psD
                            mm = te.matmul(
                                psx[:, (jj % 4) * 128:(jj % 4) * 128 + 128],
                                ustr[upar][:, jj * 128:(jj + 1) * 128],
                                wv16[:, c * 128:(c + 1) * 128],
                                start=(c == 0 and jj % 4 == 0), stop=False)
                        ustr_free_P[upar] = k.inc(mm, "P", 1)
                    for jj in range(8):
                        psx = psC if jj < 4 else psD
                        mm = te.matmul(psx[:, (jj % 4) * 128:(jj % 4) * 128 + 128],
                                       g["ones_r16"][:],
                                       g["bv16"][0:1, pair * 128:(pair + 1) * 128],
                                       start=False, stop=(jj % 4 == 3))
                    pv = k.inc(mm, "P", 1)
                    k.wait(sc, "P", pv)
                    for jj in range(8):
                        jt = jbase + jj
                        psx = psC if jj < 4 else psD
                        co = (jj % 4) * 128
                        sc.activation(vt16[par][:, jt * 130: jt * 130 + 64],
                                      psx[:, co: co + 64], Act.Copy,
                                      bias=0.0, scale=1.0)
                        av = k.inc(sc.activation(
                            vt16[par][:, jt * 130 + 65: jt * 130 + 129],
                            psx[:, co + 64: co + 128], Act.Copy,
                            bias=0.0, scale=1.0), "A", 1)
                    k.wait(te, "A", av)
                vt_ready_A = k.cnt["A"]
                k.wait(te, "A", vt_ready_A)
                k.wait(te, "V", cv_read_V[par])
                for hh in range(2):
                    for jt in range(NJT):
                        mm = te.matmul(
                            psC[0:64, hh:hh + 1],
                            vt16[par][:, jt * 130 + hh * 65: jt * 130 + hh * 65 + 64],
                            g["ones_c16"][:],
                            start=(jt == 0), stop=(jt == NJT - 1))
                pv = k.inc(mm, "P", 1)
                k.wait(ve, "P", pv)
                k.inc(ve.tensor_scalar(cv16[:, 2 * par:2 * par + 2],
                                       psC[0:64, 0:2], 0.5, None, Alu.mult),
                      "V", 1)
                qkv_used_P[0] = k.cnt["P"]
                qk_ready_A[par] = k.cnt["A"]

            qk_ready_A = [0, 0]
            av_evac_V = 0
            s_evac_A = 0
            s_evac_of = [0, 0]
            s16_free_V = [0, 0]
            tanh_A = 0
            diag_used_P = 0
            stream_w(0)
            emit_qkv(0)
            for pair in range(NPAIR):
                par = pair % 2
                if pair + 1 < NPAIR:
                    stream_w(pair + 1)
                    emit_qkv(pair + 1)
                k.wait(te, "A", qk_ready_A[par])
                for hh in range(2):
                    h = pair * 2 + hh
                    k.wait(ve, "P", diag_used_P)
                    for fi in range(NFREQ):
                        vv = ve.tensor_scalar(
                            diag16[:, fi * 128:(fi + 1) * 128],
                            g["ident16"][:], al_ap(g, h, fi), None, Alu.mult)
                    diag_V = k.inc(vv, "V", 1)
                    for it in range(NIT):
                        ilo = it * IT
                        sin_A = [0] * NFREQ
                        diag_P = [0] * NFREQ
                        rr_V = [0] * NFREQ
                        for jt in range(NJT + 1):
                            if jt < NJT:
                                spar = jt % 2
                                k.wait(te, "A", s_evac_A)
                                for ih in range(2):
                                    mm = te.matmul(
                                        ps_main[:, ih * 512:(ih + 1) * 512],
                                        k16[par][64 * hh:64 * hh + 64,
                                                 jt * 128:(jt + 1) * 128],
                                        q16[par][64 * hh:64 * hh + 64,
                                                 ilo + ih * 512: ilo + (ih + 1) * 512],
                                        start=True, stop=True)
                                pv = k.inc(mm, "P", 1)
                                k.wait(sc, "P", pv)
                                k.wait(sc, "V", s16_free_V[spar])
                                s_evac_A = k.inc(sc.activation(
                                    s16[spar][:], ps_main[:, 0:IT], Act.Identity,
                                    bias=0.0, scale=1.0), "A", 1)
                                s_evac_of[spar] = s_evac_A
                            if jt >= 1:
                                j = jt - 1
                                spar = j % 2
                                k.wait(ve, "A", s_evac_of[spar])
                                for fp in range(3):
                                    nf = 2 if fp < 2 else 1
                                    for ff in range(nf):
                                        fi = fp * 2 + ff
                                        ys = slabY[:, fi * IT:(fi + 1) * IT]
                                        ns = slabN[:, fi * IT:(fi + 1) * IT]
                                        k.wait(ve, "A", sin_A[fi])
                                        ve.tensor_scalar(ns, s16[spar][:],
                                                         k1_ap(g, fi),
                                                         k2_ap(g, h, fi),
                                                         Alu.mult, Alu.add)
                                        ve.tensor_scalar(ys, s16[spar][:],
                                                         k1_ap(g, fi),
                                                         k2_ap(g, h, fi),
                                                         Alu.mult, Alu.add)
                                        ve.tensor_tensor(ys, ys, ns, Alu.subtract)
                                        ve.tensor_scalar(ns.bitcast(F16) if False
                                                         else ns, ys, 0.5, None,
                                                         Alu.is_ge)
                                        vv = ve.tensor_tensor(ys, ys, ns,
                                                              Alu.subtract)
                                        if fi == NFREQ - 1:
                                            s16_free_V[spar] = k.cnt["V"] + 1
                                        rr_V[fi] = k.inc(vv, "V", 1)
                                    f0 = fp * 2
                                    k.wait(sc, "V", rr_V[f0 + nf - 1])
                                    k.wait(sc, "P", diag_P[f0 + nf - 1])
                                    av = k.inc(sc.activation(
                                        slabS[:, f0 * IT:(f0 + nf) * IT],
                                        slabY[:, f0 * IT:(f0 + nf) * IT],
                                        Act.Sin, bias=0.0, scale=cst(g, 4)),
                                        "A", 1)
                                    for ff in range(nf):
                                        sin_A[f0 + ff] = av
                                    k.wait(te, "A", av)
                                    k.wait(te, "V", diag_V)
                                    if fp == 0:
                                        k.wait(te, "A", tanh_A)
                                    for ff in range(nf):
                                        fi = f0 + ff
                                        for ih in range(2):
                                            mm = te.matmul(
                                                ps_main[:, IT + ih * 512:
                                                        IT + (ih + 1) * 512],
                                                diag16[:, fi * 128:(fi + 1) * 128],
                                                slabS[:, fi * IT + ih * 512:
                                                      fi * IT + (ih + 1) * 512],
                                                start=(fi == 0), stop=(fi == NFREQ - 1))
                                        diag_P[fi] = k.inc(mm, "P", 1)
                                rslot = j % 2
                                k.wait(sc, "P", max(diag_P[NFREQ - 1],
                                                    w16r_free_P[rslot]))
                                tanh_A = k.inc(sc.activation(
                                    w16r[rslot][:], ps_main[:, IT:2 * IT],
                                    Act.Tanh, bias=0.0, scale=cst(g, 3)), "A", 1)
                                k.wait(te, "A", tanh_A)
                                if j == 0:
                                    k.wait(te, "V", av_evac_V)
                                for ih in range(2):
                                    psx = psA if ih == 0 else psB
                                    mm = te.matmul(
                                        psx[0:65, 0:512],
                                        vt16[par][:, j * 130 + hh * 65:
                                                  j * 130 + hh * 65 + 65],
                                        w16r[rslot][:, ih * 512:(ih + 1) * 512],
                                        start=(j == 0), stop=(j == NJT - 1))
                                w16r_free_P[rslot] = k.inc(mm, "P", 1)
                        av_done_P = k.cnt["P"]
                        apar = h % 2
                        k.wait(ve, "P", av_done_P)
                        if it == 0:
                            k.wait(ve, f"Ga{apar}", ah_free_G[apar])
                        k.wait(ve, "Gr", rssc_free_G)
                        for ih in range(2):
                            psx = psA if ih == 0 else psB
                            ve.tensor_scalar(
                                ah16[apar][:, ilo + ih * 512: ilo + (ih + 1) * 512],
                                psx[0:64, 0:512], 0.5,
                                cv16[:, 2 * par + hh:2 * par + hh + 1],
                                Alu.mult, Alu.add)
                            vv = ve.tensor_copy(
                                rssc[64:65, ih * 512:(ih + 1) * 512],
                                psx[64:65, 0:512])
                        avv = k.inc(vv, "V", 1)
                        av_evac_V = avv
                        cv_read_V[par] = avv
                        k.wait(gp, "V", avv)
                        rssc_free_G = k.inc(gp.dma_start(
                            out=rs[h:h + 1, ilo:ilo + IT],
                            in_=rssc[64:65, 0:IT]), "Gr", 16)
                        if it == NIT - 1:
                            ah_free_G[apar] = k.inc(gp.dma_start(
                                out=dr["asp"][h, :, :], in_=ah16[apar][:]),
                                f"Ga{apar}", 16)
                    diag_used_P = k.cnt["P"]
                vt_av_P[par] = k.cnt["P"]
                qk_scores_P[par] = k.cnt["P"]
            k.wait(sc, "V")
            k.wait(sc, "Gr")
            sc.activation(lnt16[:], rs[:], Act.Ln, bias=cst(g, 1)[0:16, :],
                          scale=cst(g, 3)[0:16, :])
            k.inc(sc.activation(nrm16[:], lnt16[:], Act.Exp, bias=0.0,
                                scale=cst(g, 5)[0:16, :]), "A", 1)
        nrm_A = k.cnt["A"]
        a_spill_G = [k.cnt.get("Ga0", 0), k.cnt.get("Ga1", 0)]
        k.barrier()

        with contextlib.ExitStack() as es:
            def sb(name, shape, dtype):
                return es.enter_context(nc.sbuf_tensor(name, shape, dtype))
            k.dsem("Lwo0"), k.dsem("Lwo1")
            psO = es.enter_context(nc.psum_tensor("wo_O", [128, 512], F32))
            psN = es.enter_context(nc.psum_tensor("wo_N", [128, 512], F32))
            psS1 = es.enter_context(nc.psum_tensor("wo_S1", [128, 512], F32))
            psS2 = es.enter_context(nc.psum_tensor("wo_S2", [128, 512], F32))
            wo16 = sb("wo16", [128, DC * D], F16)
            wstg = [sb(f"wo_stg_{i}", [128, D], F32) for i in range(2)]
            ahl = [sb(f"wo_ahl_{i}", [64, 512], F16) for i in range(2)]
            nbc16 = [sb(f"wo_nbc_{i}", [64, 512], F16) for i in range(2)]
            nrow16 = [sb(f"wo_nrow_{i}", [1, 512], F16) for i in range(2)]
            anrm = [sb(f"wo_anrm_{i}", [64, 512], F16) for i in range(2)]
            worhs = sb("wo_rhs", [128, DC * 512], F16)
            o_sb = sb("wo_osb", [128, DC * 512], F32)
            sq32 = sb("wo_sq32", [128, LNT], F32)
            sq32b = sb("wo_sq32b", [128, LNT], F32)
            rowA = sb("wo_rowA", [1, LNT], F16)
            rowB = sb("wo_rowB", [1, LNT], F16)
            rowC = sb("wo_rowC", [1, LNT], F16)
            ab16 = sb("wo_ab16", [128, 2 * LNT], F16)
            t32 = sb("wo_t32", [128, 512], F32)
            aout32 = [sb(f"wo_aout_{i}", [128, 512], F32) for i in range(2)]

            wo_conv_V = 0
            for c in range(DC):
                par = c % 2
                k.wait(sy, "V", wo_conv_V if c >= 2 else 0)
                lv = k.inc(sy.dma_start(out=wstg[par][:],
                                        in_=dr["woT"][c * 128:(c + 1) * 128, :]),
                           f"Lwo{par}", 16)
                k.wait(ve, f"Lwo{par}", lv)
                wo_conv_V = k.inc(ve.tensor_copy(wo16[:, c * D:(c + 1) * D],
                                                 wstg[par][:]), "V", 1)

            rel_free_V = [0, 0]
            nbc_free_V = [0, 0]
            nrow_mm_P = [0, 0]
            for _nm in ("Gl0", "Gl1", "Gn0", "Gn1", "Gm0", "Gm1", "Go0", "Go1"):
                k.dsem(_nm)
            anrm_free_G = [0, 0]
            worhs_ready_G = 0
            aout_free_G = [0, 0]
            wo_mm_P = 0
            osb_read_V = 0
            for it4 in range(4):
                tlo = it4 * 512
                k.wait(gp, "P", wo_mm_P)
                for c in range(DC):
                    for hh in range(2):
                        hx = 2 * c + hh
                        par = hx % 2
                        k.wait(gp, "Ga0", a_spill_G[0])
                        k.wait(gp, "Ga1", a_spill_G[1])
                        k.wait(gp, "V", rel_free_V[par])
                        lv = k.inc(gp.dma_start(
                            out=ahl[par][:],
                            in_=dr["asp"][hx, :, tlo:tlo + 512]), f"Gl{par}", 16)
                        k.wait(gp, "A", nrm_A)
                        k.wait(gp, "P", nrow_mm_P[par])
                        nlv = k.inc(gp.dma_start(
                            out=nrow16[par][:],
                            in_=nrm16[hx:hx + 1, tlo:tlo + 512]), f"Gn{par}", 16)
                        k.wait(te, f"Gn{par}", nlv)
                        k.wait(te, "V", nbc_free_V[par])
                        pv = k.inc(te.matmul(
                            psN[0:64, 0:512],
                            g["ones_r16"][0:1, 0:64],
                            nrow16[par][:],
                            start=True, stop=True), "P", 1)
                        nrow_mm_P[par] = pv
                        k.wait(ve, "P", pv)
                        nbv = k.inc(ve.tensor_copy(
                            nbc16[par][:], psN[0:64, 0:512]), "V", 1)
                        nbc_free_V[par] = nbv
                        k.wait(ve, f"Gl{par}", lv)
                        k.wait(ve, f"Gm{par}", anrm_free_G[par])
                        rel_free_V[par] = k.inc(ve.tensor_tensor(
                            anrm[par][:], ahl[par][:], nbc16[par][:], Alu.mult),
                            "V", 1)
                        k.wait(gp, "V", rel_free_V[par])
                        anrm_free_G[par] = k.inc(gp.dma_start(
                            out=worhs[hh * 64:(hh + 1) * 64,
                                      c * 512:(c + 1) * 512],
                            in_=anrm[par][:]), f"Gm{par}", 16)
                k.wait(te, "Gm0", k.cnt["Gm0"])
                k.wait(te, "Gm1", k.cnt["Gm1"])
                k.wait(te, "V", wo_conv_V)
                k.wait(sc, "V", osb_read_V)
                for hc in range(DC):
                    for c in range(DC):
                        mm = te.matmul(psO[:, 0:512],
                                       wo16[:, c * D + hc * 128: c * D + (hc + 1) * 128],
                                       worhs[:, c * 512:(c + 1) * 512],
                                       start=(c == 0), stop=(c == DC - 1))
                    pv = k.inc(mm, "P", 1)
                    k.wait(sc, "P", pv)
                    av = k.inc(sc.activation(o_sb[:, hc * 512:(hc + 1) * 512],
                                             psO[:, 0:512], Act.Identity,
                                             bias=g["bor"][:, hc:hc + 1],
                                             scale=1.0), "A", 1)
                    k.wait(te, "A", av)
                wo_mm_P = k.cnt["P"]
                abv = emit_ln_tile(
                    k, g, lambda c: o_sb[:, c * 512:(c + 1) * 512],
                    psS1, psS2, psS1, psS2, [sq32, sq32b], rowA, rowB, rowC, ab16)
                for c in range(DC):
                    par = c % 2
                    k.wait(ve, "V", abv)
                    k.wait(ve, f"Go{par}", aout_free_G[par])
                    ve.tensor_tensor(t32[:], o_sb[:, c * 512:(c + 1) * 512],
                                     ab16[:, 0:LNT], Alu.mult)
                    ve.tensor_tensor(t32[:], t32[:], ab16[:, LNT:2 * LNT], Alu.add)
                    vv = k.inc(ve.tensor_scalar(aout32[par][:], t32[:],
                                                g["gr"][:, c:c + 1],
                                                g["br"][:, c:c + 1],
                                                Alu.mult, Alu.add), "V", 1)
                    k.wait(gp, "V", vv)
                    aout_free_G[par] = k.inc(gp.dma_start(
                        out=g["r"][:, c * NT + tlo: c * NT + tlo + 512],
                        in_=aout32[par][:], accum_op=Alu.add), f"Go{par}", 16)
                osb_read_V = k.cnt["V"]
    k.barrier()



def emit_output(k, g, dr):
    k.barrier()
    gp = k.gp
    for c in range(DC):
        k.inc(gp.dma_start(out=dr["outT"][c * 128:(c + 1) * 128, :],
                           in_=g["r"][:, c * NT:(c + 1) * NT]), "G", 16)
    k.wait(gp, "G")


def build_program():
    nc = bass.Bass("TRN2", target_bir_lowering=False,
                   detect_race_conditions=DEBUG_SYNC)
    dr = {}
    dr["xT"] = nc.dram_tensor("xT", [D, NT], F32, kind="ExternalInput").ap()
    dr["wqT"] = nc.dram_tensor("wqT", [D, D], F32, kind="ExternalInput").ap()
    dr["wkT"] = nc.dram_tensor("wkT", [D, D], F32, kind="ExternalInput").ap()
    dr["wvT"] = nc.dram_tensor("wvT", [D, D], F32, kind="ExternalInput").ap()
    dr["woT"] = nc.dram_tensor("woT", [D, D], F32, kind="ExternalInput").ap()
    dr["w1T"] = nc.dram_tensor("w1T", [E, D, F], F32, kind="ExternalInput").ap()
    dr["w2T"] = nc.dram_tensor("w2T", [E, F, D], F32, kind="ExternalInput").ap()
    dr["rwT"] = nc.dram_tensor("rwT", [D, 8], F32, kind="ExternalInput").ap()
    dr["b1r"] = nc.dram_tensor("b1r", [128, 128], F32, kind="ExternalInput").ap()
    dr["b2"] = nc.dram_tensor("b2", [8, D], F32, kind="ExternalInput").ap()
    for nm in ("bqr", "bkr", "bor", "gr", "br"):
        dr[nm] = nc.dram_tensor(nm, [128, 8], F32, kind="ExternalInput").ap()
    dr["bv"] = nc.dram_tensor("bv", [1, D], F32, kind="ExternalInput").ap()
    dr["alphaf"] = nc.dram_tensor("alphaf", [1, 80], F32, kind="ExternalInput").ap()
    dr["phif"] = nc.dram_tensor("phif", [1, 80], F32, kind="ExternalInput").ap()
    dr["temp"] = nc.dram_tensor("temp", [1, 1], F32, kind="ExternalInput").ap()
    dr["outT"] = nc.dram_tensor("outT", [D, NT], F32, kind="ExternalOutput").ap()
    dr["usp"] = nc.dram_tensor("usp", [D, NT], F16, kind="Internal").ap()
    dr["asp"] = nc.dram_tensor("asp", [NH, 64, NT], F16, kind="Internal").ap()

    with contextlib.ExitStack() as es:
        k = Ctx(nc, es)
        g = {}
        emit_setup(k, dr, g)
        if DO_FFN1:
            emit_ffn(k, g, dr, "f1")
        if DO_ATTN:
            emit_attention(k, g, dr)
        if DO_FFN2:
            emit_ffn(k, g, dr, "f2")
        emit_output(k, g, dr)
    return nc


def host_prep(inputs):
    asc = np.ascontiguousarray
    w = {}
    w["wqT"] = asc(inputs["L0_attn_Wq"].T)
    w["wkT"] = asc(inputs["L0_attn_Wk"].T)
    w["wvT"] = asc(inputs["L0_attn_Wv"].T)
    w["woT"] = asc(inputs["L0_attn_Wo"].T)
    w["w1T"] = asc(np.transpose(inputs["L0_ffn_w1"], (0, 2, 1)))
    w["w2T"] = asc(np.transpose(inputs["L0_ffn_w2"], (0, 2, 1)))
    w["rwT"] = asc(inputs["L0_ffn_router_w"].T)
    w["b1r"] = asc(np.transpose(inputs["L0_ffn_b1"].reshape(8, 16, 128),
                                (2, 0, 1)).reshape(128, 128))
    w["b2"] = asc(inputs["L0_ffn_b2"])
    w["bqr"] = asc(inputs["L0_attn_bq"].reshape(8, 128).T)
    w["bkr"] = asc(inputs["L0_attn_bk"].reshape(8, 128).T)
    w["bor"] = asc(inputs["L0_attn_bo"].reshape(8, 128).T)
    w["gr"] = asc(inputs["L0_attn_gamma"].reshape(8, 128).T)
    w["br"] = asc(inputs["L0_attn_beta"].reshape(8, 128).T)
    w["bv"] = asc(inputs["L0_attn_bv"].reshape(1, D))
    w["alphaf"] = asc(inputs["L0_attn_alpha"].reshape(1, 80))
    w["phif"] = asc(inputs["L0_attn_phi"].reshape(1, 80))
    w["temp"] = asc(inputs["L0_attn_temperature"].reshape(1, 1))
    return w


def kernel(**inputs):
    x = np.asarray(inputs["L0_attn_x"], dtype=np.float32)
    inputs = {kk: np.asarray(v, dtype=np.float32) for kk, v in inputs.items()}
    w = host_prep(inputs)
    nc = build_program()
    in_maps = []
    for c in range(8):
        m = dict(w)
        m["xT"] = np.ascontiguousarray(x[c].T)
        in_maps.append(m)
    res = run_bass_kernel_spmd(nc, in_maps, core_ids=list(range(8)))
    out = np.stack([np.asarray(res.results[c]["outT"]).T for c in range(8)])
    return out.astype(np.float32)


# version 04 (final)
# speedup vs baseline: 1.4651x; 1.0238x over previous
import contextlib
import numpy as np

import concourse.bass as bass
import concourse.mybir as mybir
from concourse.bass_utils import run_bass_kernel_spmd

F32 = mybir.dt.float32
F16 = mybir.dt.float16
I16 = mybir.dt.int16
Alu = mybir.AluOpType
Act = mybir.ActivationFunctionType
AX = mybir.AxisListType.X

NT = 2048
D = 1024
DC = 8
F = 2048
FC = 16
E = 8
NH = 16
HD = 64
NPAIR = 8
TT = 256
NTT = NT // TT
LNT = 512
FREQS = [31.0, 37.0, 41.0, 43.0, 47.0]
NFREQ = 5
TWO_PI_SAFE = 6.283185
INV_2PI = 1.0 / (2.0 * np.pi)
IT = 1024
NIT = NT // IT
NJT = NT // 128

DO_FFN1 = True
DO_ATTN = True
DO_FFN2 = True
DEBUG_TAPS = False
DEBUG_SYNC = False


class _DrainAfter:
    def __init__(self, eng):
        self._eng = eng

    def __getattr__(self, name):
        attr = getattr(self._eng, name)
        if name in ("wait_ge", "drain", "dma_start", "sem_inc", "nop"):
            return attr
        if not callable(attr):
            return attr

        def wrapped(*a, **kw):
            r = attr(*a, **kw)
            try:
                self._eng.drain()
            except Exception:
                pass
            return r
        return wrapped


class Ctx:
    def __init__(self, nc, es):
        self.nc = nc
        self.es = es
        self.sy, self.ve, self.sc, self.te, self.gp = (
            nc.sync, nc.vector, nc.scalar, nc.tensor, nc.gpsimd)
        if DEBUG_SYNC:
            self.ve = _DrainAfter(nc.vector)
            self.sc = _DrainAfter(nc.scalar)
        self.engs = [self.sy, self.ve, self.sc, self.te, self.gp]
        self.sems = {}
        self.cnt = {}
        self.lastwait = {}
        for name in ("L", "G", "P", "A", "V", "Q"):
            self.sems[name] = es.enter_context(nc.semaphore("sem" + name))
            self.cnt[name] = 0

    def dsem(self, name):
        if name not in self.sems:
            self.sems[name] = self.es.enter_context(
                self.nc.semaphore("sem" + name))
            self.cnt[name] = 0
        return self.sems[name]

    def inc(self, instr, name, amt=1):
        if name not in self.sems:
            self.dsem(name)
        instr.then_inc(self.sems[name], amt)
        self.cnt[name] += amt
        return self.cnt[name]

    def wait(self, eng, name, val=None):
        if val is None:
            val = self.cnt[name]
        if val <= 0:
            return
        key = (id(eng), name)
        if self.lastwait.get(key, -1) >= val:
            return
        eng.wait_ge(self.sems[name], val)
        self.lastwait[key] = val

    def barrier(self):
        for eng in self.engs:
            for name in list(self.sems):
                self.wait(eng, name)

    def sb(self, name, shape, dtype):
        return self.es.enter_context(self.nc.sbuf_tensor(name, shape, dtype))



def emit_setup(k, dr, g):
    nc = k.nc
    g["r"] = k.sb("r", [128, DC * NT], F32)
    g["b1r"] = k.sb("s_b1r", [128, 128], F32)
    g["b2_16"] = k.sb("b2_16", [8, D], F16)
    for nm in ("bqr", "bkr", "bor", "gr", "br"):
        g[nm] = k.sb("s_" + nm, [128, 8], F32)
    g["bv16"] = k.sb("bv16", [1, D], F16)
    g["rw16"] = k.sb("rw16", [128, 64], F16)
    g["ident16"] = k.sb("ident16", [128, 128], F16)
    g["ones_c32"] = k.sb("ones_c32", [128, 1], F32)
    g["ones_c16"] = k.sb("ones_c16", [128, 1], F16)
    g["ones_r32"] = k.sb("ones_r32", [1, 128], F32)
    g["ones_r16"] = k.sb("ones_r16", [1, 128], F16)
    g["kvec"] = k.sb("kvec", [128, 176], F32)
    g["cst"] = k.sb("cst", [128, 8], F32)

    sy, ve, sc, te, gp = k.sy, k.ve, k.sc, k.te, k.gp

    with contextlib.ExitStack() as es:
        def sb(name, shape, dtype):
            return es.enter_context(nc.sbuf_tensor(name, shape, dtype))
        rows = sb("rows", [1, 96], F32)
        alphast = sb("alphast", [1, 80], F32)
        phist = sb("phist", [1, 80], F32)
        t11 = sb("t11", [1, 1], F32)
        iop = sb("iop", [128, 1], F32)
        iof = sb("iof", [128, 128], F32)
        b2f = sb("b2f_stage", [8, D], F32)
        bvf = sb("bv_stage", [1, D], F32)
        rwf = sb("rw_stage", [128, 64], F32)

        for c in range(DC):
            k.inc(sy.dma_start(out=g["r"][:, c * NT:(c + 1) * NT],
                               in_=dr["xT"][c * 128:(c + 1) * 128, :]), "L", 16)
        k.inc(sy.dma_start(out=g["b1r"][:], in_=dr["b1r"]), "L", 16)
        k.inc(sy.dma_start(out=b2f[:], in_=dr["b2"]), "L", 16)
        for nm in ("bqr", "bkr", "bor", "gr", "br"):
            k.inc(sy.dma_start(out=g[nm][:], in_=dr[nm]), "L", 16)
        k.inc(sy.dma_start(out=bvf[:], in_=dr["bv"]), "L", 16)
        k.inc(sy.dma_start(out=rwf[:].rearrange("p (c e) -> p c e", e=8),
                           in_=dr["rwT"].rearrange("(c p) e -> p c e", p=128)),
              "L", 16)
        k.inc(sy.dma_start(out=phist[:], in_=dr["phif"]), "L", 16)
        k.inc(sy.dma_start(out=alphast[:], in_=dr["alphaf"]), "L", 16)
        k.inc(sy.dma_start(out=t11[:], in_=dr["temp"]), "L", 16)
        ldone = k.cnt["L"]

        k.inc(gp.iota(iop[:], pattern=[[0, 1]], channel_multiplier=1,
                      allow_small_or_imprecise_dtypes=True), "Q", 1)
        k.inc(gp.iota(iof[:], pattern=[[1, 128]], channel_multiplier=0,
                      allow_small_or_imprecise_dtypes=True), "Q", 1)

        k.wait(ve, "L", ldone)
        k.wait(ve, "Q")
        ve.tensor_scalar(g["ident16"][:], iof[:], iop[:], None, Alu.is_equal)
        for i, v in enumerate([1e-5, 1024.0 + 1e-10, -0.5, 0.5, TWO_PI_SAFE, -1.0]):
            ve.memset(g["cst"][:, i:i + 1], v)
        ve.memset(g["ones_c32"][:], 1.0)
        ve.memset(g["ones_c16"][:], 1.0)
        ve.memset(g["ones_r32"][:], 1.0)
        ve.memset(g["ones_r16"][:], 1.0)
        ve.tensor_copy(g["b2_16"][:], b2f[:])
        ve.tensor_copy(g["bv16"][:], bvf[:])
        ve.tensor_copy(g["rw16"][:], rwf[:])
        k.wait(sc, "L", ldone)
        k.inc(sc.activation(rows[0:1, 7:8], t11[:], Act.Abs, bias=0.0, scale=1.0),
              "A", 1)
        k.wait(ve, "A")
        ve.tensor_scalar(rows[0:1, 6:7], rows[0:1, 7:8], 8.0, 0.8, Alu.mult, Alu.add)
        ve.drain()
        ve.reciprocal(rows[0:1, 5:6], rows[0:1, 6:7])
        for i, f in enumerate(FREQS):
            ve.memset(rows[0:1, i:i + 1], 1.0 / f)
        ve.drain()
        ve.tensor_scalar(rows[0:1, 8:13], rows[0:1, 0:5], rows[0:1, 5:6],
                         None, Alu.mult)
        k.inc(ve.tensor_scalar(rows[0:1, 16:96], phist[:], INV_2PI, 2.0,
                               Alu.mult, Alu.add), "V", 1)
        with nc.psum_tensor("ps_setup", [128, 512], F32) as psb:
            k.wait(te, "V")
            te.matmul(psb[:, 0:5], g["ones_r32"][:], rows[0:1, 8:13],
                      start=True, stop=True)
            te.matmul(psb[:, 16:96], g["ones_r32"][:], rows[0:1, 16:96],
                      start=True, stop=True)
            k.inc(te.matmul(psb[:, 96:176], g["ones_r32"][:], alphast[:],
                            start=True, stop=True), "P", 1)
            k.wait(ve, "P")
            ve.tensor_copy(g["kvec"][:, 0:5], psb[:, 0:5])
            ve.tensor_copy(g["kvec"][:, 16:96], psb[:, 16:96])
            k.inc(ve.tensor_copy(g["kvec"][:, 96:176], psb[:, 96:176]), "V", 1)
        k.barrier()


def cst(g, i):
    return g["cst"][:, i:i + 1]


def k1_ap(g, fi):
    return g["kvec"][:, fi:fi + 1]


def k2_ap(g, h, fi):
    return g["kvec"][:, 16 + h * 5 + fi:16 + h * 5 + fi + 1]


def al_ap(g, h, fi):
    return g["kvec"][:, 96 + h * 5 + fi:96 + h * 5 + fi + 1]



def emit_ln_tile(k, g, src_fn, psSUM, psSQ, psBC1, psBC2, sqp, rowA, rowB,
                 rowC, ab16):
    te, sc, ve = k.te, k.sc, k.ve
    k.wait(te, "V")
    sq_mm_P = [0, 0]
    for c in range(DC):
        par = c % 2
        k.wait(sc, "P", sq_mm_P[par])
        av = k.inc(sc.activation(sqp[par][:], src_fn(c), Act.Square,
                                 bias=0.0, scale=1.0), "A", 1)
        te.matmul(psSUM[0:1, 0:LNT], g["ones_c32"][:], src_fn(c),
                  start=(c == 0), stop=(c == DC - 1))
        k.wait(te, "A", av)
        mm = te.matmul(psSQ[0:1, 0:LNT], g["ones_c32"][:], sqp[par][:],
                       start=(c == 0), stop=(c == DC - 1))
        sq_mm_P[par] = k.inc(mm, "P", 1)
    pv = k.cnt["P"]
    k.wait(ve, "P", pv)
    ve.tensor_scalar(rowA[:], psSUM[0:1, 0:LNT], 1.0 / D, None, Alu.mult)
    ve.tensor_scalar(psSUM[0:1, 0:LNT], psSQ[0:1, 0:LNT], 1.0 / D, None, Alu.mult)
    ve.tensor_tensor(rowB[:], rowA[:], rowA[:], Alu.mult)
    k.inc(ve.tensor_tensor(psSQ[0:1, 0:LNT], psSUM[0:1, 0:LNT], rowB[:],
                           Alu.subtract), "V", 1)
    k.wait(sc, "V")
    sc.activation(psSUM[0:1, 0:LNT], psSQ[0:1, 0:LNT], Act.Ln,
                  bias=cst(g, 0)[0:1, :], scale=1.0)
    k.inc(sc.activation(rowC[:], psSUM[0:1, 0:LNT], Act.Exp, bias=0.0,
                        scale=cst(g, 2)[0:1, :]), "A", 1)
    k.wait(ve, "A")
    ve.tensor_tensor(rowB[:], rowA[:], rowC[:], Alu.mult)
    k.inc(ve.tensor_scalar(rowA[:], rowB[:], -1.0, None, Alu.mult), "V", 1)
    k.wait(te, "V")
    te.matmul(psBC1[:, 0:LNT], g["ones_r16"][:], rowC[:], start=True, stop=True)
    pv = k.inc(te.matmul(psBC2[:, 0:LNT], g["ones_r16"][:], rowA[:],
                         start=True, stop=True), "P", 1)
    k.wait(ve, "P", pv)
    ve.tensor_copy(ab16[:, 0:LNT], psBC1[:, 0:LNT])
    return k.inc(ve.tensor_copy(ab16[:, LNT:2 * LNT], psBC2[:, 0:LNT]), "V", 1)



def emit_ffn(k, g, dr, tag):
    nc = k.nc
    te, sc, ve, sy, gp = k.te, k.sc, k.ve, k.sy, k.gp
    k.barrier()
    with contextlib.ExitStack() as es:
        def sb(name, shape, dtype):
            return es.enter_context(nc.sbuf_tensor(f"{name}_{tag}", shape, dtype))

        def psf(name, shape, dtype=F32):
            return es.enter_context(nc.psum_tensor(f"{name}_{tag}", shape, dtype))

        k.dsem("Lst0"), k.dsem("Lst1"), k.dsem("Gc0"), k.dsem("Gc1")
        u16 = sb("u16", [128, DC * NT], F16)
        w1s = sb("w1s", [128, DC * F], F16)
        w2s = sb("w2s", [128, FC * D], F16)
        stage = sb("wstage", [128, 2048], F32)
        g16 = sb("g16", [128, FC * 512], F16)
        graw = sb("graw", [128, 512], F16)
        tmp32 = [sb(f"tmp32_{i}", [128, 512], F32) for i in range(2)]
        rowA = sb("rowA", [1, LNT], F16)
        rowB = sb("rowB", [1, LNT], F16)
        rowC = sb("rowC", [1, LNT], F16)
        ab16 = sb("ab16", [128, 2 * LNT], F16)
        t16 = graw
        cem16 = sb("cem16", [8, NT], F16)
        lgt = sb("lgt", [128, 8], F32)
        msk1 = sb("msk1", [128, 8], F32)
        msk2 = sb("msk2", [128, 8], F32)
        xmk = sb("xmk", [128, 8], F32)
        mrow = sb("mrow", [128, 8], F32)
        ctm16 = sb("ctm16", [128, 8], F16)
        cbc16 = [sb(f"cbc16_{i}", [128, 512], F16) for i in range(2)]
        cemrow = [sb("cemrow_0", [1, 512], F16)]

        ps_main = psf("ps_main", [128, 2048])
        psA = psf("psA", [128, 512])
        psB = psf("psB", [128, 512])
        psC = psf("psC", [128, 512])

        for tt in range(NT // LNT):
            lo = tt * LNT
            abv = emit_ln_tile(
                k, g,
                lambda c, lo=lo: g["r"][:, c * NT + lo: c * NT + lo + LNT],
                psA, psB, psC, psA, [tmp32[0], tmp32[1]], rowA, rowB, rowC, ab16)
            for c in range(DC):
                k.wait(ve, "V", abv)
                ve.tensor_tensor(t16[:], g["r"][:, c * NT + lo: c * NT + lo + LNT],
                                 ab16[:, 0:LNT], Alu.mult)
                vv = ve.tensor_tensor(u16[:, c * NT + lo: c * NT + lo + LNT],
                                      t16[:], ab16[:, LNT:2 * LNT], Alu.add)
            k.inc(vv, "V", 1)
        u_ready_V = k.cnt["V"]

        with nc.psum_tensor(f"psT16_{tag}", [128, 128], F16) as psT16:
            for tj in range(NT // 128):
                jlo = tj * 128
                k.wait(te, "V", u_ready_V)
                for c in range(DC):
                    mm = te.matmul(psA[:, 0:8],
                                   u16[:, c * NT + jlo: c * NT + jlo + 128],
                                   g["rw16"][:, 8 * c:8 * c + 8],
                                   start=(c == 0), stop=(c == DC - 1))
                pv = k.inc(mm, "P", 1)
                k.wait(ve, "P", pv)
                k.inc(ve.tensor_copy(lgt[:], psA[:, 0:8]), "V", 1)
                ve.drain()
                ve.reduce_max(mrow[:, 0:1], lgt[:], axis=AX)
                ve.drain()
                ve.tensor_scalar(msk1[:], lgt[:], mrow[:, 0:1], None, Alu.is_equal)
                ve.drain()
                ve.scalar_tensor_tensor(xmk[:], msk1[:], -1e30, lgt[:],
                                        Alu.mult, Alu.add)
                ve.drain()
                ve.reduce_max(mrow[:, 1:2], xmk[:], axis=AX)
                ve.drain()
                ve.tensor_scalar(msk2[:], xmk[:], mrow[:, 1:2], None, Alu.is_equal)
                ve.drain()
                k.inc(ve.tensor_tensor(mrow[:, 2:3], mrow[:, 0:1], mrow[:, 1:2],
                                       Alu.subtract), "V", 1)
                k.wait(sc, "V")
                k.inc(sc.activation(mrow[:, 3:4], mrow[:, 2:3], Act.Tanh,
                                    bias=0.0, scale=cst(g, 3)), "A", 1)
                k.wait(ve, "A")
                ve.tensor_scalar(mrow[:, 4:5], mrow[:, 3:4], 0.25, 0.25,
                                 Alu.mult, Alu.add)
                ve.tensor_scalar(mrow[:, 5:6], mrow[:, 3:4], -0.25, 0.25,
                                 Alu.mult, Alu.add)
                ve.drain()
                ve.tensor_scalar(xmk[:], msk1[:], mrow[:, 4:5], None, Alu.mult)
                ve.drain()
                k.inc(ve.scalar_tensor_tensor(ctm16[:], msk2[:], mrow[:, 5:6],
                                              xmk[:], Alu.mult, Alu.add), "V", 1)
                k.wait(te, "V")
                pv = k.inc(te.transpose(psT16[0:8, 0:128], ctm16[:],
                                        g["ident16"][:]), "P", 1)
                k.wait(ve, "P", pv)
                k.inc(ve.tensor_copy(cem16[0:8, jlo:jlo + 128],
                                     psT16[0:8, 0:128]), "V", 1)
        cem_ready_V = k.cnt["V"]

        if DEBUG_TAPS:
            dbg_u = k.nc.dram_tensor(f"dbg_u16_{tag}", [128, DC * NT], F16,
                                     kind="ExternalOutput").ap()
            dbg_c = k.nc.dram_tensor(f"dbg_cem_{tag}", [8, NT], F16,
                                     kind="ExternalOutput").ap()
            k.wait(gp, "V", cem_ready_V)
            k.inc(gp.dma_start(out=dbg_u, in_=u16[:]), "G", 16)
            k.inc(gp.dma_start(out=dbg_c, in_=cem16[:]), "G", 16)

        stage_free = [0, 0]

        def stream_w1(e, gate_P):
            for hc in range(DC):
                for half in range(2):
                    par = (hc * 2 + half) % 2
                    k.wait(sy, "V", stage_free[par])
                    lv = k.inc(sy.dma_start(
                        out=stage[:, par * 1024:(par + 1) * 1024],
                        in_=dr["w1T"][e, hc * 128:(hc + 1) * 128,
                                      half * 1024:(half + 1) * 1024]),
                        f"Lst{par}", 16)
                    k.wait(ve, f"Lst{par}", lv)
                    k.wait(ve, "P", gate_P)
                    stage_free[par] = k.inc(ve.tensor_copy(
                        w1s[:, hc * F + half * 1024: hc * F + (half + 1) * 1024],
                        stage[:, par * 1024:(par + 1) * 1024]), "V", 1)
            return k.cnt["V"]

        def stream_w2(e, gate_P):
            for fc in range(FC):
                par = fc % 2
                k.wait(sy, "V", stage_free[par])
                lv = k.inc(sy.dma_start(
                    out=stage[:, par * 1024:(par + 1) * 1024],
                    in_=dr["w2T"][e, fc * 128:(fc + 1) * 128, :]),
                    f"Lst{par}", 16)
                k.wait(ve, f"Lst{par}", lv)
                k.wait(ve, "P", gate_P)
                stage_free[par] = k.inc(ve.tensor_copy(
                    w2s[:, fc * D: fc * D + 1024],
                    stage[:, par * 1024:(par + 1) * 1024]), "V", 1)
            return k.cnt["V"]

        w1_ready = stream_w1(0, 0)
        w2_ready = stream_w2(0, 0)
        aw_done_P = 0
        bw_done_P = 0
        gelu_free_A = {0: 0, 1: 0}
        graw_free_V = 0
        cbc_used_V = [0, 0]
        g16_free_P = {0: 0, 1: 0}
        evac_last_A = 0
        k.dsem("Gt0"), k.dsem("Gt1")
        r_acc_G = [[0, 0] for _ in range(NTT)]
        tmp_free_G = [0, 0]

        cemrow_G = [0, 0]
        cemrow_mm_P = [0, 0]

        def issue_cemrow(e2, tp2):
            k.wait(gp, "V", cem_ready_V)
            k.wait(gp, "P", cemrow_mm_P[0])
            cemrow_G[0] = k.inc(gp.dma_start(
                out=cemrow[0][:],
                in_=cem16[e2:e2 + 1, tp2 * 512:(tp2 + 1) * 512]), "Gc0", 16)

        issue_cemrow(0, 0)
        cbc_V = [0, 0]

        def build_cbc(e2, tp2):
            par2 = tp2 % 2
            k.wait(te, "V", max(cem_ready_V, cbc_used_V[par2]))
            k.wait(te, "Gc0", cemrow_G[0])
            pv = k.inc(te.matmul(psC[:, 0:512], g["ones_r16"][:],
                                 cemrow[0][:],
                                 start=True, stop=True), "P", 1)
            cemrow_mm_P[0] = pv
            k.wait(ve, "P", pv)
            cbc_V[par2] = k.inc(ve.tensor_copy(cbc16[par2][:], psC[:, 0:512]),
                                "V", 1)
            if tp2 + 1 < NTT // 2:
                issue_cemrow(e2, tp2 + 1)
            elif e2 + 1 < E:
                issue_cemrow(e2 + 1, 0)

        build_cbc(0, 0)
        for e in range(E):
            for tp in range(NTT // 2):
                gpar = tp % 2
                plo = tp * 512
                cbv = cbc_V[gpar]
                k.wait(te, "V", w1_ready)
                for fc in range(FC):
                    par = fc % 2
                    psg = psA if par == 0 else psB
                    k.wait(te, "A", gelu_free_A[par])
                    for hc in range(DC):
                        mm = te.matmul(
                            psg[:, 0:512],
                            w1s[:, hc * F + fc * 128: hc * F + (fc + 1) * 128],
                            u16[:, hc * NT + plo: hc * NT + plo + 512],
                            start=(hc == 0), stop=(hc == DC - 1))
                    pv = k.inc(mm, "P", 1)
                    k.wait(sc, "P", max(pv, g16_free_P[0]))
                    k.wait(sc, "V", graw_free_V)
                    av = k.inc(sc.activation(
                        graw[:, 0:512], psg[:, 0:512], Act.Gelu,
                        bias=g["b1r"][:, e * 16 + fc:e * 16 + fc + 1],
                        scale=1.0), "A", 1)
                    gelu_free_A[par] = av
                    k.wait(ve, "A", av)
                    k.wait(ve, "V", cbv)
                    k.wait(ve, "P", g16_free_P[0])
                    graw_free_V = k.inc(ve.tensor_tensor(
                        g16[:, fc * 512:(fc + 1) * 512],
                        graw[:, 0:512], cbc16[gpar][:], Alu.mult), "V", 1)
                g_ready_V = k.cnt["V"]
                cbc_used_V[gpar] = g_ready_V
                if tp + 1 < NTT // 2:
                    build_cbc(e, tp + 1)
                elif e + 1 < E:
                    build_cbc(e + 1, 0)
                if tp == NTT // 2 - 1:
                    aw_done_P = k.cnt["P"]
                for sub in range(2):
                    tt = tp * 2 + sub
                    lo = tt * TT
                    k.wait(te, "V", max(w2_ready, g_ready_V))
                    k.wait(te, "A", evac_last_A)
                    for hc in range(DC):
                        first = True
                        if e == 0:
                            te.matmul(ps_main[:, hc * TT:(hc + 1) * TT],
                                      g["b2_16"][:, hc * 128:(hc + 1) * 128],
                                      cem16[:, lo:lo + TT], start=True, stop=False)
                            first = False
                        for fc in range(FC):
                            mm = te.matmul(
                                ps_main[:, hc * TT:(hc + 1) * TT],
                                w2s[:, fc * D + hc * 128: fc * D + (hc + 1) * 128],
                                g16[:, fc * 512 + sub * TT: fc * 512 + (sub + 1) * TT],
                                start=first, stop=(fc == FC - 1))
                            first = False
                    pv = k.inc(mm, "P", 1)
                    if sub == 1:
                        g16_free_P[0] = pv
                    if tt == NTT - 1:
                        bw_done_P = pv
                    k.wait(sc, "P", pv)
                    for q in range(4):
                        k.wait(sc, f"Gt{q % 2}", tmp_free_G[q % 2])
                        for hh in range(2):
                            hc = q * 2 + hh
                            av = k.inc(sc.activation(
                                tmp32[q % 2][:, hh * TT:(hh + 1) * TT],
                                ps_main[:, hc * TT:(hc + 1) * TT],
                                Act.Copy, bias=0.0, scale=1.0), "A", 1)
                        if q == 3:
                            evac_last_A = av
                        k.wait(gp, "A", av)
                        k.wait(gp, "Gt0", r_acc_G[tt][0])
                        k.wait(gp, "Gt1", r_acc_G[tt][1])
                        for hh in range(2):
                            hc = q * 2 + hh
                            gv = k.inc(gp.dma_start(
                                out=g["r"][:, hc * NT + lo: hc * NT + lo + TT],
                                in_=tmp32[q % 2][:, hh * TT:(hh + 1) * TT],
                                accum_op=Alu.add), f"Gt{q % 2}", 16)
                        tmp_free_G[q % 2] = gv
                    r_acc_G[tt] = [k.cnt["Gt0"], k.cnt["Gt1"]]
            if e + 1 < E:
                w1_ready = stream_w1(e + 1, aw_done_P)
                w2_ready = stream_w2(e + 1, bw_done_P)
    k.barrier()



def emit_attention(k, g, dr):
    nc = k.nc
    te, sc, ve, sy, gp = k.te, k.sc, k.ve, k.sy, k.gp
    k.barrier()
    with contextlib.ExitStack() as es0:
        def sb0(name, shape, dtype):
            return es0.enter_context(nc.sbuf_tensor(name, shape, dtype))

        rs = sb0("rs", [16, NT], F32)
        lnt16 = sb0("lnt16", [16, NT], F16)
        nrm16 = sb0("nrm16", [16, NT], F16)

        with contextlib.ExitStack() as es:
            def sb(name, shape, dtype):
                return es.enter_context(nc.sbuf_tensor(name, shape, dtype))
            psA = es.enter_context(nc.psum_tensor("l2_psA", [128, 512], F32))
            psB = es.enter_context(nc.psum_tensor("l2_psB", [128, 512], F32))
            psC = es.enter_context(nc.psum_tensor("l2_psC", [128, 512], F32))
            sq32 = sb("l2_sq32", [128, LNT], F32)
            sq32b = sb("l2_sq32b", [128, LNT], F32)
            rowA = sb("l2_rowA", [1, LNT], F16)
            rowB = sb("l2_rowB", [1, LNT], F16)
            rowC = sb("l2_rowC", [1, LNT], F16)
            ab16 = sb("l2_ab16", [128, 2 * LNT], F16)
            f32t = sb("l2_f32t", [128, LNT], F32)
            ut16 = [sb(f"l2_ut16_{i}", [128, LNT], F16) for i in range(2)]
            ut_free_G = [0, 0]
            for tt in range(NT // LNT):
                lo = tt * LNT
                abv = emit_ln_tile(
                    k, g,
                    lambda c, lo=lo: g["r"][:, c * NT + lo: c * NT + lo + LNT],
                    psA, psB, psC, psA, [sq32, sq32b], rowA, rowB, rowC, ab16)
                for c in range(DC):
                    par = c % 2
                    k.wait(ve, "V", abv)
                    k.wait(ve, f"Gu{par}", ut_free_G[par])
                    ve.tensor_tensor(f32t[:],
                                     g["r"][:, c * NT + lo: c * NT + lo + LNT],
                                     ab16[:, 0:LNT], Alu.mult)
                    vv = k.inc(ve.tensor_tensor(ut16[par][:], f32t[:],
                                                ab16[:, LNT:2 * LNT], Alu.add),
                               "V", 1)
                    k.wait(gp, "V", vv)
                    ut_free_G[par] = k.inc(gp.dma_start(
                        out=dr["usp"][c * 128:(c + 1) * 128, lo:lo + LNT],
                        in_=ut16[par][:]), f"Gu{par}", 16)
        u_sp_G = [k.cnt["Gu0"], k.cnt["Gu1"]]
        k.barrier()

        with contextlib.ExitStack() as es:
            def sb(name, shape, dtype):
                return es.enter_context(nc.sbuf_tensor(name, shape, dtype))

            ps_main = es.enter_context(nc.psum_tensor("at_main", [128, 2048], F32))
            psA = es.enter_context(nc.psum_tensor("at_A", [128, 512], F32))
            psB = es.enter_context(nc.psum_tensor("at_B", [128, 512], F32))
            psC = es.enter_context(nc.psum_tensor("at_C", [128, 512], F32))
            psD = es.enter_context(nc.psum_tensor("at_D", [128, 512], F32))

            for _nm in ("Lw", "Lu0", "Lu1"):
                k.dsem(_nm)
            wq16 = sb("wq16", [128, D], F16)
            wk16 = sb("wk16", [128, D], F16)
            wv16 = sb("wv16", [128, D], F16)
            wstage = sb("at_wstage", [128, D], F32)
            ustr = [sb(f"ustr_{i}", [128, IT], F16) for i in range(2)]
            q16 = [sb(f"q16_{i}", [128, NT], F16) for i in range(2)]
            k16 = [sb(f"k16_{i}", [128, NT], F16) for i in range(2)]
            vt16 = [sb(f"vt16_{i}", [128, NJT * 130], F16) for i in range(2)]
            s16 = [sb(f"s16_{i}", [128, IT], F16) for i in range(2)]
            slabY = sb("slabY", [128, NFREQ * IT], F16)
            slabN = sb("slabN", [128, NFREQ * IT], I16)
            slabS = sb("slabS", [128, NFREQ * IT], F16)
            diag16 = sb("diag16", [128, NFREQ * 128], F16)
            w16r = [sb(f"w16r_{i}", [128, IT], F16) for i in range(2)]
            cv16 = sb("cv16", [64, 4], F32)
            ah16 = [sb(f"ah16_{i}", [64, NT], F16) for i in range(2)]
            rssc = sb("rssc", [128, IT], F32)

            wconv_V = 0
            qk_scores_P = {0: 0, 1: 0}
            vt_av_P = {0: 0, 1: 0}
            ustr_free_P = [0, 0]
            w16r_free_P = [0, 0]
            ah_free_G = [0, 0]

            cv_read_V = {0: 0, 1: 0}
            rssc_free_G = 0
            k.dsem("Gr"), k.dsem("Ga0"), k.dsem("Ga1")

            for parx in range(2):
                for jt in range(NJT):
                    ve.memset(vt16[parx][:, jt * 130 + 64: jt * 130 + 65], 1.0)
                    mv = ve.memset(vt16[parx][:, jt * 130 + 129: jt * 130 + 130], 1.0)
            vt_ones_V = k.inc(mv, "V", 1)

            def stream_w(pair):
                nonlocal wconv_V
                par = pair % 2
                for (nm, dst) in (("wqT", wq16), ("wkT", wk16), ("wvT", wv16)):
                    k.wait(sy, "V", wconv_V)
                    lv = k.inc(sy.dma_start(
                        out=wstage[:].rearrange("p (c n) -> p c n", n=128),
                        in_=dr[nm][:, pair * 128:(pair + 1) * 128]
                            .rearrange("(c p) n -> p c n", p=128)), "Lw", 16)
                    k.wait(ve, "Lw", lv)
                    k.wait(ve, "P", qkv_used_P[0])
                    wconv_V = k.inc(ve.tensor_copy(dst[:], wstage[:]), "V", 1)

            qkv_used_P = [0]

            def emit_qkv(pair):
                par = pair % 2
                k.wait(te, "V")
                for (wsb, dst, brow) in ((wq16, q16[par], g["bqr"]),
                                         (wk16, k16[par], g["bkr"])):
                    k.wait(te, "P", qk_scores_P[par])
                    k.wait(te, "V", wconv_V)
                    for th in range(2):
                        tlo = th * IT
                        for c in range(DC):
                            upar = c % 2
                            k.wait(sy, "P", ustr_free_P[upar])
                            k.wait(sy, "Gu0", u_sp_G[0])
                            k.wait(sy, "Gu1", u_sp_G[1])
                            lv = k.inc(sy.dma_start(
                                out=ustr[upar][:],
                                in_=dr["usp"][c * 128:(c + 1) * 128,
                                              tlo:tlo + IT]), f"Lu{upar}", 16)
                            k.wait(te, f"Lu{upar}", lv)
                            for ih in range(2):
                                psx = psC if ih == 0 else psD
                                mm = te.matmul(
                                    psx[:, 0:512],
                                    wsb[:, c * 128:(c + 1) * 128],
                                    ustr[upar][:, ih * 512:(ih + 1) * 512],
                                    start=(c == 0), stop=(c == DC - 1))
                            ustr_free_P[upar] = k.inc(mm, "P", 1)
                        pv = k.cnt["P"]
                        k.wait(sc, "P", pv)
                        for ih in range(2):
                            psx = psC if ih == 0 else psD
                            av = k.inc(sc.activation(
                                dst[:, tlo + ih * 512: tlo + (ih + 1) * 512],
                                psx[:, 0:512], Act.Identity,
                                bias=brow[:, pair:pair + 1], scale=1.0), "A", 1)
                        k.wait(te, "A", av)
                k.wait(te, "P", vt_av_P[par])
                k.wait(te, "V", vt_ones_V)
                for jh in range(2):
                    jbase = jh * 8
                    for c in range(DC):
                        upar = c % 2
                        k.wait(sy, "P", ustr_free_P[upar])
                        lv = k.inc(sy.dma_start(
                            out=ustr[upar][:],
                            in_=dr["usp"][c * 128:(c + 1) * 128,
                                          jbase * 128: jbase * 128 + IT]),
                            f"Lu{upar}", 16)
                        k.wait(te, f"Lu{upar}", lv)
                        for jj in range(8):
                            psx = psC if jj < 4 else psD
                            mm = te.matmul(
                                psx[:, (jj % 4) * 128:(jj % 4) * 128 + 128],
                                ustr[upar][:, jj * 128:(jj + 1) * 128],
                                wv16[:, c * 128:(c + 1) * 128],
                                start=(c == 0 and jj % 4 == 0), stop=False)
                        ustr_free_P[upar] = k.inc(mm, "P", 1)
                    for jj in range(8):
                        psx = psC if jj < 4 else psD
                        mm = te.matmul(psx[:, (jj % 4) * 128:(jj % 4) * 128 + 128],
                                       g["ones_r16"][:],
                                       g["bv16"][0:1, pair * 128:(pair + 1) * 128],
                                       start=False, stop=(jj % 4 == 3))
                    pv = k.inc(mm, "P", 1)
                    k.wait(sc, "P", pv)
                    for jj in range(8):
                        jt = jbase + jj
                        psx = psC if jj < 4 else psD
                        co = (jj % 4) * 128
                        sc.activation(vt16[par][:, jt * 130: jt * 130 + 64],
                                      psx[:, co: co + 64], Act.Copy,
                                      bias=0.0, scale=1.0)
                        av = k.inc(sc.activation(
                            vt16[par][:, jt * 130 + 65: jt * 130 + 129],
                            psx[:, co + 64: co + 128], Act.Copy,
                            bias=0.0, scale=1.0), "A", 1)
                    k.wait(te, "A", av)
                vt_ready_A = k.cnt["A"]
                k.wait(te, "A", vt_ready_A)
                k.wait(te, "V", cv_read_V[par])
                for hh in range(2):
                    for jt in range(NJT):
                        mm = te.matmul(
                            psC[0:64, hh:hh + 1],
                            vt16[par][:, jt * 130 + hh * 65: jt * 130 + hh * 65 + 64],
                            g["ones_c16"][:],
                            start=(jt == 0), stop=(jt == NJT - 1))
                pv = k.inc(mm, "P", 1)
                k.wait(ve, "P", pv)
                k.inc(ve.tensor_scalar(cv16[:, 2 * par:2 * par + 2],
                                       psC[0:64, 0:2], 0.5, None, Alu.mult),
                      "V", 1)
                qkv_used_P[0] = k.cnt["P"]
                qk_ready_A[par] = k.cnt["A"]

            qk_ready_A = [0, 0]
            av_evac_V = 0
            s_evac_A = 0
            s_evac_of = [0, 0]
            s16_free_V = [0, 0]
            tanh_A = 0
            diag_used_P = 0
            stream_w(0)
            emit_qkv(0)
            for pair in range(NPAIR):
                par = pair % 2
                if pair + 1 < NPAIR:
                    stream_w(pair + 1)
                    emit_qkv(pair + 1)
                k.wait(te, "A", qk_ready_A[par])
                for hh in range(2):
                    h = pair * 2 + hh
                    k.wait(ve, "P", diag_used_P)
                    for fi in range(NFREQ):
                        vv = ve.tensor_scalar(
                            diag16[:, fi * 128:(fi + 1) * 128],
                            g["ident16"][:], al_ap(g, h, fi), None, Alu.mult)
                    diag_V = k.inc(vv, "V", 1)
                    for it in range(NIT):
                        ilo = it * IT
                        sin_A = [0] * NFREQ
                        diag_P = [0] * NFREQ
                        rr_V = [0] * NFREQ
                        for jt in range(NJT + 1):
                            if jt < NJT:
                                spar = jt % 2
                                k.wait(te, "A", s_evac_A)
                                for ih in range(2):
                                    mm = te.matmul(
                                        ps_main[:, ih * 512:(ih + 1) * 512],
                                        k16[par][64 * hh:64 * hh + 64,
                                                 jt * 128:(jt + 1) * 128],
                                        q16[par][64 * hh:64 * hh + 64,
                                                 ilo + ih * 512: ilo + (ih + 1) * 512],
                                        start=True, stop=True)
                                pv = k.inc(mm, "P", 1)
                                k.wait(sc, "P", pv)
                                k.wait(sc, "V", s16_free_V[spar])
                                s_evac_A = k.inc(sc.activation(
                                    s16[spar][:], ps_main[:, 0:IT], Act.Identity,
                                    bias=0.0, scale=1.0), "A", 1)
                                s_evac_of[spar] = s_evac_A
                            if jt >= 1:
                                j = jt - 1
                                spar = j % 2
                                k.wait(ve, "A", s_evac_of[spar])
                                for fp in range(3):
                                    nf = 2 if fp < 2 else 1
                                    for ff in range(nf):
                                        fi = fp * 2 + ff
                                        ys = slabY[:, fi * IT:(fi + 1) * IT]
                                        ns = slabN[:, fi * IT:(fi + 1) * IT]
                                        k.wait(ve, "A", sin_A[fi])
                                        ve.tensor_scalar(ns, s16[spar][:],
                                                         k1_ap(g, fi),
                                                         k2_ap(g, h, fi),
                                                         Alu.mult, Alu.add)
                                        ve.tensor_scalar(ys, s16[spar][:],
                                                         k1_ap(g, fi),
                                                         k2_ap(g, h, fi),
                                                         Alu.mult, Alu.add)
                                        ve.tensor_tensor(ys, ys, ns, Alu.subtract)
                                        ve.tensor_scalar(ns.bitcast(F16) if False
                                                         else ns, ys, 0.5, None,
                                                         Alu.is_ge)
                                        vv = ve.tensor_tensor(ys, ys, ns,
                                                              Alu.subtract)
                                        if fi == NFREQ - 1:
                                            s16_free_V[spar] = k.cnt["V"] + 1
                                        rr_V[fi] = k.inc(vv, "V", 1)
                                    f0 = fp * 2
                                    k.wait(sc, "V", rr_V[f0 + nf - 1])
                                    k.wait(sc, "P", diag_P[f0 + nf - 1])
                                    av = k.inc(sc.activation(
                                        slabS[:, f0 * IT:(f0 + nf) * IT],
                                        slabY[:, f0 * IT:(f0 + nf) * IT],
                                        Act.Sin, bias=0.0, scale=cst(g, 4)),
                                        "A", 1)
                                    for ff in range(nf):
                                        sin_A[f0 + ff] = av
                                    k.wait(te, "A", av)
                                    k.wait(te, "V", diag_V)
                                    if fp == 0:
                                        k.wait(te, "A", tanh_A)
                                    for ff in range(nf):
                                        fi = f0 + ff
                                        for ih in range(2):
                                            mm = te.matmul(
                                                ps_main[:, IT + ih * 512:
                                                        IT + (ih + 1) * 512],
                                                diag16[:, fi * 128:(fi + 1) * 128],
                                                slabS[:, fi * IT + ih * 512:
                                                      fi * IT + (ih + 1) * 512],
                                                start=(fi == 0), stop=(fi == NFREQ - 1))
                                        diag_P[fi] = k.inc(mm, "P", 1)
                                rslot = j % 2
                                k.wait(sc, "P", max(diag_P[NFREQ - 1],
                                                    w16r_free_P[rslot]))
                                tanh_A = k.inc(sc.activation(
                                    w16r[rslot][:], ps_main[:, IT:2 * IT],
                                    Act.Tanh, bias=0.0, scale=cst(g, 3)), "A", 1)
                                k.wait(te, "A", tanh_A)
                                if j == 0:
                                    k.wait(te, "V", av_evac_V)
                                for ih in range(2):
                                    psx = psA if ih == 0 else psB
                                    mm = te.matmul(
                                        psx[0:65, 0:512],
                                        vt16[par][:, j * 130 + hh * 65:
                                                  j * 130 + hh * 65 + 65],
                                        w16r[rslot][:, ih * 512:(ih + 1) * 512],
                                        start=(j == 0), stop=(j == NJT - 1))
                                w16r_free_P[rslot] = k.inc(mm, "P", 1)
                        av_done_P = k.cnt["P"]
                        apar = h % 2
                        k.wait(ve, "P", av_done_P)
                        if it == 0:
                            k.wait(ve, f"Ga{apar}", ah_free_G[apar])
                        k.wait(ve, "Gr", rssc_free_G)
                        for ih in range(2):
                            psx = psA if ih == 0 else psB
                            ve.tensor_scalar(
                                ah16[apar][:, ilo + ih * 512: ilo + (ih + 1) * 512],
                                psx[0:64, 0:512], 0.5,
                                cv16[:, 2 * par + hh:2 * par + hh + 1],
                                Alu.mult, Alu.add)
                            vv = ve.tensor_copy(
                                rssc[64:65, ih * 512:(ih + 1) * 512],
                                psx[64:65, 0:512])
                        avv = k.inc(vv, "V", 1)
                        av_evac_V = avv
                        cv_read_V[par] = avv
                        k.wait(gp, "V", avv)
                        rssc_free_G = k.inc(gp.dma_start(
                            out=rs[h:h + 1, ilo:ilo + IT],
                            in_=rssc[64:65, 0:IT]), "Gr", 16)
                        if it == NIT - 1:
                            ah_free_G[apar] = k.inc(gp.dma_start(
                                out=dr["asp"][h, :, :], in_=ah16[apar][:]),
                                f"Ga{apar}", 16)
                    diag_used_P = k.cnt["P"]
                vt_av_P[par] = k.cnt["P"]
                qk_scores_P[par] = k.cnt["P"]
            k.wait(sc, "V")
            k.wait(sc, "Gr")
            sc.activation(lnt16[:], rs[:], Act.Ln, bias=cst(g, 1)[0:16, :],
                          scale=cst(g, 3)[0:16, :])
            k.inc(sc.activation(nrm16[:], lnt16[:], Act.Exp, bias=0.0,
                                scale=cst(g, 5)[0:16, :]), "A", 1)
        nrm_A = k.cnt["A"]
        a_spill_G = [k.cnt.get("Ga0", 0), k.cnt.get("Ga1", 0)]
        k.barrier()

        with contextlib.ExitStack() as es:
            def sb(name, shape, dtype):
                return es.enter_context(nc.sbuf_tensor(name, shape, dtype))
            k.dsem("Lwo0"), k.dsem("Lwo1")
            psO = es.enter_context(nc.psum_tensor("wo_O", [128, 512], F32))
            psN = es.enter_context(nc.psum_tensor("wo_N", [128, 512], F32))
            psS1 = es.enter_context(nc.psum_tensor("wo_S1", [128, 512], F32))
            psS2 = es.enter_context(nc.psum_tensor("wo_S2", [128, 512], F32))
            wo16 = sb("wo16", [128, DC * D], F16)
            wstg = [sb(f"wo_stg_{i}", [128, D], F32) for i in range(2)]
            ahl = [sb(f"wo_ahl_{i}", [64, 512], F16) for i in range(2)]
            nbc16 = [sb(f"wo_nbc_{i}", [64, 512], F16) for i in range(2)]
            nrow16 = [sb(f"wo_nrow_{i}", [1, 512], F16) for i in range(2)]
            anrm = [sb(f"wo_anrm_{i}", [64, 512], F16) for i in range(2)]
            worhs = sb("wo_rhs", [128, DC * 512], F16)
            o_sb = sb("wo_osb", [128, DC * 512], F32)
            sq32 = sb("wo_sq32", [128, LNT], F32)
            sq32b = sb("wo_sq32b", [128, LNT], F32)
            rowA = sb("wo_rowA", [1, LNT], F16)
            rowB = sb("wo_rowB", [1, LNT], F16)
            rowC = sb("wo_rowC", [1, LNT], F16)
            ab16 = sb("wo_ab16", [128, 2 * LNT], F16)
            t32 = sb("wo_t32", [128, 512], F32)
            aout32 = [sb(f"wo_aout_{i}", [128, 512], F32) for i in range(2)]

            wo_conv_V = 0
            for c in range(DC):
                par = c % 2
                k.wait(sy, "V", wo_conv_V if c >= 2 else 0)
                lv = k.inc(sy.dma_start(out=wstg[par][:],
                                        in_=dr["woT"][c * 128:(c + 1) * 128, :]),
                           f"Lwo{par}", 16)
                k.wait(ve, f"Lwo{par}", lv)
                wo_conv_V = k.inc(ve.tensor_copy(wo16[:, c * D:(c + 1) * D],
                                                 wstg[par][:]), "V", 1)

            rel_free_V = [0, 0]
            nbc_free_V = [0, 0]
            nrow_mm_P = [0, 0]
            for _nm in ("Gl0", "Gl1", "Gn0", "Gn1", "Gm0", "Gm1", "Go0", "Go1"):
                k.dsem(_nm)
            anrm_free_G = [0, 0]
            worhs_ready_G = 0
            aout_free_G = [0, 0]
            wo_mm_P = 0
            osb_read_V = 0
            for it4 in range(4):
                tlo = it4 * 512
                k.wait(gp, "P", wo_mm_P)
                for c in range(DC):
                    for hh in range(2):
                        hx = 2 * c + hh
                        par = hx % 2
                        k.wait(gp, "Ga0", a_spill_G[0])
                        k.wait(gp, "Ga1", a_spill_G[1])
                        k.wait(gp, "V", rel_free_V[par])
                        lv = k.inc(gp.dma_start(
                            out=ahl[par][:],
                            in_=dr["asp"][hx, :, tlo:tlo + 512]), f"Gl{par}", 16)
                        k.wait(gp, "A", nrm_A)
                        k.wait(gp, "P", nrow_mm_P[par])
                        nlv = k.inc(gp.dma_start(
                            out=nrow16[par][:],
                            in_=nrm16[hx:hx + 1, tlo:tlo + 512]), f"Gn{par}", 16)
                        k.wait(te, f"Gn{par}", nlv)
                        k.wait(te, "V", nbc_free_V[par])
                        pv = k.inc(te.matmul(
                            psN[0:64, 0:512],
                            g["ones_r16"][0:1, 0:64],
                            nrow16[par][:],
                            start=True, stop=True), "P", 1)
                        nrow_mm_P[par] = pv
                        k.wait(ve, "P", pv)
                        nbv = k.inc(ve.tensor_copy(
                            nbc16[par][:], psN[0:64, 0:512]), "V", 1)
                        nbc_free_V[par] = nbv
                        k.wait(ve, f"Gl{par}", lv)
                        k.wait(ve, f"Gm{par}", anrm_free_G[par])
                        rel_free_V[par] = k.inc(ve.tensor_tensor(
                            anrm[par][:], ahl[par][:], nbc16[par][:], Alu.mult),
                            "V", 1)
                        k.wait(gp, "V", rel_free_V[par])
                        anrm_free_G[par] = k.inc(gp.dma_start(
                            out=worhs[hh * 64:(hh + 1) * 64,
                                      c * 512:(c + 1) * 512],
                            in_=anrm[par][:]), f"Gm{par}", 16)
                k.wait(te, "Gm0", k.cnt["Gm0"])
                k.wait(te, "Gm1", k.cnt["Gm1"])
                k.wait(te, "V", wo_conv_V)
                k.wait(sc, "V", osb_read_V)
                for hc in range(DC):
                    for c in range(DC):
                        mm = te.matmul(psO[:, 0:512],
                                       wo16[:, c * D + hc * 128: c * D + (hc + 1) * 128],
                                       worhs[:, c * 512:(c + 1) * 512],
                                       start=(c == 0), stop=(c == DC - 1))
                    pv = k.inc(mm, "P", 1)
                    k.wait(sc, "P", pv)
                    av = k.inc(sc.activation(o_sb[:, hc * 512:(hc + 1) * 512],
                                             psO[:, 0:512], Act.Identity,
                                             bias=g["bor"][:, hc:hc + 1],
                                             scale=1.0), "A", 1)
                    k.wait(te, "A", av)
                wo_mm_P = k.cnt["P"]
                abv = emit_ln_tile(
                    k, g, lambda c: o_sb[:, c * 512:(c + 1) * 512],
                    psS1, psS2, psS1, psS2, [sq32, sq32b], rowA, rowB, rowC, ab16)
                for c in range(DC):
                    par = c % 2
                    k.wait(ve, "V", abv)
                    k.wait(ve, f"Go{par}", aout_free_G[par])
                    ve.tensor_tensor(t32[:], o_sb[:, c * 512:(c + 1) * 512],
                                     ab16[:, 0:LNT], Alu.mult)
                    ve.tensor_tensor(t32[:], t32[:], ab16[:, LNT:2 * LNT], Alu.add)
                    vv = k.inc(ve.tensor_scalar(aout32[par][:], t32[:],
                                                g["gr"][:, c:c + 1],
                                                g["br"][:, c:c + 1],
                                                Alu.mult, Alu.add), "V", 1)
                    k.wait(gp, "V", vv)
                    aout_free_G[par] = k.inc(gp.dma_start(
                        out=g["r"][:, c * NT + tlo: c * NT + tlo + 512],
                        in_=aout32[par][:], accum_op=Alu.add), f"Go{par}", 16)
                osb_read_V = k.cnt["V"]
    k.barrier()



def emit_output(k, g, dr):
    k.barrier()
    gp = k.gp
    for c in range(DC):
        k.inc(gp.dma_start(out=dr["outT"][c * 128:(c + 1) * 128, :],
                           in_=g["r"][:, c * NT:(c + 1) * NT]), "G", 16)
    k.wait(gp, "G")


def build_program():
    nc = bass.Bass("TRN2", target_bir_lowering=False,
                   detect_race_conditions=DEBUG_SYNC)
    dr = {}
    dr["xT"] = nc.dram_tensor("xT", [D, NT], F32, kind="ExternalInput").ap()
    dr["wqT"] = nc.dram_tensor("wqT", [D, D], F32, kind="ExternalInput").ap()
    dr["wkT"] = nc.dram_tensor("wkT", [D, D], F32, kind="ExternalInput").ap()
    dr["wvT"] = nc.dram_tensor("wvT", [D, D], F32, kind="ExternalInput").ap()
    dr["woT"] = nc.dram_tensor("woT", [D, D], F32, kind="ExternalInput").ap()
    dr["w1T"] = nc.dram_tensor("w1T", [E, D, F], F32, kind="ExternalInput").ap()
    dr["w2T"] = nc.dram_tensor("w2T", [E, F, D], F32, kind="ExternalInput").ap()
    dr["rwT"] = nc.dram_tensor("rwT", [D, 8], F32, kind="ExternalInput").ap()
    dr["b1r"] = nc.dram_tensor("b1r", [128, 128], F32, kind="ExternalInput").ap()
    dr["b2"] = nc.dram_tensor("b2", [8, D], F32, kind="ExternalInput").ap()
    for nm in ("bqr", "bkr", "bor", "gr", "br"):
        dr[nm] = nc.dram_tensor(nm, [128, 8], F32, kind="ExternalInput").ap()
    dr["bv"] = nc.dram_tensor("bv", [1, D], F32, kind="ExternalInput").ap()
    dr["alphaf"] = nc.dram_tensor("alphaf", [1, 80], F32, kind="ExternalInput").ap()
    dr["phif"] = nc.dram_tensor("phif", [1, 80], F32, kind="ExternalInput").ap()
    dr["temp"] = nc.dram_tensor("temp", [1, 1], F32, kind="ExternalInput").ap()
    dr["outT"] = nc.dram_tensor("outT", [D, NT], F32, kind="ExternalOutput").ap()
    dr["usp"] = nc.dram_tensor("usp", [D, NT], F16, kind="Internal").ap()
    dr["asp"] = nc.dram_tensor("asp", [NH, 64, NT], F16, kind="Internal").ap()

    with contextlib.ExitStack() as es:
        k = Ctx(nc, es)
        g = {}
        emit_setup(k, dr, g)
        if DO_FFN1:
            emit_ffn(k, g, dr, "f1")
        if DO_ATTN:
            emit_attention(k, g, dr)
        if DO_FFN2:
            emit_ffn(k, g, dr, "f2")
        emit_output(k, g, dr)
    return nc


def host_prep(inputs):
    asc = np.ascontiguousarray
    w = {}
    w["wqT"] = asc(inputs["L0_attn_Wq"].T)
    w["wkT"] = asc(inputs["L0_attn_Wk"].T)
    w["wvT"] = asc(inputs["L0_attn_Wv"].T)
    w["woT"] = asc(inputs["L0_attn_Wo"].T)
    w["w1T"] = asc(np.transpose(inputs["L0_ffn_w1"], (0, 2, 1)))
    w["w2T"] = asc(np.transpose(inputs["L0_ffn_w2"], (0, 2, 1)))
    w["rwT"] = asc(inputs["L0_ffn_router_w"].T)
    w["b1r"] = asc(np.transpose(inputs["L0_ffn_b1"].reshape(8, 16, 128),
                                (2, 0, 1)).reshape(128, 128))
    w["b2"] = asc(inputs["L0_ffn_b2"])
    w["bqr"] = asc(inputs["L0_attn_bq"].reshape(8, 128).T)
    w["bkr"] = asc(inputs["L0_attn_bk"].reshape(8, 128).T)
    w["bor"] = asc(inputs["L0_attn_bo"].reshape(8, 128).T)
    w["gr"] = asc(inputs["L0_attn_gamma"].reshape(8, 128).T)
    w["br"] = asc(inputs["L0_attn_beta"].reshape(8, 128).T)
    w["bv"] = asc(inputs["L0_attn_bv"].reshape(1, D))
    w["alphaf"] = asc(inputs["L0_attn_alpha"].reshape(1, 80))
    w["phif"] = asc(inputs["L0_attn_phi"].reshape(1, 80))
    w["temp"] = asc(inputs["L0_attn_temperature"].reshape(1, 1))
    return w


def kernel(**inputs):
    x = np.asarray(inputs["L0_attn_x"], dtype=np.float32)
    inputs = {kk: np.asarray(v, dtype=np.float32) for kk, v in inputs.items()}
    w = host_prep(inputs)
    nc = build_program()
    in_maps = []
    for c in range(8):
        m = dict(w)
        m["xT"] = np.ascontiguousarray(x[c].T)
        in_maps.append(m)
    res = run_bass_kernel_spmd(nc, in_maps, core_ids=list(range(8)))
    out = np.stack([np.asarray(res.results[c]["outT"]).T for c in range(8)])
    return out.astype(np.float32)
